# version 01
# baseline (speedup 1.0000x reference)
import numpy as np
import ml_dtypes

import concourse.bass as bass
import concourse.mybir as mybir

F32 = mybir.dt.float32
BF16 = mybir.dt.bfloat16
I32 = mybir.dt.int32
U8 = mybir.dt.uint8
AF = mybir.ActivationFunctionType
ALU = mybir.AluOpType
AX = mybir.AxisListType
BF = ml_dtypes.bfloat16

ENGINES = ("sync", "tensor", "vector", "scalar", "gpsimd")

S = 2048
D = 2048
HID = 8192
NT = S // 128
ND = D // 128
NH = HID // 128
EPS_LN = 1e-5
EPS_RMS = 1e-6

SET_NL_EXP = "natural_log_exp_and_others"
SET_GELU_T = "gelu_apprx_tanh_and_others"

ARENA_B = 131072


class Chan:
    __slots__ = ("sem", "count", "name", "owner")

    def __init__(self, sem, name):
        self.sem = sem
        self.count = 0
        self.name = name
        self.owner = None

    def inc(self, amt=1):
        self.count += amt
        return _Inc(self, self.count, amt)

    def at(self, val=None):
        return _Wait(self, self.count if val is None else val)


class _Inc:
    __slots__ = ("chan", "after", "amt")

    def __init__(self, chan, after, amt):
        self.chan, self.after, self.amt = chan, after, amt

    @property
    def done(self):
        return _Wait(self.chan, self.after)


class _Wait:
    __slots__ = ("chan", "val")

    def __init__(self, chan, val):
        self.chan, self.val = chan, val


class _Item:
    __slots__ = ("fn", "waits", "incs", "tag")

    def __init__(self, fn, waits, incs, tag):
        self.fn, self.waits, self.incs, self.tag = fn, waits, incs, tag


def _as_waits(w):
    if w is None:
        return []
    if isinstance(w, (_Wait, _Inc)):
        w = [w]
    out = []
    for x in w:
        if x is None:
            continue
        if isinstance(x, _Inc):
            x = x.done
        assert isinstance(x, _Wait), f"bad wait token {x!r}"
        out.append(x)
    return out


class Prog:
    def __init__(self, nc):
        self.nc = nc
        self.items = {e: [] for e in ENGINES}
        self.n_ops = 0

    def chan(self, name):
        return Chan(self.nc.alloc_semaphore(name=name), name)

    def op(self, engine, fn, waits=None, incs=None, tag=""):
        waits = _as_waits(waits)
        if incs is None:
            incs = []
        elif isinstance(incs, _Inc):
            incs = [incs]
        for w in waits:
            assert w.val <= w.chan.count, (
                f"op {tag!r} on {engine} waits {w.chan.name}>={w.val} "
                f"but only {w.chan.count} planned")
        for x in incs:
            ch = x.chan
            if ch.owner is None:
                ch.owner = engine
            assert ch.owner == engine, (
                f"chan {ch.name} inc'd from {engine} but owned by {ch.owner} "
                f"(op {tag!r}) — single-engine channels only")
        self.items[engine].append(_Item(fn, waits, incs, tag))
        self.n_ops += 1

    def _emit(self, eng, items):
        for it in items:
            for w in it.waits:
                eng.wait_ge(w.chan.sem, w.val)
            inst = it.fn(eng)
            if it.incs:
                if inst is not None:
                    inst.then_inc(it.incs[0].chan.sem, it.incs[0].amt)
                    rest = it.incs[1:]
                else:
                    rest = it.incs
                for x in rest:
                    eng.sem_inc(x.chan.sem, x.amt)

    def build(self):
        with self.nc.Block() as block:
            for ename in ENGINES:
                items = self.items[ename]
                if not items:
                    continue

                def make(items):
                    def f(eng):
                        self._emit(eng, items)
                    return f

                getattr(block, ename)(make(items))
        return self.nc


def _bf(a):
    return np.ascontiguousarray(np.asarray(a)).astype(BF)


def _c(a):
    return np.ascontiguousarray(np.asarray(a))


def prepare_inputs(inp):
    g = {}
    f32 = np.float32

    g["ident_bf"] = _bf(np.eye(128, dtype=f32))
    g["ones_row_bf"] = _bf(np.ones((1, 128), dtype=f32))
    g["ones_col_bf"] = _bf(np.ones((128, 1), dtype=f32))
    tri = np.where(np.arange(128)[None, :] > np.arange(128)[:, None],
                   np.float32(-1e30), np.float32(0.0))
    g["mask_tri_bf"] = _bf(tri)
    P = np.zeros((128, 128), dtype=f32)
    for i in range(0, 128, 2):
        P[i, i + 1] = 1.0
        P[i + 1, i] = 1.0
    g["pswap_bf"] = _bf(P)
    selb = np.zeros((16, 16 * 128), dtype=f32)
    for n in range(16):
        selb[n, 128 * n:128 * (n + 1)] = 1.0
    g["sel16_bf"] = _bf(selb)
    ROPE = 64
    freqs = 1.0 / 10000.0 ** (np.arange(0, ROPE, 2, dtype=f32) / ROPE)
    ang = np.outer(np.arange(S, dtype=f32), freqs)
    cos64 = np.repeat(np.cos(ang), 2, axis=1).T
    sin64 = np.repeat(np.sin(ang), 2, axis=1).T
    sgn = np.tile(np.array([-1.0, 1.0], dtype=f32), 32)[:, None]
    g["ropecos_bf"] = _bf(np.concatenate([cos64, cos64], 0))
    g["ropesin_bf"] = _bf(np.concatenate([sin64 * sgn, sin64 * sgn], 0))
    m4 = np.zeros((128, 4 * 512), dtype=f32)
    rr = np.arange(128)[:, None]
    cc = np.arange(512)[None, :]
    for q in range(4):
        m4[:, 512 * q:512 * (q + 1)] = np.where(cc > q * 128 + rr,
                                                np.float32(-1e30), 0.0)
    g["mask4_bf"] = _bf(m4)

    g["wupq"] = _c(inp["L0_ffn_w_up_q"]).astype(np.int32)
    g["wupsc"] = _c(inp["L0_ffn_w_up_scale"]).astype(f32)
    g["bup_pd"] = _c(np.asarray(inp["L0_ffn_b_up"], f32).reshape(NH, 128).T)
    g["l0ua"] = _c(inp["L0_ffn_w_up_lora_a"]).astype(f32)
    g["l0ub"] = _c(inp["L0_ffn_w_up_lora_b"]).astype(f32)
    g["wdnq"] = _c(inp["L0_ffn_w_down_q"]).astype(np.int32)
    g["wdnsc"] = _c(inp["L0_ffn_w_down_scale"]).astype(f32)
    g["bdn_row"] = _c(np.asarray(inp["L0_ffn_b_down"], f32)[None, :])
    g["l0da"] = _c(inp["L0_ffn_w_down_lora_a"]).astype(f32)
    g["l0db"] = _c(inp["L0_ffn_w_down_lora_b"]).astype(f32)

    W_x = np.asarray(inp["L0_attn_W_x"], f32)
    g["wx3T"] = _c(W_x.T)
    g["wdtT"] = _c(np.asarray(inp["L0_attn_W_dt"], f32).T)
    g["bdt_pd"] = _c(np.asarray(inp["L0_attn_b_dt"], f32).reshape(ND, 128).T)
    g["alog"] = _c(inp["L0_attn_A_log"]).astype(f32)
    g["d_pd"] = _c(np.asarray(inp["L0_attn_D"], f32).reshape(ND, 128).T)

    wq = np.asarray(inp["L1_attn_wq"], f32)
    QKH, NOPE = 192, 128
    perm = [h * QKH + i for h in range(16) for i in range(NOPE)]
    perm += [h * QKH + NOPE + i for h in range(16) for i in range(64)]
    g["wqTr"] = _c(wq[perm].T)
    wkva = np.asarray(inp["L1_attn_wkv_a"], f32)
    g["wkvaTx"] = _c(np.concatenate([wkva[:512], wkva[512:], wkva[512:]], 0).T)
    wkvb = np.asarray(inp["L1_attn_wkv_b"], f32).reshape(16, 256, 512)
    g["wkvbn"] = _c(wkvb[:, :128, :].reshape(2048, 512))
    g["wkvbvT"] = _c(wkvb[:, 128:, :].transpose(2, 0, 1).reshape(512, 2048))
    g["woT"] = _c(np.asarray(inp["L1_attn_wo"], f32).T)
    g["kvw_pd"] = _c(np.asarray(inp["L1_attn_kv_norm_w"], f32).reshape(4, 128).T)

    g["sbias_row"] = _c(np.asarray(inp["L1_ffn_simple_bias"], f32)[None, :])
    g["wdT1"] = _c(np.asarray(inp["L1_ffn_w_down"], f32).T)
    g["wupT1"] = _c(np.asarray(inp["L1_ffn_w_up"], f32).T)
    g["wrT"] = _c(np.asarray(inp["L1_ffn_w_router"], f32).T)
    for nm, key in [("ln1s", "L1_ffn_ln1_scale"), ("ln1b", "L1_ffn_ln1_bias"),
                    ("ln2s", "L1_ffn_ln2_scale"), ("ln2b", "L1_ffn_ln2_bias")]:
        g[nm + "_pd"] = _c(np.asarray(inp[key], f32).reshape(ND, 128).T)
    g["wqT2"] = _c(np.asarray(inp["L1_ffn_wq"], f32).T)
    g["wkT2"] = _c(np.asarray(inp["L1_ffn_wk"], f32).T)
    g["wvT2"] = _c(np.asarray(inp["L1_ffn_wv"], f32).T)
    g["woT2"] = _c(np.asarray(inp["L1_ffn_wo"], f32).T)
    g["wff1T"] = _c(np.asarray(inp["L1_ffn_w_ff1"], f32).T)
    g["wff2T"] = _c(np.asarray(inp["L1_ffn_w_ff2"], f32).T)

    xs = [np.ascontiguousarray(np.asarray(inp["L0_attn_x"][c], f32))
          for c in range(np.asarray(inp["L0_attn_x"]).shape[0])]
    return g, xs


DT_OF = {np.dtype(np.float32): F32, np.dtype(BF): BF16,
         np.dtype(np.int32): I32}


def shapes_of(d):
    return {k: (v.shape, DT_OF[v.dtype]) for k, v in d.items()}


class KB:
    def __init__(self, nc, shapes):
        self.nc = nc
        self.p = Prog(nc)
        self.din = {}
        for name, (shape, dt) in shapes.items():
            self.din[name] = nc.dram_tensor(
                name, list(shape), dt, kind="ExternalInput").ap()
        self.scr = {}
        self.cur_act_set = None
        self._flags = {}
        self._mk()

    dbg = frozenset()

    def dram(self, name, shape, dt):
        if name not in self.scr:
            kind = "ExternalOutput" if name in self.dbg else "Internal"
            self.scr[name] = self.nc.dram_tensor(
                name, list(shape), dt, kind=kind).ap()
        return self.scr[name]

    def dram_out(self, name, shape, dt):
        self.scr[name] = self.nc.dram_tensor(
            name, list(shape), dt, kind="ExternalOutput").ap()
        return self.scr[name]

    def _mk(self):
        nc, p = self.nc, self.p
        self.PSB = nc.alloc_psum_tensor("psb", [128, 2048], F32)
        self.PSS = [nc.alloc_psum_tensor(f"pss{i}", [128, 512], F32)
                    for i in range(4)]
        self.ARENA = nc.alloc_sbuf_tensor("ARENA", [128, ARENA_B], U8,
                                          is_transpose_dest=True)
        self.XT = [nc.alloc_sbuf_tensor(f"XT{i}", [128, 2048], F32)
                   for i in range(2)]
        self.WB = [nc.alloc_sbuf_tensor(f"WB{i}", [128, 2048], BF16,
                                        is_transpose_dest=True)
                   for i in range(2)]
        self.TMP = [nc.alloc_sbuf_tensor("TMP0", [128, 2048], F32)]
        self.SM = nc.alloc_sbuf_tensor("SM", [128, 512], F32)
        self.SMB = nc.alloc_sbuf_tensor("SMB", [128, 512], BF16)
        self.TUP = nc.alloc_sbuf_tensor("TUP", [16, 2048], BF16)
        self.TDN = nc.alloc_sbuf_tensor("TDN", [16, 2048], BF16)
        self.ROW1 = nc.alloc_sbuf_tensor("ROW1", [1, 2048], BF16)
        self.ROW2 = nc.alloc_sbuf_tensor("ROW2", [1, 2048], BF16)
        self.ident = nc.alloc_sbuf_tensor("identc", [128, 128], BF16)
        self.onesr = nc.alloc_sbuf_tensor("onesr", [1, 128], BF16)
        self.onesc = nc.alloc_sbuf_tensor("onesc", [128, 1], BF16)
        self.masktri = nc.alloc_sbuf_tensor("masktri", [128, 128], BF16)
        self.ldc = p.chan("ldconst")
        for nm, t in [("ident_bf", self.ident), ("ones_row_bf", self.onesr),
                      ("ones_col_bf", self.onesc),
                      ("mask_tri_bf", self.masktri)]:
            p.op("sync", lambda e, nm=nm, t=t: e.dma_start(
                    out=t[:], in_=self.din[nm]),
                 incs=self.ldc.inc(16), tag=f"ldc_{nm}")
        self.const_ready = self.ldc.at()

    def carve(self, off_b, n, dt):
        size = {F32: 4, BF16: 2, I32: 4}[dt]
        assert off_b % 32 == 0 and off_b + n * size <= ARENA_B, (off_b, n, dt)
        return self.ARENA[:, off_b:off_b + n * size].bitcast(dt)

    def xfm(self, i):
        return self.carve(4096 * i, 2048, BF16)


    def ensure_msk4(self):
        if hasattr(self, "MSK4"):
            return self._flags["msk4_ready"]
        nc, p = self.nc, self.p
        self.MSK4 = nc.alloc_sbuf_tensor("MSK4", [128, 2048], BF16)
        ch = p.chan("msk4")
        p.op("sync", lambda e: e.dma_start(out=self.MSK4[:],
                                           in_=self.din["mask4_bf"]),
             incs=ch.inc(16), tag="ld_msk4")
        self._flags["msk4_ready"] = ch.at()
        return self._flags["msk4_ready"]

    def act_set(self, set_name, waits=None):
        if self.cur_act_set == set_name:
            return
        self.cur_act_set = set_name
        from concourse.hw_specs import get_activation_tables
        tabs = list(get_activation_tables(self.nc.m.arch).keys())
        idx = tabs.index(set_name)

        def fn(e, idx=idx):
            return e.add_instruction(mybir.InstLoadActFuncSet(
                name=e.bass.get_next_instruction_name(),
                act_func_set_id=idx, ins=[], outs=[]))
        self.p.op("scalar", fn, waits=waits, tag=f"actset_{set_name}")

    def cast_weight(self, name, waits=None):
        key = name + "_bf"
        if key in self.scr:
            return self.scr[key], self._flags[key]
        src = self.din[name]
        dst = self.dram(key, src.shape, BF16)
        ch = self.p.chan("cast_" + name)
        self.p.op("gpsimd", lambda e: e.dma_start(out=dst, in_=src),
                  waits=waits, incs=ch.inc(16), tag=f"cast_{name}")
        self._flags[key] = ch.at()
        return dst, self._flags[key]

    def once(self, key):
        if key in self._flags:
            return False
        self._flags[key] = True
        return True


def emit_ln(kb, x_in, xn_tm, tag, start=None, inner_stats=None):
    p = kb.p
    ld = [p.chan(tag + "_ld0"), p.chan(tag + "_ld1")]
    stc = [p.chan(tag + "_st0"), p.chan(tag + "_st1")]
    trc = p.chan(tag + "_tr")
    cs = p.chan(tag + "_cs")
    cv = p.chan(tag + "_cv")
    SM = kb.SM
    start = _as_waits(start)
    xt_free = [None, None]
    wb_free = [None, None]
    inner_done = []
    for i in range(NT):
        b = i % 2
        xt = kb.XT[b]
        p.op("sync", lambda e, xt=xt, i=i: e.dma_start(
                out=xt[:], in_=x_in[128 * i:128 * (i + 1), :]),
             waits=(start if i < 2 else []) + _as_waits(xt_free[b]),
             incs=(ldi := ld[b].inc(16)), tag=f"{tag}_ld{i}")
        sum_c = SM[:, i:i + 1]
        sq_c = SM[:, 16 + i:17 + i]
        p.op("vector", lambda e, xt=xt, o=sum_c: e.reduce_sum(o, xt[:], AX.X),
             waits=[ldi.done], incs=(esum := cv.inc()), tag=f"{tag}_sum{i}")
        p.op("scalar", lambda e, xt=xt, o=sq_c: e.activation(
                kb.TMP[0][:], xt[:], AF.Square, accum_out=o),
             waits=[ldi.done], incs=(esq := cs.inc()), tag=f"{tag}_sq{i}")
        mean_c = SM[:, 32 + i:33 + i]
        m2_c = SM[:, 96 + i:97 + i]
        var_c = SM[:, 112 + i:113 + i]
        lnv_c = SM[:, 128 + i:129 + i]
        rstd_c = SM[:, 48 + i:49 + i]
        nb_c = SM[:, 64 + i:65 + i]
        p.op("vector", lambda e, a=sum_c, o=mean_c: e.tensor_scalar_mul(
                o, a, 1.0 / D), waits=[esum.done], incs=(em := cv.inc()))
        p.op("scalar", lambda e, a=mean_c, o=m2_c: e.activation(
                o, a, AF.Square), waits=[em.done], incs=(em2 := cs.inc()))
        p.op("vector", lambda e, a=sq_c, b2=m2_c, o=var_c: e.scalar_tensor_tensor(
                o, a, 1.0 / D, b2, ALU.mult, ALU.subtract),
             waits=[esq.done, em2.done], incs=(ev0 := cv.inc()))
        p.op("vector", lambda e, o=var_c: e.tensor_scalar_add(o, o, EPS_LN),
             waits=[ev0.done], incs=(ev := cv.inc()))
        p.op("scalar", lambda e, a=var_c, o=lnv_c: e.activation(
                o, a, AF.Ln), waits=[ev.done],
             incs=(el := cs.inc()))
        p.op("scalar", lambda e, a=lnv_c, o=rstd_c: e.activation(
                o, a, AF.Exp, scale=-0.5), waits=[el.done],
             incs=(er := cs.inc()))
        p.op("vector", lambda e, a=mean_c, r=rstd_c, o=nb_c: e.scalar_tensor_tensor(
                o, a, -1.0, r, ALU.mult, ALU.mult),
             waits=[er.done], incs=(en := cv.inc()))
        wbuf = kb.WB[b]
        p.op("scalar", lambda e, xt=xt, r=rstd_c, nb=nb_c, o=wbuf: e.activation(
                o[:], xt[:], AF.Identity, bias=nb, scale=r),
             waits=[en.done] + _as_waits(wb_free[b]),
             incs=(ecp := cs.inc()), tag=f"{tag}_norm{i}")
        xt_free[b] = [ecp.done]
        extra = []
        if inner_stats is not None:
            s_c, q_c = inner_stats(i)
            p.op("vector", lambda e, wb=wbuf, o=s_c: e.reduce_sum(o, wb[:], AX.X),
                 waits=[ecp.done], incs=(eis := cv.inc()))
            p.op("scalar", lambda e, wb=wbuf, o=q_c: e.activation(
                    kb.TMP[0][:], wb[:], AF.Square, accum_out=o),
                 waits=[ecp.done], incs=(eiq := cs.inc()))
            extra = [eis.done, eiq.done]
            inner_done.append([eis.done, eiq.done])
        p.op("sync", lambda e, wbuf=wbuf, i=i: e.dma_start(
                out=xn_tm[128 * i:128 * (i + 1), :], in_=wbuf[:]),
             waits=[ecp.done], incs=(est := stc[b].inc(16)), tag=f"{tag}_st{i}")
        wb_free[b] = [est.done] + extra
    all_st = [stc[0].at(), stc[1].at()]
    fm_ready = []
    for dt in range(ND):
        p.op("sync", lambda e, dt=dt: e.dma_start_transpose(
                kb.xfm(dt), xn_tm[:, 128 * dt:128 * (dt + 1)]),
             waits=all_st if dt == 0 else None,
             incs=(etr := trc.inc(16)), tag=f"{tag}_tr{dt}")
    fm_all = trc.at()
    fm_ready = [fm_all] * ND
    return dict(fm_ready=fm_ready, tm_done=all_st, inner_done=inner_done,
                rstd=lambda i: kb.SM[:, 48 + i:49 + i],
                negb=lambda i: kb.SM[:, 64 + i:65 + i])


def emit_dequant(kb, start=None):
    p = kb.p
    wup_bf = kb.dram("wup_bf", (HID, D), BF16)
    wdn_bf = kb.dram("wdn_bf", (D, HID), BF16)
    cd = p.chan("dq_d")
    cqi = [p.chan("dq_i0"), p.chan("dq_i1")]
    cst = [p.chan("dq_s0"), p.chan("dq_s1")]
    cq = p.chan("dq_v")
    QI = [kb.carve(65536 + 8192 * i, 2048, I32) for i in range(2)]
    QF = kb.carve(65536 + 16384, 2048, F32)
    QS = kb.carve(65536 + 24576, 128, F32)
    QO = [kb.carve(65536 + 25088 + 4096 * i, 2048, BF16) for i in range(2)]
    qi_free = [None, None]
    qo_free = [None, None]
    sc_free = [None]

    def one(mat_q, mat_sc, out_bf, rows, cols, scdim, tagp):
        ncolt = cols // 2048
        scblk = scdim // ncolt
        for r in range(rows // 128):
            p.op("sync", lambda e, r=r: e.dma_start(
                    out=QS[:, 0:scdim],
                    in_=mat_sc[128 * r:128 * (r + 1), :]),
                 waits=_as_waits(sc_free[0]) + (_as_waits(start) if r == 0 else []),
                 incs=(lsc := cd.inc(16)), tag=f"{tagp}_sc{r}")
            last_mul = None
            for c in range(ncolt):
                b = (r * ncolt + c) % 2
                p.op("sync", lambda e, r=r, c=c, b=b: e.dma_start(
                        out=QI[b], in_=mat_q[128 * r:128 * (r + 1),
                                             2048 * c:2048 * (c + 1)]),
                     waits=qi_free[b], incs=(lqi := cqi[b].inc(16)),
                     tag=f"{tagp}_qi")
                p.op("vector", lambda e, b=b: e.tensor_scalar_add(
                        QF, QI[b], -7.5),
                     waits=[lqi.done], incs=(e1 := cq.inc()), tag=f"{tagp}_sub")
                qi_free[b] = [e1.done]

                def mulfn(e, c=c, b=b, scblk=scblk):
                    in0 = QF.rearrange("p (a v) -> p a v", v=64)
                    in1 = QS[:, scblk * c:scblk * (c + 1)].unsqueeze(
                        2).broadcast_to((128, 32, 64))
                    out = QO[b].rearrange("p (a v) -> p a v", v=64)
                    return e.tensor_tensor(out, in0, in1, ALU.mult)
                p.op("vector", mulfn,
                     waits=[e1.done, lsc.done] + _as_waits(qo_free[b]),
                     incs=(e2 := cq.inc()), tag=f"{tagp}_mul")
                last_mul = e2
                p.op("sync", lambda e, r=r, c=c, b=b: e.dma_start(
                        out=out_bf[128 * r:128 * (r + 1),
                                   2048 * c:2048 * (c + 1)], in_=QO[b]),
                     waits=[e2.done], incs=(e3 := cst[b].inc(16)),
                     tag=f"{tagp}_st")
                qo_free[b] = [e3.done]
            sc_free[0] = [last_mul.done]

    one(kb.din["wupq"], kb.din["wupsc"], wup_bf, HID, D, 32, "dqu")
    one(kb.din["wdnq"], kb.din["wdnsc"], wdn_bf, D, HID, 128, "dqd")
    return dict(wup_bf=wup_bf, wdn_bf=wdn_bf,
                done=[cst[0].at(), cst[1].at()])


def emit_ffn0(kb, x_in, x_out, ci, start=None):
    p = kb.p
    kb.act_set(SET_NL_EXP, waits=None)

    if kb.once("ffn0_setup"):
        dq = emit_dequant(kb, start=None)
        kb._flags["dq"] = dq
        sld = p.chan("ffn0_sld")
        lua_bf, w1 = kb.cast_weight("l0ua")
        lub_bf, w2 = kb.cast_weight("l0ub")
        dla_bf, w3 = kb.cast_weight("l0da")
        dlb_bf, w4 = kb.cast_weight("l0db")
        bdn_bf, w5 = kb.cast_weight("bdn_row")
        nc = kb.nc
        kb.LUA = nc.alloc_sbuf_tensor("LUA", [128, 256], BF16)
        kb.LUB = nc.alloc_sbuf_tensor("LUB", [16, 8192], BF16)
        kb.DLA = nc.alloc_sbuf_tensor("DLA", [128, 1024], BF16)
        kb.DLB = nc.alloc_sbuf_tensor("DLB", [16, 2048], BF16)
        kb.BUP = nc.alloc_sbuf_tensor("BUP", [128, 64], F32)
        p.op("sync", lambda e: e.dma_start(
                out=kb.LUA[:].rearrange("p (k r) -> p k r", r=16),
                in_=lua_bf.rearrange("(k p) r -> p k r", p=128)),
             waits=[w1], incs=sld.inc(16))
        p.op("sync", lambda e: e.dma_start(out=kb.LUB[:], in_=lub_bf),
             waits=[w2], incs=sld.inc(16))
        p.op("sync", lambda e: e.dma_start(
                out=kb.DLA[:].rearrange("p (k r) -> p k r", r=16),
                in_=dla_bf.rearrange("(k p) r -> p k r", p=128)),
             waits=[w3], incs=sld.inc(16))
        p.op("sync", lambda e: e.dma_start(out=kb.DLB[:], in_=dlb_bf),
             waits=[w4], incs=sld.inc(16))
        p.op("sync", lambda e: e.dma_start(out=kb.ROW1[:], in_=bdn_bf),
             waits=[w5], incs=sld.inc(16))
        p.op("sync", lambda e: e.dma_start(out=kb.BUP[:], in_=kb.din["bup_pd"]),
             incs=sld.inc(16))
        kb._flags["ffn0_small"] = sld.at()
    dq = kb._flags["dq"]
    small_ready = kb._flags["ffn0_small"]

    xn_tm = kb.dram(f"xn_tm", (S, D), BF16)
    g_nat = kb.dram(f"ffn_g", (HID, D), BF16)
    ln = emit_ln(kb, x_in, xn_tm, f"f{ci}ln", start=start)

    mmc = p.chan(f"f{ci}mm")
    evc = p.chan(f"f{ci}ev")
    wc = [p.chan(f"f{ci}wl0"), p.chan(f"f{ci}wl1")]
    wrc = p.chan(f"f{ci}wr")
    gc = [p.chan(f"f{ci}g0"), p.chan(f"f{ci}g1")]
    gl_c = [p.chan(f"f{ci}gl0"), p.chan(f"f{ci}gl1")]
    tdc = [p.chan(f"f{ci}td0"), p.chan(f"f{ci}td1")]
    dnc = [p.chan(f"f{ci}dn0"), p.chan(f"f{ci}dn1")]
    dnv = p.chan(f"f{ci}dnv")
    xoc = p.chan(f"f{ci}xo")

    for k in range(ND):
        for c in range(4):
            p.op("tensor", lambda e, k=k, c=c: e.matmul(
                    kb.PSB[0:16, 512 * c:512 * (c + 1)],
                    kb.LUA[:, 16 * k:16 * (k + 1)],
                    kb.xfm(k)[:, 512 * c:512 * (c + 1)],
                    start=(k == 0), stop=(k == ND - 1)),
                 waits=[ln["fm_ready"][k], small_ready] if c == 0 else None,
                 incs=(tup_mm := mmc.inc()) if (k == ND - 1 and c == 3) else None,
                 tag=f"f{ci}tup")
    for c in range(4):
        p.op("scalar", lambda e, c=c: e.activation(
                kb.TUP[:, 512 * c:512 * (c + 1)],
                kb.PSB[0:16, 512 * c:512 * (c + 1)], AF.Copy),
             waits=[tup_mm.done] if c == 0 else None,
             incs=(tup_ev := evc.inc()), tag=f"f{ci}tupev")
    tup_done = tup_ev.done

    UPW = [kb.carve(65536 + 16384 * i, 8192, BF16) for i in range(2)]
    GF = [kb.carve(98304 + 4096 * i, 2048, BF16) for i in range(2)]
    upw_free = [None, None]
    gf_free = [None, None]
    bank_free = {0: None, 1: [tup_done]}
    g_done_all = []
    for mq in range(NH // 4):
        wb = mq % 2
        for k in range(ND):
            p.op("sync", lambda e, mq=mq, k=k, wb=wb: e.dma_start_transpose(
                    UPW[wb][:, 512 * k:512 * (k + 1)],
                    dq["wup_bf"][512 * mq:512 * (mq + 1),
                                 128 * k:128 * (k + 1)]),
                 waits=(dq["done"] if mq < 2 and k == 0 else []) +
                       (_as_waits(upw_free[wb]) if k == 0 else []),
                 incs=(wl := wc[wb].inc(16)), tag=f"f{ci}upw{mq}_{k}")
        wl_done = wl.done
        for mm in range(4):
            m = 4 * mq + mm
            pb = m % 2
            banks = ([kb.PSS[c][:, 0:512] for c in range(4)] if pb == 0 else
                     [kb.PSB[:, 512 * c:512 * (c + 1)] for c in range(4)])
            for k in range(ND):
                for c in range(4):
                    w = []
                    if k == 0 and c == 0:
                        w = [wl_done] + _as_waits(bank_free[pb])
                        if mq == 0 and mm == 0:
                            w += [ln["fm_ready"][ND - 1], tup_done]
                    p.op("tensor", lambda e, k=k, c=c, m=m, wb=wb, banks=banks: e.matmul(
                            banks[c],
                            UPW[wb][:, 512 * k + 128 * (m % 4):512 * k + 128 * (m % 4) + 128],
                            kb.xfm(k)[:, 512 * c:512 * (c + 1)],
                            start=(k == 0), stop=False),
                         waits=w, tag=f"f{ci}up{m}")
            for c in range(4):
                p.op("tensor", lambda e, c=c, m=m, banks=banks: e.matmul(
                        banks[c], kb.LUB[:, 128 * m:128 * (m + 1)],
                        kb.TUP[:, 512 * c:512 * (c + 1)],
                        start=False, stop=True),
                     incs=(ul := mmc.inc()) if c == 3 else None,
                     tag=f"f{ci}uplora{m}")
            gfb = m % 2
            for c in range(4):
                p.op("scalar", lambda e, c=c, m=m, gfb=gfb, banks=banks: e.activation(
                        GF[gfb][:, 512 * c:512 * (c + 1)], banks[c],
                        AF.Relu, bias=kb.BUP[:, m:m + 1], scale=1.0),
                     waits=([ul.done] + _as_waits(gf_free[gfb])) if c == 0 else None,
                     incs=(ge := evc.inc()), tag=f"f{ci}gev{m}")
            bank_free[pb] = [ge.done]
            p.op("sync", lambda e, m=m, gfb=gfb: e.dma_start(
                    out=g_nat[128 * m:128 * (m + 1), :], in_=GF[gfb]),
                 waits=[ge.done], incs=(gs := gc[gfb].inc(16)), tag=f"f{ci}gst{m}")
            gf_free[gfb] = [gs.done]

        upw_free[wb] = [_Wait(mmc, mmc.count)]
    up_all_mm = _Wait(mmc, mmc.count)
    g_done_all = [gc[0].at(), gc[1].at()]

    td_mm_done = [None, None]
    for m in range(NH):
        b = m % 2
        p.op("sync", lambda e, m=m, b=b: e.dma_start(
                out=kb.WB[b][:], in_=g_nat[128 * m:128 * (m + 1), :]),
             waits=(g_done_all + _as_waits(bank_free[1]) if m == 0 else
                    _as_waits(td_mm_done[b])),
             incs=(gl := tdc[b].inc(16)), tag=f"f{ci}tdl{m}")
        for c in range(4):
            p.op("tensor", lambda e, m=m, b=b, c=c: e.matmul(
                    kb.PSB[0:16, 512 * c:512 * (c + 1)],
                    kb.DLA[:, 16 * m:16 * (m + 1)],
                    kb.WB[b][:, 512 * c:512 * (c + 1)],
                    start=(m == 0), stop=(m == NH - 1)),
                 waits=[gl.done, up_all_mm] if c == 0 and m == 0 else
                       ([gl.done] if c == 0 else None),
                 incs=(tdm := mmc.inc()) if c == 3 else None,
                 tag=f"f{ci}tdmm")
        td_mm_done[b] = [tdm.done]
    for c in range(4):
        p.op("scalar", lambda e, c=c: e.activation(
                kb.TDN[:, 512 * c:512 * (c + 1)],
                kb.PSB[0:16, 512 * c:512 * (c + 1)], AF.Copy),
             waits=[tdm.done] if c == 0 else None,
             incs=(tdn_ev := evc.inc()), tag=f"f{ci}tdnev")
    tdn_done = tdn_ev.done

    WRESQ = kb.carve(0, 64 * 512, BF16)
    GSB = [kb.carve(65536 + 16384 * i, 8192, BF16) for i in range(2)]
    banks8 = [kb.PSS[i][:, 0:512] for i in range(4)] + \
             [kb.PSB[:, 512 * i:512 * (i + 1)] for i in range(4)]
    bank8_free = [bank_free[0]] * 4 + [[tdn_done]] * 4
    gsb_free = [None, None]
    xt_free = [None, None]
    wres_free = None
    for qtr in range(4):
        for k in range(NH):
            p.op("sync", lambda e, qtr=qtr, k=k: e.dma_start_transpose(
                    WRESQ[:, 512 * k:512 * (k + 1)],
                    dq["wdn_bf"][512 * qtr:512 * (qtr + 1),
                                 128 * k:128 * (k + 1)]),
                 waits=([up_all_mm, tdn_done] + _as_waits(wres_free))
                       if k == 0 else None,
                 incs=(wr := wrc.inc(16)), tag=f"f{ci}dw{qtr}_{k}")
        wres_ready = wr.done
        for j in range(NT):
            gb = j % 2
            p.op("gpsimd", lambda e, j=j, gb=gb: e.dma_start(
                    out=GSB[gb].rearrange("p (k t) -> p k t", t=128),
                    in_=g_nat[:, 128 * j:128 * (j + 1)].rearrange(
                        "(k p) t -> p k t", p=128)),
                 waits=(g_done_all if (qtr == 0 and j < 2) else []) +
                       _as_waits(gsb_free[gb]),
                 incs=(gl2 := gl_c[gb].inc(16)), tag=f"f{ci}gsl{qtr}_{j}")
            bk = (qtr * NT + j) % 8
            for k in range(NH):
                w = []
                if k == 0:
                    w = [gl2.done, wres_ready] + _as_waits(bank8_free[bk])
                p.op("tensor", lambda e, k=k, gb=gb, bk=bk: e.matmul(
                        banks8[bk], GSB[gb][:, 128 * k:128 * (k + 1)],
                        WRESQ[:, 512 * k:512 * (k + 1)],
                        start=(k == 0), stop=False),
                     waits=w, tag=f"f{ci}dn{qtr}_{j}")
            p.op("tensor", lambda e, j=j, qtr=qtr, bk=bk: e.matmul(
                    banks8[bk], kb.TDN[:, 128 * j:128 * (j + 1)],
                    kb.DLB[:, 512 * qtr:512 * (qtr + 1)],
                    start=False, stop=False),
                 waits=[tdn_done] if (qtr == 0 and j == 0) else None,
                 tag=f"f{ci}dnlora")
            p.op("tensor", lambda e, qtr=qtr, bk=bk: e.matmul(
                    banks8[bk], kb.onesr[:],
                    kb.ROW1[:, 512 * qtr:512 * (qtr + 1)],
                    start=False, stop=True),
                 incs=(dmm := mmc.inc()), tag=f"f{ci}dnbias")
            gsb_free[gb] = [_Wait(mmc, mmc.count)]
            xb = j % 2
            p.op("sync", lambda e, j=j, qtr=qtr, xb=xb: e.dma_start(
                    out=kb.XT[xb][:, 0:512],
                    in_=x_in[128 * j:128 * (j + 1),
                             512 * qtr:512 * (qtr + 1)]),
                 waits=_as_waits(xt_free[xb]), incs=(xl := dnc[xb].inc(16)),
                 tag=f"f{ci}xl{qtr}_{j}")
            p.op("vector", lambda e, xb=xb, bk=bk: e.scalar_tensor_tensor(
                    kb.XT[xb][:, 512:1024], banks8[bk], 0.5,
                    kb.XT[xb][:, 0:512], ALU.mult, ALU.add),
                 waits=[dmm.done, xl.done], incs=(add := dnv.inc()),
                 tag=f"f{ci}add{qtr}_{j}")
            bank8_free[bk] = [add.done]
            p.op("sync", lambda e, j=j, qtr=qtr, xb=xb: e.dma_start(
                    out=x_out[128 * j:128 * (j + 1),
                              512 * qtr:512 * (qtr + 1)],
                    in_=kb.XT[xb][:, 512:1024]),
                 waits=[add.done], incs=(xo := xoc.inc(16)),
                 tag=f"f{ci}xo{qtr}_{j}")
            xt_free[xb] = [add.done]
        wres_free = [_Wait(mmc, mmc.count)]
    return dict(done=xoc.at(), xoc=xoc)


def emit_mamba(kb, x_in, x_out, start=None):
    p, nc = kb.p, kb.nc
    kb.act_set(SET_NL_EXP)

    if kb.once("mamba_setup"):
        sld = p.chan("mb_sld")
        sldv = p.chan("mb_sldv")
        slds = p.chan("mb_slds")
        wx3_bf, w1 = kb.cast_weight("wx3T")
        wdt_bf, w2 = kb.cast_weight("wdtT")
        kb.MWX = nc.alloc_sbuf_tensor("MWX", [128, 16 * 96], BF16)
        kb.MWDT = nc.alloc_sbuf_tensor("MWDT", [64, 2048], BF16)
        kb.MBDT = nc.alloc_sbuf_tensor("MBDT", [128, 16], F32)
        kb.MA = nc.alloc_sbuf_tensor("MA", [128, 256], F32)
        kb.MD = nc.alloc_sbuf_tensor("MD", [128, 16], F32)
        kb.SEL = nc.alloc_sbuf_tensor("SEL", [16, 2048], BF16)
        p.op("sync", lambda e: e.dma_start(
                out=kb.MWX[:].rearrange("p (k m) -> p k m", m=96),
                in_=wx3_bf.rearrange("(k p) m -> p k m", p=128)),
             waits=[w1], incs=sld.inc(16))
        p.op("sync", lambda e: e.dma_start(out=kb.MWDT[:], in_=wdt_bf),
             waits=[w2], incs=sld.inc(16))
        p.op("sync", lambda e: e.dma_start(out=kb.MBDT[:], in_=kb.din["bdt_pd"]),
             incs=sld.inc(16))
        p.op("sync", lambda e: e.dma_start(
                out=kb.MA[:].rearrange("p (k n) -> p k n", n=16),
                in_=kb.din["alog"].rearrange("(k p) n -> p k n", p=128)),
             incs=(la := sld.inc(16)))
        p.op("sync", lambda e: e.dma_start(out=kb.MD[:], in_=kb.din["d_pd"]),
             incs=sld.inc(16))
        p.op("sync", lambda e: e.dma_start(out=kb.SEL[:], in_=kb.din["sel16_bf"]),
             incs=sld.inc(16))
        p.op("scalar", lambda e: e.activation(kb.MA[:], kb.MA[:], AF.Exp),
             waits=[sld.at()], incs=(ea := slds.inc()))
        p.op("vector", lambda e: e.tensor_scalar_mul(kb.MA[:], kb.MA[:], -1.0),
             waits=[ea.done], incs=(eneg := sldv.inc()))
        kb._flags["mb_small"] = [sld.at(), sldv.at()]
    small = kb._flags["mb_small"]

    xn_tm = kb.dram("mb_xn_tm", (S, D), BF16)
    y_fm = kb.dram("mb_y_fm", (D, S), BF16)
    ln = emit_ln(kb, x_in, xn_tm, "mbln", start=start)

    DPB = kb.carve(65536, 2048, BF16)[0:64, :]
    BCT = kb.carve(69632, 2048, BF16)[0:16, :]
    CCT = kb.carve(73728, 2048, BF16)[0:16, :]
    DELTA = kb.carve(77824, 2048, F32)
    DU = kb.carve(86016, 2048, F32)
    AEXP = kb.carve(94208, 2048, F32)
    UBUF = kb.carve(102400, 2048, F32)
    HBUF = kb.carve(110592, 2048, BF16)
    YM = kb.carve(114688, 2048, BF16)
    YACC = kb.carve(118784, 2048, F32)
    YTB = kb.carve(126976, 2048, BF16)

    mm = p.chan("mb_mm")
    ev = p.chan("mb_ev")
    dv = p.chan("mb_dv")
    yst = p.chan("mb_yst")
    xoc = p.chan("mb_xo")

    banksets = [(kb.PSS[0][0:64, :], kb.PSS[1][0:16, :], kb.PSS[2][0:16, :]),
                (kb.PSS[3][0:64, :], kb.PSB[0:16, 0:512], kb.PSB[0:16, 512:1024])]
    bs_free = [None, None]
    for c in range(4):
        bs = banksets[c % 2]
        for k in range(ND):
            w = []
            if k == 0:
                w = [ln["fm_ready"][k]] + small + _as_waits(bs_free[c % 2])
            else:
                w = [ln["fm_ready"][k]] if c == 0 else []
            for j, (lo, hi) in enumerate([(0, 64), (64, 80), (80, 96)]):
                p.op("tensor", lambda e, k=k, c=c, j=j, lo=lo, hi=hi, bs=bs: e.matmul(
                        bs[j], kb.MWX[:, 96 * k + lo:96 * k + hi],
                        kb.xfm(k)[:, 512 * c:512 * (c + 1)],
                        start=(k == 0), stop=(k == ND - 1)),
                     waits=w if j == 0 else None,
                     incs=(xmm := mm.inc()) if (k == ND - 1 and j == 2) else None,
                     tag=f"mb_xp{c}")
        for j, dst in enumerate([DPB, BCT, CCT]):
            p.op("scalar", lambda e, j=j, c=c, dst=dst, bs=bs: e.activation(
                    dst[:, 512 * c:512 * (c + 1)], bs[j], AF.Identity),
                 waits=[xmm.done] if j == 0 else None,
                 incs=(xev := ev.inc()), tag=f"mb_xpev{c}")
        bs_free[c % 2] = [xev.done]
    xp_done = xev.done

    delta_mm_free = _Wait(ev, ev.count)
    ymul_prev = None
    umul_prev = None
    scan_prev = None
    tmp0_free = None
    delta_free = None
    ytb_free = None
    kb._mb_wb_free = [None, None]
    kb._mb_xt_free = [None, None]
    for dt in range(ND):
        for c in range(4):
            p.op("tensor", lambda e, dt=dt, c=c: e.matmul(
                    kb.PSB[:, 512 * c:512 * (c + 1)],
                    kb.MWDT[:, 128 * dt:128 * (dt + 1)],
                    DPB[:, 512 * c:512 * (c + 1)],
                    start=True, stop=True),
                 waits=([xp_done, delta_mm_free] if c == 0 else None),
                 incs=(dmm := mm.inc()) if c == 3 else None, tag=f"mb_dl{dt}")
        p.op("scalar", lambda e, dt=dt: e.activation(
                kb.TMP[0][:], kb.PSB[:], AF.Identity,
                bias=kb.MBDT[:, dt:dt + 1], scale=1.0),
             waits=[dmm.done] + _as_waits(scan_prev) + _as_waits(tmp0_free),
             incs=(p0 := ev.inc()), tag=f"mb_sp0_{dt}")
        delta_mm_free = p0.done
        p.op("scalar", lambda e: e.activation(AEXP, kb.TMP[0][:], AF.Abs),
             incs=(p1 := ev.inc()))
        p.op("scalar", lambda e: e.activation(AEXP, AEXP, AF.Exp, scale=-1.0),
             incs=(p2 := ev.inc()))
        p.op("scalar", lambda e: e.activation(AEXP, AEXP, AF.Ln, bias=1.0),
             incs=(p3 := ev.inc()))
        p.op("vector", lambda e: e.tensor_scalar_max(UBUF, kb.TMP[0][:], 0.0),
             waits=[p0.done] + _as_waits(scan_prev), incs=(p4 := dv.inc()))
        tmp0_free = [p4.done]
        p.op("vector", lambda e: e.tensor_add(DELTA, UBUF, AEXP),
             waits=[p3.done, p4.done] + _as_waits(delta_free),
             incs=(p5 := dv.inc()), tag=f"mb_sp5_{dt}")
        p.op("vector", lambda e, dt=dt: e.tensor_mul(DU, DELTA, kb.xfm(dt)),
             waits=[p5.done], incs=(pdu := dv.inc()), tag=f"mb_du{dt}")
        p.op("vector", lambda e, dt=dt: e.tensor_scalar_mul(
                YACC, kb.xfm(dt), kb.MD[:, dt:dt + 1]),
             waits=[pdu.done], incs=(py0 := dv.inc()), tag=f"mb_xd{dt}")
        yadd_prev = py0
        for n in range(16):
            for c in range(4):
                p.op("tensor", lambda e, n=n, c=c: e.matmul(
                        kb.PSB[:, 512 * c:512 * (c + 1)],
                        kb.SEL[:, 128 * n:128 * (n + 1)],
                        BCT[:, 512 * c:512 * (c + 1)],
                        start=True, stop=True),
                     waits=([delta_mm_free] + _as_waits(umul_prev))
                           if c == 0 else None,
                     incs=(bmm := mm.inc()) if c == 3 else None,
                     tag=f"mb_bb{dt}_{n}")
            for c in range(4):
                p.op("tensor", lambda e, n=n, c=c: e.matmul(
                        kb.PSS[c][:], kb.SEL[:, 128 * n:128 * (n + 1)],
                        CCT[:, 512 * c:512 * (c + 1)],
                        start=True, stop=True),
                     waits=(_as_waits(ymul_prev) + ([xp_done] if dt == 0 and n == 0 else []))
                           if c == 0 else None,
                     incs=(cmm := mm.inc()) if c == 3 else None,
                     tag=f"mb_cb{dt}_{n}")
            p.op("scalar", lambda e, dt=dt, n=n: e.activation(
                    AEXP, DELTA, AF.Exp, scale=kb.MA[:, 16 * dt + n:16 * dt + n + 1]),
                 waits=[p5.done] if n == 0 else _as_waits(scan_prev),
                 incs=(pa := ev.inc()), tag=f"mb_a{dt}_{n}")
            if n == 15:
                delta_free = [pa.done]
            p.op("vector", lambda e: e.tensor_mul(UBUF, DU, kb.PSB[:]),
                 waits=[bmm.done, pdu.done] + _as_waits(scan_prev),
                 incs=(pu := dv.inc()), tag=f"mb_u{dt}_{n}")
            umul_prev = [pu.done]
            delta_mm_free = pu.done
            p.op("vector", lambda e: e.tensor_tensor_scan(
                    HBUF, AEXP, UBUF, 0.0, ALU.mult, ALU.add),
                 waits=[pa.done] + _as_waits(ymul_prev),
                 incs=(ph := dv.inc()), tag=f"mb_h{dt}_{n}")
            scan_prev = [ph.done]
            for c in range(4):
                p.op("vector", lambda e, c=c: e.tensor_mul(
                        YM[:, 512 * c:512 * (c + 1)], HBUF[:, 512 * c:512 * (c + 1)],
                        kb.PSS[c][:]),
                     waits=[ph.done, cmm.done] if c == 0 else None,
                     incs=(pm := dv.inc()), tag=f"mb_ym{dt}_{n}")
            ymul_prev = [pm.done]
            p.op("vector", lambda e: e.tensor_add(YACC, YACC, YM),
                 waits=[pm.done, yadd_prev.done], incs=(pya := dv.inc()),
                 tag=f"mb_ya{dt}_{n}")
            yadd_prev = pya
        p.op("vector", lambda e: e.tensor_copy(YTB, YACC),
             waits=[pya.done] + _as_waits(ytb_free),
             incs=(pyt := dv.inc()), tag=f"mb_yt{dt}")
        p.op("sync", lambda e, dt=dt: e.dma_start(
                out=y_fm[128 * dt:128 * (dt + 1), :], in_=YTB),
             waits=[pyt.done], incs=(ys := yst.inc(16)), tag=f"mb_yst{dt}")
        ytb_free = [ys.done]
    y_all = yst.at()

    trl = [p.chan("mb_tl0"), p.chan("mb_tl1")]
    xlc = [p.chan("mb_xl0"), p.chan("mb_xl1")]
    add_prev = None
    for j in range(NT):
        wbuf = kb.WB[j % 2]
        p.op("sync", lambda e, j=j, wbuf=wbuf: e.dma_start_transpose(
                wbuf[:], y_fm[:, 128 * j:128 * (j + 1)]),
             waits=[y_all] if j < 2 else _as_waits(kb._mb_wb_free[j % 2]),
             incs=(ty := trl[j % 2].inc(16)), tag=f"mb_ytr{j}")
        p.op("sync", lambda e, j=j: e.dma_start(
                out=kb.XT[j % 2][:], in_=x_in[128 * j:128 * (j + 1), :]),
             waits=_as_waits(kb._mb_xt_free[j % 2]) if j >= 2 else [],
             incs=(xl := xlc[j % 2].inc(16)), tag=f"mb_xl{j}")
        p.op("vector", lambda e, j=j, wbuf=wbuf: e.tensor_add(
                kb.TMP[0][:], kb.XT[j % 2][:], wbuf[:]),
             waits=[ty.done, xl.done] + _as_waits(add_prev),
             incs=(pad := dv.inc()), tag=f"mb_add{j}")
        p.op("sync", lambda e, j=j: e.dma_start(
                out=x_out[128 * j:128 * (j + 1), :], in_=kb.TMP[0][:]),
             waits=[pad.done], incs=(xo := xoc.inc(16)), tag=f"mb_xo{j}")
        add_prev = [xo.done]
        kb._mb_wb_free[j % 2] = [pad.done]
        kb._mb_xt_free[j % 2] = [pad.done]
    return dict(done=xoc.at())


MLA_SCALE = float((128 + 64) ** -0.5)


def emit_mla(kb, x_in, x_out, start=None, stages=("proj", "heads", "wo"),
             nhead=16):
    p, nc = kb.p, kb.nc
    kb.act_set(SET_NL_EXP)

    if kb.once("mla_setup"):
        sld = p.chan("ml_sld")
        for w in ["wqTr", "wkvaTx", "wkvbn", "wkvbvT", "woT"]:
            kb.cast_weight(w)
        kb.KVW = nc.alloc_sbuf_tensor("KVW", [128, 4], F32)
        kb.PSW = nc.alloc_sbuf_tensor("PSW", [128, 128], BF16)
        mskr = kb.ensure_msk4()
        p.op("sync", lambda e: e.dma_start(out=kb.KVW[:], in_=kb.din["kvw_pd"]),
             incs=sld.inc(16))
        p.op("sync", lambda e: e.dma_start(out=kb.PSW[:], in_=kb.din["pswap_bf"]),
             incs=sld.inc(16))
        kb._flags["ml_small"] = [sld.at(), mskr]
    small = kb._flags["ml_small"]

    xn_tm = kb.dram("ml_xn_tm", (S, D), BF16)
    qn_hbm = kb.dram("ml_qn", (2048, S), BF16)
    qp_hbm = kb.dram("ml_qp", (1024, S), BF16)
    kvc_hbm = kb.dram("ml_kvc", (512, S), BF16)
    ov_hbm = kb.dram("ml_ov", (D, S), BF16)
    wq_bf = kb.scr["wqTr_bf"]
    wkva_bf = kb.scr["wkvaTx_bf"]
    wkvbn_bf = kb.scr["wkvbn_bf"]
    wkvbv_bf = kb.scr["wkvbvT_bf"]
    wo_bf = kb.scr["woT_bf"]
    cwaits = [kb._flags[k + "_bf"] for k in
              ["wqTr", "wkvaTx", "wkvbn", "wkvbvT", "woT"]]

    ln = emit_ln(kb, x_in, xn_tm, "mlln", start=start)

    mm = p.chan("ml_mm")
    ev = p.chan("ml_ev")
    dv = p.chan("ml_dv")
    wl = [p.chan("ml_wl0"), p.chan("ml_wl1")]
    rt = p.chan("ml_rt")
    kvtc = p.chan("ml_kvt")
    qst = p.chan("ml_qst")
    xoc = p.chan("ml_xo")

    ROPC = kb.carve(65536, 2048, BF16)
    ROPS = kb.carve(69632, 2048, BF16)
    WQB = [kb.carve(73728 + 4096 * i, 2048, BF16) for i in range(2)]
    QNB = kb.carve(81920, 2048, BF16)
    RPB = kb.carve(86016, 2048, BF16)
    RT2 = kb.carve(94208, 2048, BF16)
    QPB = kb.carve(90112, 2048, BF16)
    SQB = kb.carve(98304, 2048, BF16)
    KVL = [kb.carve(102400 + 4096 * i, 2048, BF16) for i in range(4)]
    KPE = kb.carve(118784, 2048, BF16)
    RSR = kb.carve(122880, 2048, F32)[0:1, :]
    p.op("sync", lambda e: e.dma_start(out=ROPC, in_=kb.din["ropecos_bf"]),
         incs=(rc := rt.inc(16)), tag="ml_ropec")
    p.op("sync", lambda e: e.dma_start(out=ROPS, in_=kb.din["ropesin_bf"]),
         incs=(rs := rt.inc(16)), tag="ml_ropes")
    rope_ready = [rt.at()]

    def rope_pair(src_bf, swap_psum_set, out_bf, waits_src, tagp):
        toks = []
        for c in range(4):
            p.op("tensor", lambda e, c=c: e.matmul(
                    swap_psum_set[c], kb.PSW[:],
                    src_bf[:, 512 * c:512 * (c + 1)], start=True, stop=True),
                 waits=(waits_src + small) if c == 0 else None,
                 incs=(sw := mm.inc()) if c == 3 else None, tag=tagp + "sw")
        for c in range(4):
            p.op("vector", lambda e, c=c: e.tensor_mul(
                    out_bf[:, 512 * c:512 * (c + 1)],
                    src_bf[:, 512 * c:512 * (c + 1)],
                    ROPC[:, 512 * c:512 * (c + 1)]),
                 waits=([sw.done] + rope_ready) if c == 0 else None,
                 incs=dv.inc(), tag=tagp + "c")
            p.op("vector", lambda e, c=c: e.tensor_mul(
                    RT2[:, 512 * c:512 * (c + 1)], swap_psum_set[c],
                    ROPS[:, 512 * c:512 * (c + 1)]),
                 incs=dv.inc(), tag=tagp + "s")
            p.op("vector", lambda e, c=c: e.tensor_add(
                    out_bf[:, 512 * c:512 * (c + 1)],
                    out_bf[:, 512 * c:512 * (c + 1)],
                    RT2[:, 512 * c:512 * (c + 1)]),
                 incs=(tk := dv.inc()), tag=tagp + "a")
            toks.append(tk)
        return toks[-1]

    wqb_free = [None, None]
    rope_busy = None
    qnb_free = None
    rpb_free = None
    pss_free = None
    for m in range(24):
        b = m % 2
        p.op("sync", lambda e, m=m, b=b: e.dma_start(
                out=WQB[b].rearrange("p (k m) -> p k m", m=128),
                in_=wq_bf[:, 128 * m:128 * (m + 1)].rearrange(
                    "(k p) m -> p k m", p=128)),
             waits=[cwaits[0]] + _as_waits(wqb_free[b]) if m < 2 else
                   _as_waits(wqb_free[b]),
             incs=(qw := wl[b].inc(16)), tag=f"ml_qw{m}")
        for k in range(ND):
            for c in range(4):
                w = None
                if k == 0 and c == 0:
                    w = [qw.done] + _as_waits(pss_free)
                    if m == 0:
                        w += [ln["fm_ready"][ND - 1]]
                elif c == 0 and m == 0:
                    w = [ln["fm_ready"][k]]
                p.op("tensor", lambda e, k=k, c=c, b=b: e.matmul(
                        kb.PSS[c][:], WQB[b][:, 128 * k:128 * (k + 1)],
                        kb.xfm(k)[:, 512 * c:512 * (c + 1)],
                        start=(k == 0), stop=(k == ND - 1)),
                     waits=w,
                     incs=(qm := mm.inc()) if (k == ND - 1 and c == 3) else None,
                     tag=f"ml_q{m}")
        wqb_free[b] = [qm.done]
        if m < 16:
            for c in range(4):
                p.op("scalar", lambda e, c=c: e.activation(
                        QNB[:, 512 * c:512 * (c + 1)], kb.PSS[c][:], AF.Identity),
                     waits=([qm.done] + _as_waits(qnb_free)) if c == 0 else None,
                     incs=(qe := ev.inc()), tag=f"ml_qev{m}")
            pss_free = [qe.done]
            p.op("sync", lambda e, m=m: e.dma_start(
                    out=qn_hbm[128 * m:128 * (m + 1), :], in_=QNB),
                 waits=[qe.done], incs=(qs := qst.inc(16)), tag=f"ml_qnst{m}")
            qnb_free = [qs.done]
        else:
            for c in range(4):
                p.op("scalar", lambda e, c=c: e.activation(
                        RPB[:, 512 * c:512 * (c + 1)], kb.PSS[c][:], AF.Identity),
                     waits=([qm.done] + _as_waits(rpb_free)) if c == 0 else None,
                     incs=(qe := ev.inc()), tag=f"ml_qpev{m}")
            tk = rope_pair(RPB, [kb.PSS[c][:] for c in range(4)], QPB,
                           [qe.done] + _as_waits(rope_busy), f"ml_rp{m}")
            rope_busy = [tk.done]
            pss_free = [tk.done]
            rpb_free = [tk.done]
            pr = m - 16
            p.op("sync", lambda e, pr=pr: e.dma_start(
                    out=qp_hbm[128 * pr:128 * (pr + 1), :], in_=QPB),
                 waits=[tk.done], incs=(qs := qst.inc(16)), tag=f"ml_qpst{pr}")
            rope_busy = [qs.done]
    q_all = qst.at()

    sq_mm = None
    for m in range(5):
        b = m % 2
        p.op("sync", lambda e, m=m, b=b: e.dma_start(
                out=WQB[b].rearrange("p (k m) -> p k m", m=128),
                in_=wkva_bf[:, 128 * m:128 * (m + 1)].rearrange(
                    "(k p) m -> p k m", p=128)),
             waits=[cwaits[1]] + _as_waits(wqb_free[b]),
             incs=(kw := wl[b].inc(16)), tag=f"ml_kw{m}")
        for k in range(ND):
            for c in range(4):
                p.op("tensor", lambda e, k=k, c=c, b=b: e.matmul(
                        kb.PSS[c][:], WQB[b][:, 128 * k:128 * (k + 1)],
                        kb.xfm(k)[:, 512 * c:512 * (c + 1)],
                        start=(k == 0), stop=(k == ND - 1)),
                     waits=([kw.done] + _as_waits(pss_free))
                           if (k == 0 and c == 0) else None,
                     incs=(km := mm.inc()) if (k == ND - 1 and c == 3) else None,
                     tag=f"ml_kv{m}")
        wqb_free[b] = [km.done]
        if m < 4:
            for c in range(4):
                p.op("scalar", lambda e, c=c: e.activation(
                        kb.TMP[0][:, 512 * c:512 * (c + 1)], kb.PSS[c][:],
                        AF.Identity),
                     waits=([km.done] + (_as_waits([kc.done]) if m > 0 else []))
                           if c == 0 else None,
                     incs=(ke := ev.inc()), tag=f"ml_kvev{m}")
            p.op("vector", lambda e, m=m: e.tensor_copy(KVL[m], kb.TMP[0][:]),
                 waits=[ke.done], incs=(kc := dv.inc()), tag=f"ml_kvcp{m}")
            p.op("scalar", lambda e: e.activation(SQB, kb.TMP[0][:], AF.Square),
                 waits=([ke.done] + ([sq_mm.done] if m > 0 else [])),
                 incs=(ksq := ev.inc()), tag=f"ml_sq{m}")
            pss_free = [ke.done]
            for c in range(4):
                p.op("tensor", lambda e, c=c, m=m: e.matmul(
                        kb.PSB[0:1, 512 * c:512 * (c + 1)], kb.onesc[:],
                        SQB[:, 512 * c:512 * (c + 1)],
                        start=(m == 0), stop=(m == 3)),
                     waits=[ksq.done] if c == 0 else None,
                     incs=(sq_mm := mm.inc()) if c == 3 else None,
                     tag=f"ml_sqmm{m}")
        else:
            for c in range(4):
                p.op("scalar", lambda e, c=c: e.activation(
                        RPB[:, 512 * c:512 * (c + 1)], kb.PSS[c][:], AF.Identity),
                     waits=([km.done] + _as_waits(rpb_free)) if c == 0 else None,
                     incs=(ke := ev.inc()), tag=f"ml_kpev")
            tk = rope_pair(RPB, [kb.PSS[c][:] for c in range(4)], KPE,
                           [ke.done] + _as_waits(rope_busy), "ml_kpe")
            pss_free = [tk.done]
    kpe_done = tk.done

    p.op("vector", lambda e: e.tensor_scalar(
            RSR, kb.PSB[0:1, 0:2048], 1.0 / 512, EPS_RMS, ALU.mult, ALU.add),
         waits=[sq_mm.done], incs=(r1 := dv.inc()), tag="ml_rs1")
    p.op("scalar", lambda e: e.activation(RSR, RSR, AF.Ln),
         waits=[r1.done], incs=(r2 := ev.inc()))
    p.op("scalar", lambda e: e.activation(RSR, RSR, AF.Exp, scale=-0.5),
         incs=(r3 := ev.inc()))
    p.op("vector", lambda e: e.tensor_copy(kb.ROW2[:], RSR),
         waits=[r3.done], incs=(r4 := dv.inc()), tag="ml_rs4")
    for c in range(4):
        p.op("tensor", lambda e, c=c: e.matmul(
                kb.PSS[c][:], kb.onesr[:], kb.ROW2[0:1, 512 * c:512 * (c + 1)],
                start=True, stop=True),
             waits=([r4.done] + _as_waits(pss_free)) if c == 0 else None,
             incs=(rb := mm.inc()) if c == 3 else None, tag="ml_rbc")
    for ct in range(4):
        for c in range(4):
            p.op("vector", lambda e, ct=ct, c=c: e.scalar_tensor_tensor(
                    KVL[ct][:, 512 * c:512 * (c + 1)],
                    KVL[ct][:, 512 * c:512 * (c + 1)],
                    kb.KVW[:, ct:ct + 1], kb.PSS[c][:], ALU.mult, ALU.mult),
                 waits=[rb.done, small[0]] if (ct == 0 and c == 0) else None,
                 incs=(kvn := dv.inc()), tag=f"ml_kvn{ct}")
    kvc_done = kvn.done
    pss_free = [kvn.done]
    for ct in range(4):
        p.op("sync", lambda e, ct=ct: e.dma_start(
                out=kvc_hbm[128 * ct:128 * (ct + 1), :], in_=KVL[ct]),
             waits=[kvc_done], incs=(ks2 := qst.inc(16)), tag=f"ml_kvst{ct}")
    kv_st = ks2.done
    if "heads" not in stages:
        return dict(done=_Wait(qst, qst.count))

    KVT = [kb.carve(1024 * j, 512, BF16) for j in range(16)]
    ATN = [kb.carve(16384 + 1024 * j, 512, BF16) for j in range(16)]
    PRB = [kb.carve(32768 + 4096 * i, 2048, BF16) for i in range(4)]
    QLT = [kb.carve(49152 + 4096 * i, 2048, BF16) for i in range(4)]
    QNH = [kb.carve(65536 + 4096 * i, 2048, BF16) for i in range(2)]
    QPH = [kb.carve(73728 + 4096 * i, 2048, BF16) for i in range(2)]
    OLB = kb.carve(81920, 2048, BF16)
    OVH = kb.carve(86016, 2048, BF16)
    WKB = [kb.carve(90112 + 1024 * i, 512, BF16) for i in range(2)]
    WVB = [kb.carve(92160 + 1024 * i, 512, BF16) for i in range(2)]

    hlc = {k: [p.chan(f"ml_h{k}0"), p.chan(f"ml_h{k}1")]
           for k in ("qn", "wk", "wv", "qp")}
    for j in range(16):
        p.op("sync", lambda e, j=j: e.dma_start_transpose(
                KVT[j], kvc_hbm[:, 128 * j:128 * (j + 1)]),
             waits=[kv_st, q_all] if j == 0 else None,
             incs=(kt := kvtc.inc(16)), tag=f"ml_kvt{j}")
    kvt_ready = kt.done

    hload_free = {"qn": [None, None], "qp": [None, None],
                  "wk": [None, None], "wv": [None, None]}
    sc = p.chan("ml_sc")
    atn_evac = None
    olb_free = None
    ovh_free = None
    ctx_free = None
    prb_free = None
    psb_scores_free = None
    for h in range(nhead):
        b = h % 2
        p.op("sync", lambda e, h=h, b=b: e.dma_start(
                out=QNH[b], in_=qn_hbm[128 * h:128 * (h + 1), :]),
             waits=([q_all, kpe_done] if h < 2 else []) +
                   _as_waits(hload_free["qn"][b]),
             incs=(lqn := hlc["qn"][b].inc(16)), tag=f"ml_hqn{h}")
        p.op("sync", lambda e, h=h, b=b: e.dma_start(
                out=WKB[b], in_=wkvbn_bf[128 * h:128 * (h + 1), :]),
             waits=([cwaits[2]] if h < 2 else []) + _as_waits(hload_free["wk"][b]),
             incs=(lwk := hlc["wk"][b].inc(16)), tag=f"ml_hwk{h}")
        p.op("sync", lambda e, h=h, b=b: e.dma_start(
                out=WVB[b].rearrange("p (ct m) -> p ct m", m=128),
                in_=wkvbv_bf[:, 128 * h:128 * (h + 1)].rearrange(
                    "(ct p) m -> p ct m", p=128)),
             waits=([cwaits[3]] if h < 2 else []) + _as_waits(hload_free["wv"][b]),
             incs=(lwv := hlc["wv"][b].inc(16)), tag=f"ml_hwv{h}")
        if h % 2 == 0:
            pr = h // 2
            p.op("sync", lambda e, pr=pr: e.dma_start(
                    out=QPH[pr % 2], in_=qp_hbm[128 * pr:128 * (pr + 1), :]),
                 waits=([q_all, kpe_done] if pr < 2 else []) +
                       _as_waits(hload_free["qp"][pr % 2]),
                 incs=(lqp := hlc["qp"][pr % 2].inc(16)), tag=f"ml_hqp{pr}")
            hload_qp = lqp.done
        qph_buf = QPH[(h // 2) % 2]
        qpe_lo = 64 * (h % 2)

        qle = None
        for ct in range(4):
            for c in range(4):
                if ct == 0 and c == 0:
                    w = ([lqn.done, lwk.done] + _as_waits(ctx_free) +
                         (_as_waits(pss_free) if h == 0 else []))
                elif c == 0:
                    w = [qle.done]
                else:
                    w = None
                p.op("tensor", lambda e, ct=ct, c=c, b=b: e.matmul(
                        kb.PSS[c][:], WKB[b][:, 128 * ct:128 * (ct + 1)],
                        QNH[b][:, 512 * c:512 * (c + 1)],
                        start=True, stop=True),
                     waits=w,
                     incs=(qlm := mm.inc()) if c == 3 else None,
                     tag=f"ml_ql{h}_{ct}")
            for c in range(4):
                p.op("scalar", lambda e, ct=ct, c=c: e.activation(
                        QLT[ct][:, 512 * c:512 * (c + 1)], kb.PSS[c][:],
                        AF.Identity),
                     waits=[qlm.done] if c == 0 else None,
                     incs=(qle := ev.inc()), tag=f"ml_qle{h}_{ct}")
        qlat_done = qle.done
        if h == 0 and "ml_dbg_qlat" in kb.dbg:
            dbg_ql = kb.dram("ml_dbg_qlat", (512, S), BF16)
            for ct in range(4):
                p.op("sync", lambda e, ct=ct: e.dma_start(
                        out=dbg_ql[128 * ct:128 * (ct + 1), :], in_=QLT[ct]),
                     waits=[qlat_done], incs=qst.inc(16), tag="dbg_ql")

        for qc in range(4):
            nkc = qc + 1
            nkt = 4 * (qc + 1)
            ex_prev = None
            for qil in range(4):
                qa = 4 * qc + qil
                for kc in range(nkc):
                    for ct in range(4):
                        w = None
                        if ct == 0 and kc == 0 and qil == 0:
                            w = [qlat_done, kvt_ready] + \
                                _as_waits(psb_scores_free)
                            if h == 0 and qc == 0:
                                w += [kpe_done]
                        elif ct == 0 and kc == 0:
                            w = _as_waits(ex_prev)
                        p.op("tensor", lambda e, qa=qa, kc=kc, ct=ct: e.matmul(
                                kb.PSB[:, 512 * kc:512 * (kc + 1)],
                                QLT[ct][:, 128 * qa:128 * (qa + 1)],
                                KVL[ct][:, 512 * kc:512 * (kc + 1)],
                                start=(ct == 0), stop=False),
                             waits=w, tag=f"ml_s{h}_{qa}")
                    p.op("tensor", lambda e, qa=qa, kc=kc, qph_buf=qph_buf,
                         qpe_lo=qpe_lo, nkc=nkc: e.matmul(
                                kb.PSB[:, 512 * kc:512 * (kc + 1)],
                                qph_buf[qpe_lo:qpe_lo + 64,
                                        128 * qa:128 * (qa + 1)],
                                KPE[qpe_lo:qpe_lo + 64,
                                    512 * kc:512 * (kc + 1)],
                                start=False, stop=(kc != nkc - 1)),
                         waits=[hload_qp] if (qc == 0 and qil == 0 and kc == 0)
                               else None,
                         tag=f"ml_sp{h}_{qa}")
                p.op("tensor", lambda e, qa=qa, qil=qil, qc=qc: e.matmul(
                        kb.PSB[:, 512 * qc:512 * (qc + 1)], kb.ident[:],
                        kb.MSK4[:, 512 * qil:512 * (qil + 1)],
                        start=False, stop=True),
                     incs=(smm := mm.inc()), tag=f"ml_sm{h}_{qa}")
                nm = kb.SM[:, 160 + qil:161 + qil]
                nms = kb.SM[:, 164 + qil:165 + qil]
                sums = kb.SM[:, 168 + qil:169 + qil]
                rec = kb.SM[:, 172 + qil:173 + qil]
                p.op("vector", lambda e, nm=nm, nkc=nkc: e.reduce_max(
                        nm, kb.PSB[:, 0:512 * nkc], AX.X, negate=True),
                     waits=[smm.done], incs=(mx := dv.inc()),
                     tag=f"ml_mx{h}_{qa}")
                p.op("vector", lambda e, nm=nm, nms=nms: e.tensor_scalar_mul(
                        nms, nm, MLA_SCALE),
                     waits=[mx.done], incs=(mxs := dv.inc()))
                p.op("scalar", lambda e, qil=qil, nms=nms, sums=sums, nkc=nkc:
                     e.activation(
                        PRB[qil][:, 0:512 * nkc], kb.PSB[:, 0:512 * nkc],
                        AF.Exp, bias=nms, scale=MLA_SCALE, accum_out=sums),
                     waits=[mxs.done] + (_as_waits(prb_free) if qil == 0 else []),
                     incs=(ex := ev.inc()), tag=f"ml_ex{h}_{qa}")
                ex_prev = [ex.done]
                p.op("vector", lambda e, sums=sums, rec=rec: e.reciprocal(
                        rec, sums),
                     waits=[ex.done], incs=(rc2 := dv.inc()))
                p.op("vector", lambda e, qil=qil, rec=rec: e.tensor_scalar_mul(
                        kb.SMB[:, 128 * qil:128 * (qil + 1)], kb.ident[:], rec),
                     waits=[rc2.done], incs=(dg := dv.inc()),
                     tag=f"ml_dg{h}_{qa}")
                if qil == 3:
                    psb_scores_free = [ex.done]
            diag_done = dg.done
            tr_ev = None
            for kt in range(nkt):
                ps = kb.PSS[kt % 2][:]
                for qil in range(4):
                    p.op("tensor", lambda e, kt=kt, qil=qil, ps=ps: e.matmul(
                            ps[:, 128 * qil:128 * (qil + 1)],
                            PRB[qil][:, 128 * kt:128 * (kt + 1)],
                            kb.SMB[:, 128 * qil:128 * (qil + 1)],
                            start=True, stop=True),
                         waits=([diag_done] + _as_waits(atn_evac)
                                + (_as_waits(pss_free) if (h == 0 and qc == 0
                                                           and kt < 2) else []))
                         if qil == 0 else None,
                         incs=(tmm := mm.inc()) if qil == 3 else None,
                         tag=f"ml_tr{h}_{kt}")
                p.op("scalar", lambda e, kt=kt, ps=ps: e.activation(
                        ATN[kt], ps, AF.Identity),
                     waits=[tmm.done], incs=(tr_ev := ev.inc()),
                     tag=f"ml_tre{h}_{kt}")
                if kt >= 1:
                    atn_evac = [tr_ev.done]
            atn_done = tr_ev.done
            prb_free = [tmm.done]
            if h == 0 and "ml_dbg_atn" in kb.dbg:
                dbg_at = kb.dram("ml_dbg_atn", (4, 2048, 512), BF16)
                for kt in range(nkt):
                    p.op("sync", lambda e, kt=kt, qc=qc: e.dma_start(
                            out=dbg_at[qc, 128 * kt:128 * (kt + 1), :],
                            in_=ATN[kt]),
                         waits=[atn_done], incs=qst.inc(16), tag="dbg_at")
            for ct in range(4):
                for kt in range(nkt):
                    p.op("tensor", lambda e, ct=ct, kt=kt, nkt=nkt: e.matmul(
                            kb.PSS[ct][:], KVT[kt][:, 128 * ct:128 * (ct + 1)],
                            ATN[kt], start=(kt == 0), stop=(kt == nkt - 1)),
                         waits=[atn_done] + _as_waits(olb_free)
                               if (ct == 0 and kt == 0) else None,
                         incs=(cm2 := mm.inc()) if kt == nkt - 1 else None,
                         tag=f"ml_ol{h}_{ct}")
                p.op("scalar", lambda e, ct=ct: e.activation(
                        OLB[:, 512 * ct:512 * (ct + 1)], kb.PSS[ct][:],
                        AF.Identity),
                     waits=[cm2.done], incs=(ole := ev.inc()),
                     tag=f"ml_ole{h}_{ct}")
            ctx_free = [ole.done]
            for ct in range(4):
                p.op("tensor", lambda e, ct=ct, b=b: e.matmul(
                        kb.PSB[:, 0:512], WVB[b][:, 128 * ct:128 * (ct + 1)],
                        OLB[:, 512 * ct:512 * (ct + 1)],
                        start=(ct == 0), stop=(ct == 3)),
                     waits=([lwv.done, ole.done] + _as_waits(psb_scores_free))
                           if ct == 0 else None,
                     incs=(ovm := mm.inc()) if ct == 3 else None,
                     tag=f"ml_ov{h}_{qc}")
            p.op("scalar", lambda e, qc=qc: e.activation(
                    OVH[:, 512 * qc:512 * (qc + 1)], kb.PSB[:, 0:512],
                    AF.Identity),
                 waits=[ovm.done] + (_as_waits(ovh_free) if qc == 0 else []),
                 incs=(ove := ev.inc()), tag=f"ml_ove{h}_{qc}")
            psb_scores_free = [ove.done]
            olb_free = [ovm.done]
        p.op("sync", lambda e, h=h: e.dma_start(
                out=ov_hbm[128 * h:128 * (h + 1), :], in_=OVH),
             waits=[ove.done], incs=(ovs := qst.inc(16)), tag=f"ml_ovst{h}")
        ovh_free = [ovs.done]
        hload_free["qn"][b] = [qlm.done]
        hload_free["wk"][b] = [qlm.done]
        hload_free["wv"][b] = [ovm.done]
        if h % 2 == 1:
            hload_free["qp"][(h // 2) % 2] = [smm.done]
    ov_all = qst.at()
    if "wo" not in stages:
        return dict(done=ov_all)

    WOT = [kb.carve(4096 * k, 2048, BF16) for k in range(16)]
    OVS = [kb.carve(65536 + 4096 * i, 2048, BF16) for i in range(2)]
    wo_c = p.chan("ml_wo")
    wog = [p.chan("ml_wog0"), p.chan("ml_wog1")]
    mxl = [p.chan("ml_mxl0"), p.chan("ml_mxl1")]
    for k in range(16):
        p.op("sync", lambda e, k=k: e.dma_start(
                out=WOT[k], in_=wo_bf[128 * k:128 * (k + 1), :]),
             waits=[ov_all, cwaits[4]] if k == 0 else None,
             incs=(wol := wo_c.inc(16)), tag=f"ml_wol{k}")
    wo_ready = wol.done
    ovs_free = [None, None]
    bank_free2 = [None] * 8
    add_prev = None
    xt_free = [None, None]
    for j in range(NT):
        ob = j % 2
        p.op("gpsimd", lambda e, j=j, ob=ob: e.dma_start(
                out=OVS[ob].rearrange("p (k t) -> p k t", t=128),
                in_=ov_hbm[:, 128 * j:128 * (j + 1)].rearrange(
                    "(k p) t -> p k t", p=128)),
             waits=([ov_all] if j < 2 else []) + _as_waits(ovs_free[ob]),
             incs=(ols := wog[ob].inc(16)), tag=f"ml_ovsl{j}")
        p.op("sync", lambda e, j=j, ob=ob: e.dma_start(
                out=kb.XT[ob][:], in_=x_in[128 * j:128 * (j + 1), :]),
             waits=_as_waits(xt_free[ob]), incs=(xl := mxl[ob].inc(16)),
             tag=f"ml_xl{j}")
        bks = [j % 2 * 4 + c for c in range(4)]
        for c in range(4):
            bkid = bks[c]
            bank = (kb.PSS[bkid][:] if bkid < 4 else
                    kb.PSB[:, 512 * (bkid - 4):512 * (bkid - 3)])
            for k in range(16):
                p.op("tensor", lambda e, k=k, c=c, ob=ob, bank=bank: e.matmul(
                        bank, OVS[ob][:, 128 * k:128 * (k + 1)],
                        WOT[k][:, 512 * c:512 * (c + 1)],
                        start=(k == 0), stop=(k == 15)),
                     waits=([ols.done, wo_ready] + _as_waits(bank_free2[bkid]))
                           if k == 0 else None,
                     incs=(wm := mm.inc()) if k == 15 else None,
                     tag=f"ml_wo{j}_{c}")
            p.op("vector", lambda e, c=c, ob=ob, bank=bank: e.tensor_add(
                    kb.TMP[0][:, 512 * c:512 * (c + 1)],
                    kb.XT[ob][:, 512 * c:512 * (c + 1)], bank),
                 waits=[wm.done, xl.done] + (_as_waits(add_prev) if c == 0 else []),
                 incs=(ad := dv.inc()), tag=f"ml_add{j}_{c}")
            bank_free2[bkid] = [ad.done]
        ovs_free[ob] = [wm.done]
        p.op("sync", lambda e, j=j: e.dma_start(
                out=x_out[128 * j:128 * (j + 1), :], in_=kb.TMP[0][:]),
             waits=[ad.done], incs=(xo := xoc.inc(16)), tag=f"ml_xo{j}")
        add_prev = [xo.done]
        xt_free[ob] = [ad.done]
    return dict(done=xoc.at())


MIX_SCALE = float(128 ** -0.5)


def emit_mix1(kb, x_in, x_out, ci, start=None,
              stages=("pre", "qkv", "heads", "wo", "ff")):
    p, nc = kb.p, kb.nc
    kb.act_set(SET_NL_EXP)

    if kb.once("mx_setup"):
        sld = p.chan("mx_sld")
        for w in ["wrT", "wdT1", "wupT1", "wqT2", "wkT2", "wvT2", "woT2",
                  "wff1T", "wff2T", "sbias_row"]:
            kb.cast_weight(w)
        kb.LN1S = nc.alloc_sbuf_tensor("LN1S", [128, 16], F32)
        kb.LN1B = nc.alloc_sbuf_tensor("LN1B", [128, 16], F32)
        kb.LN2S = nc.alloc_sbuf_tensor("LN2S", [128, 16], F32)
        kb.LN2B = nc.alloc_sbuf_tensor("LN2B", [128, 16], F32)
        kb.LGT = nc.alloc_sbuf_tensor("LGT", [128, 64], F32)
        p.op("vector", lambda e: e.memset(kb.LGT[:], 0.0), tag="mx_lgt0")
        kb.T1B = nc.alloc_sbuf_tensor("T1B", [32, 2048], BF16)
        kb.SBR = nc.alloc_sbuf_tensor("SBR", [1, 2048], BF16)
        kb.BIAS_BC = nc.alloc_sbuf_tensor("BIAS_BC", [128, 2048], BF16)
        kb.WDT1 = nc.alloc_sbuf_tensor("WDT1", [128, 512], BF16)
        kb.WUPT1 = nc.alloc_sbuf_tensor("WUPT1", [32, 2048], BF16)
        kb.WRT = nc.alloc_sbuf_tensor("WRT", [128, 48], BF16)
        for nm, t in [("ln1s_pd", kb.LN1S), ("ln1b_pd", kb.LN1B),
                      ("ln2s_pd", kb.LN2S), ("ln2b_pd", kb.LN2B)]:
            p.op("sync", lambda e, nm=nm, t=t: e.dma_start(
                    out=t[:], in_=kb.din[nm]), incs=sld.inc(16))
        sb_bf, wsb = kb.cast_weight("sbias_row")
        p.op("sync", lambda e: e.dma_start(out=kb.SBR[:], in_=sb_bf),
             waits=[wsb], incs=sld.inc(16))
        wd1_bf, wd1w = kb.cast_weight("wdT1")
        p.op("sync", lambda e: e.dma_start(
                out=kb.WDT1[:].rearrange("p (k r) -> p k r", r=32),
                in_=wd1_bf.rearrange("(k p) r -> p k r", p=128)),
             waits=[wd1w], incs=sld.inc(16))
        wu1_bf, wu1w = kb.cast_weight("wupT1")
        p.op("sync", lambda e: e.dma_start(out=kb.WUPT1[:], in_=wu1_bf),
             waits=[wu1w], incs=sld.inc(16))
        wr_bf, wrw = kb.cast_weight("wrT")
        p.op("sync", lambda e: e.dma_start(
                out=kb.WRT[:].rearrange("p (k r) -> p k r", r=3),
                in_=wr_bf.rearrange("(k p) r -> p k r", p=128)),
             waits=[wrw], incs=sld.inc(16))
        small_ready = sld.at()
        bmm = p.chan("mx_bb")
        for c in range(4):
            p.op("tensor", lambda e, c=c: e.matmul(
                    kb.PSS[c][:], kb.onesr[:],
                    kb.SBR[0:1, 512 * c:512 * (c + 1)], start=True, stop=True),
                 waits=[small_ready] if c == 0 else None,
                 incs=(bm := bmm.inc()) if c == 3 else None, tag="mx_bbc")
        bevc = p.chan("mx_bevc")
        for c in range(4):
            p.op("scalar", lambda e, c=c: e.activation(
                    kb.BIAS_BC[:, 512 * c:512 * (c + 1)], kb.PSS[c][:],
                    AF.Identity),
                 waits=[bm.done] if c == 0 else None,
                 incs=(bfin := bevc.inc()), tag="mx_bbe")
        kb._flags["mx_small"] = [small_ready, bfin.done, kb.ensure_msk4()]
        kb._flags["mx_bias_pss_free"] = [bfin.done]
    small = kb._flags["mx_small"]

    xn_tm = kb.dram(f"mx{ci}_xn", (S, D), BF16)
    xn1_tm = kb.dram(f"mx{ci}_xn1", (S, D), BF16)
    xn2_tm = kb.dram(f"mx{ci}_xn2", (S, D), BF16)
    xi_hbm = kb.dram(f"mx{ci}_xi", (S, D), F32)
    q_hbm = kb.dram(f"mx{ci}_q", (D, S), BF16)
    k_hbm = kb.dram(f"mx{ci}_k", (D, S), BF16)
    v_hbm = kb.dram(f"mx{ci}_v", (D, S), BF16)
    ctx_hbm = kb.dram(f"mx{ci}_ctx", (D, S), BF16)
    g2_hbm = kb.dram(f"mx{ci}_g", (HID, D), BF16)

    SM = kb.SM
    lnr = emit_ln(kb, x_in, xn_tm, f"x{ci}ln", start=start,
                  inner_stats=lambda i: (SM[:, 176 + i:177 + i],
                                         SM[:, 192 + i:193 + i]))
    mm = p.chan(f"mx{ci}mm")
    ev = p.chan(f"mx{ci}ev")
    dv = p.chan(f"mx{ci}dv")
    st1 = p.chan(f"mx{ci}st")

    ln1_toks = []
    for i in range(NT):
        s_c = SM[:, 176 + i:177 + i]
        q_c = SM[:, 192 + i:193 + i]
        m_c = SM[:, 368 + i:369 + i]
        v_c = SM[:, 384 + i:385 + i]
        r_c = SM[:, 208 + i:209 + i]
        n_c = SM[:, 224 + i:225 + i]
        p.op("vector", lambda e, s=s_c, o=m_c: e.tensor_scalar_mul(o, s, 1.0 / D),
             waits=[lnr["inner_done"][i][0]], incs=(a1 := dv.inc()))
        p.op("scalar", lambda e, m=m_c, o=v_c: e.activation(o, m, AF.Square),
             waits=[a1.done, lnr["inner_done"][i][1]], incs=(a2 := ev.inc()))
        p.op("vector", lambda e, q=q_c, m2=v_c, o=v_c: e.scalar_tensor_tensor(
                o, q, 1.0 / D, m2, ALU.mult, ALU.subtract),
             waits=[a2.done], incs=(a3 := dv.inc()))
        p.op("vector", lambda e, o=v_c: e.tensor_scalar_add(o, o, EPS_LN),
             waits=[a3.done], incs=(a4 := dv.inc()))
        p.op("scalar", lambda e, v=v_c, o=r_c: e.activation(o, v, AF.Ln),
             waits=[a4.done], incs=(a5 := ev.inc()))
        p.op("scalar", lambda e, r=r_c: e.activation(r, r, AF.Exp, scale=-0.5),
             incs=(a6 := ev.inc()))
        p.op("vector", lambda e, m=m_c, r=r_c, o=n_c: e.scalar_tensor_tensor(
                o, m, -1.0, r, ALU.mult, ALU.mult),
             waits=[a6.done], incs=(a7 := dv.inc()))
        ln1_toks.append(a7.done)

    msk_toks = []
    lg_prev = None
    for j in range(NT):
        bk = j % 2
        for k in range(ND):
            p.op("tensor", lambda e, j=j, k=k, bk=bk: e.matmul(
                    kb.PSB[:, 512 * bk:512 * bk + 3],
                    kb.xfm(k)[:, 128 * j:128 * (j + 1)],
                    kb.WRT[:, 3 * k:3 * (k + 1)],
                    start=(k == 0), stop=(k == ND - 1)),
                 waits=(lnr["fm_ready"][k] + [] if False else
                        ([lnr["fm_ready"][k]] + small +
                         _as_waits(kb._flags.get("mx_bias_pss_free")) +
                         _as_waits(lg_prev)))
                       if k == 0 else None,
                 incs=(lgm := mm.inc()) if k == ND - 1 else None,
                 tag=f"mx_rt{j}")
        p.op("scalar", lambda e, j=j, bk=bk: e.activation(
                kb.LGT[:, 4 * j:4 * j + 3], kb.PSB[:, 512 * bk:512 * bk + 3],
                AF.Identity),
             waits=[lgm.done], incs=(lge := ev.inc()), tag=f"mx_rte{j}")
        lg_prev = [lge.done] if j % 2 == 1 else lg_prev
        l0 = kb.LGT[:, 4 * j:4 * j + 1]
        l1 = kb.LGT[:, 4 * j + 1:4 * j + 2]
        l2 = kb.LGT[:, 4 * j + 2:4 * j + 3]
        g01 = SM[:, 400 + j:401 + j]
        g02 = SM[:, 416 + j:417 + j]
        g12 = SM[:, 432 + j:433 + j]
        is0 = SM[:, 304 + j:305 + j]
        is1 = SM[:, 320 + j:321 + j]
        is2 = SM[:, 336 + j:337 + j]
        is01 = SM[:, 352 + j:353 + j]
        p.op("vector", lambda e, a=l0, b2=l1, o=g01: e.tensor_tensor(
                o, a, b2, ALU.is_ge), waits=[lge.done], incs=(w1 := dv.inc()))
        p.op("vector", lambda e, a=l0, b2=l2, o=g02: e.tensor_tensor(
                o, a, b2, ALU.is_ge), incs=(w2 := dv.inc()))
        p.op("vector", lambda e, a=l1, b2=l2, o=g12: e.tensor_tensor(
                o, a, b2, ALU.is_ge), incs=(w3 := dv.inc()))
        p.op("vector", lambda e, a=g01, b2=g02, o=is0: e.tensor_mul(o, a, b2),
             waits=[w3.done], incs=(w4 := dv.inc()))
        p.op("vector", lambda e, a=is0, o=is1: e.tensor_scalar(
                o, a, -1.0, 1.0, ALU.mult, ALU.add),
             waits=[w4.done], incs=(w5 := dv.inc()))
        p.op("vector", lambda e, a=is1, b2=g12, o=is1: e.tensor_mul(o, a, b2),
             waits=[w5.done], incs=(w6 := dv.inc()))
        p.op("vector", lambda e, a=is0, b2=is1, o=is01: e.tensor_add(o, a, b2),
             waits=[w6.done], incs=(w7 := dv.inc()))
        p.op("vector", lambda e, a=is01, o=is2: e.tensor_scalar(
                o, a, -1.0, 1.0, ALU.mult, ALU.add),
             waits=[w7.done], incs=(mk := dv.inc()),
             tag=f"mx_msk{j}")
        msk_toks.append(mk.done)
    masks_done = mk.done
    if f"mx{ci}_msk2" in kb.dbg:
        dbgM2 = kb.dram(f"mx{ci}_msk2", (128, 576), F32)
        p.op("sync", lambda e: e.dma_start(out=dbgM2[:, 0:512], in_=SM[:]),
             waits=[masks_done], incs=st1.inc(16), tag="dbgMsk2")
        p.op("sync", lambda e: e.dma_start(out=dbgM2[:, 512:576], in_=kb.LGT[:]),
             incs=st1.inc(16), tag="dbgLgt2")

    for c in range(4):
        for k in range(ND):
            p.op("tensor", lambda e, c=c, k=k: e.matmul(
                    kb.PSB[0:32, 1024 + 512 * (c % 2):1024 + 512 * (c % 2) + 512]
                    if False else kb.PSS[c][0:32, :],
                    kb.WDT1[:, 32 * k:32 * (k + 1)],
                    kb.xfm(k)[:, 512 * c:512 * (c + 1)],
                    start=(k == 0), stop=(k == ND - 1)),
                 waits=([lnr["fm_ready"][k]] + small +
                        _as_waits(kb._flags.get("mx_bias_pss_free")))
                       if k == 0 else None,
                 incs=(t1m := mm.inc()) if k == ND - 1 else None,
                 tag=f"mx_t1{c}")
    for c in range(4):
        p.op("scalar", lambda e, c=c: e.activation(
                kb.T1B[:, 512 * c:512 * (c + 1)], kb.PSS[c][0:32, :],
                AF.Identity),
             waits=[t1m.done] if c == 0 else None, incs=(t1e := ev.inc()),
             tag=f"mx_t1e{c}")
    t1_done = t1e.done

    x1st = [p.chan(f"mx{ci}x1a"), p.chan(f"mx{ci}x1b")]
    xld = [p.chan(f"mx{ci}xla"), p.chan(f"mx{ci}xlb")]
    wb_free = [None, None]
    for i in range(NT):
        b = i % 2
        wbuf = kb.WB[b]
        p.op("sync", lambda e, i=i, b=b: e.dma_start(
                out=kb.XT[b][:, 0:1024].bitcast(BF16),
                in_=xn_tm[128 * i:128 * (i + 1), :]),
             waits=([lnr["tm_done"][0], lnr["tm_done"][1]] if i < 2 else []) +
                   _as_waits(wb_free[b]),
             incs=(xr := xld[b].inc(16)), tag=f"mx_x1l{i}")
        r_c = SM[:, 208 + i:209 + i]
        n_c = SM[:, 224 + i:225 + i]
        p.op("scalar", lambda e, b=b, r=r_c, n=n_c, wbuf=wbuf: e.activation(
                wbuf[:], kb.XT[b][:, 0:1024].bitcast(BF16), AF.Identity,
                bias=n, scale=r),
             waits=[xr.done, ln1_toks[i]], incs=(xno := ev.inc()),
             tag=f"mx_x1n{i}")
        p.op("sync", lambda e, i=i, wbuf=wbuf: e.dma_start(
                out=xn1_tm[128 * i:128 * (i + 1), :], in_=wbuf[:]),
             waits=[xno.done], incs=(x1s := x1st[b].inc(16)),
             tag=f"mx_x1s{i}")
        wb_free[b] = [x1s.done]
    xn1_all = [x1st[0].at(), x1st[1].at()]

    trc = p.chan(f"mx{ci}tr")
    s1ap = p.chan(f"mx{ci}s1")
    for dt in range(ND):
        p.op("sync", lambda e, dt=dt: e.dma_start_transpose(
                kb.xfm(dt), xn1_tm[:, 128 * dt:128 * (dt + 1)]),
             waits=(xn1_all + [t1m.done, lgm.done]) if dt == 0 else None,
             incs=(tq := trc.inc(16)), tag=f"mx_x1t{dt}")
    tr_all = tq.done
    for dt in range(ND):
        p.op("vector", lambda e, dt=dt: e.tensor_scalar(
                kb.xfm(dt), kb.xfm(dt), kb.LN1S[:, dt:dt + 1],
                kb.LN1B[:, dt:dt + 1], ALU.mult, ALU.add),
             waits=[tr_all, masks_done] if dt == 0 else None,
             incs=(sb1 := dv.inc()), tag=f"mx_s1b{dt}")
    xn1_ready = sb1.done

    if "qkv" not in stages:
        return dict(done=_Wait(dv, dv.count))
    WQB = [kb.carve(65536 + 4096 * i, 2048, BF16) for i in range(2)]
    QNB = [kb.carve(73728 + 4096 * i, 2048, BF16) for i in range(2)]
    wlc = [p.chan(f"mx{ci}wl0"), p.chan(f"mx{ci}wl1")]
    pst = [p.chan(f"mx{ci}ps0"), p.chan(f"mx{ci}ps1")]
    wqb_free = [None, None]
    qnb_free = [None, None]
    pss_free = kb._flags.get("mx_bias_pss_free")
    mi = 0
    for wname, out_h in [("wqT2", q_hbm), ("wkT2", k_hbm), ("wvT2", v_hbm)]:
        w_bf = kb.scr[wname + "_bf"]
        cw = kb._flags[wname + "_bf"]
        for m2 in range(16):
            b = mi % 2
            p.op("sync", lambda e, m2=m2, b=b, w_bf=w_bf: e.dma_start(
                    out=WQB[b].rearrange("p (k m) -> p k m", m=128),
                    in_=w_bf[:, 128 * m2:128 * (m2 + 1)].rearrange(
                        "(k p) m -> p k m", p=128)),
                 waits=([cw] if mi < 2 else []) + _as_waits(wqb_free[b]),
                 incs=(qw := wlc[b].inc(16)), tag=f"mx_w{wname}{m2}")
            for k in range(ND):
                for c in range(4):
                    if k == 0 and c == 0:
                        w = [qw.done, xn1_ready] + _as_waits(pss_free)
                    else:
                        w = None
                    p.op("tensor", lambda e, k=k, c=c, b=b: e.matmul(
                            kb.PSS[c][:], WQB[b][:, 128 * k:128 * (k + 1)],
                            kb.xfm(k)[:, 512 * c:512 * (c + 1)],
                            start=(k == 0), stop=(k == ND - 1)),
                         waits=w,
                         incs=(qm := mm.inc()) if (k == ND - 1 and c == 3) else None,
                         tag=f"mx_p{wname}{m2}")
            wqb_free[b] = [qm.done]
            for c in range(4):
                p.op("scalar", lambda e, c=c, b=b: e.activation(
                        QNB[b][:, 512 * c:512 * (c + 1)], kb.PSS[c][:],
                        AF.Identity),
                     waits=([qm.done] + _as_waits(qnb_free[b])) if c == 0 else None,
                     incs=(qe := ev.inc()), tag=f"mx_e{wname}{m2}")
            pss_free = [qe.done]
            p.op("sync", lambda e, m2=m2, b=b, out_h=out_h: e.dma_start(
                    out=out_h[128 * m2:128 * (m2 + 1), :], in_=QNB[b]),
                 waits=[qe.done], incs=(qs := pst[b].inc(16)),
                 tag=f"mx_s{wname}{m2}")
            qnb_free[b] = [qs.done]
            mi += 1
    qkv_all = [pst[0].at(), pst[1].at()]

    if "heads" not in stages:
        return dict(done=qkv_all[0])
    KH = [kb.carve(4096 * i, 2048, BF16) for i in range(2)]
    QH = [kb.carve(8192 + 4096 * i, 2048, BF16) for i in range(2)]
    VTM = [kb.carve(16384 + 4096 * i, 2048, BF16) for i in range(2)]
    PRB = [kb.carve(24576 + 4096 * i, 2048, BF16) for i in range(4)]
    ATN = [kb.carve(40960 + 1024 * j, 512, BF16) for j in range(16)]
    CXH = kb.carve(57344, 2048, BF16)
    hlc = {k2: [p.chan(f"mx{ci}h{k2}0"), p.chan(f"mx{ci}h{k2}1")]
           for k2 in ("k", "q", "v")}
    cst = p.chan(f"mx{ci}cst")
    hfree = {k2: [None, None] for k2 in ("k", "q", "v")}
    atn_evac = None
    ctx_free = None
    cxh_free = None
    prb_free = None
    psb_scores_free = None
    for h in range(16):
        b = h % 2
        p.op("sync", lambda e, h=h, b=b: e.dma_start(
                out=KH[b], in_=k_hbm[128 * h:128 * (h + 1), :]),
             waits=(qkv_all if h < 2 else []) + _as_waits(hfree["k"][b]),
             incs=(lk := hlc["k"][b].inc(16)), tag=f"mx_hk{h}")
        p.op("sync", lambda e, h=h, b=b: e.dma_start(
                out=QH[b], in_=q_hbm[128 * h:128 * (h + 1), :]),
             waits=(qkv_all if h < 2 else []) + _as_waits(hfree["q"][b]),
             incs=(lq := hlc["q"][b].inc(16)), tag=f"mx_hq{h}")
        for kt in range(16):
            p.op("sync", lambda e, h=h, b=b, kt=kt: e.dma_start_transpose(
                    VTM[b][:, 128 * kt:128 * (kt + 1)],
                    v_hbm[128 * h:128 * (h + 1), 128 * kt:128 * (kt + 1)]),
                 waits=((qkv_all if h < 2 else []) + _as_waits(hfree["v"][b]))
                       if kt == 0 else None,
                 incs=(lv := hlc["v"][b].inc(16)), tag=f"mx_hv{h}")
        for qc in range(4):
            nkc = qc + 1
            nkt = 4 * (qc + 1)
            ex_prev = None
            for qil in range(4):
                qa = 4 * qc + qil
                for kc in range(nkc):
                    p.op("tensor", lambda e, qa=qa, kc=kc, b=b, qc=qc: e.matmul(
                            kb.PSB[:, 512 * kc:512 * (kc + 1)],
                            QH[b][:, 128 * qa:128 * (qa + 1)],
                            KH[b][:, 512 * kc:512 * (kc + 1)],
                            start=True, stop=(kc != qc)),
                         waits=([lq.done, lk.done] + _as_waits(psb_scores_free))
                               if (kc == 0 and qil == 0) else
                               (_as_waits(ex_prev) if kc == 0 else None),
                         tag=f"mx_s{h}_{qa}")
                p.op("tensor", lambda e, qa=qa, qil=qil, qc=qc: e.matmul(
                        kb.PSB[:, 512 * qc:512 * (qc + 1)], kb.ident[:],
                        kb.MSK4[:, 512 * qil:512 * (qil + 1)],
                        start=False, stop=True),
                     incs=(smm := mm.inc()), tag=f"mx_m{h}_{qa}")
                nm = SM[:, 448 + qil:449 + qil]
                nms = SM[:, 452 + qil:453 + qil]
                sums = SM[:, 456 + qil:457 + qil]
                rec = SM[:, 460 + qil:461 + qil]
                p.op("vector", lambda e, nm=nm, nkc=nkc: e.reduce_max(
                        nm, kb.PSB[:, 0:512 * nkc], AX.X, negate=True),
                     waits=[smm.done], incs=(mx := dv.inc()),
                     tag=f"mx_mx{h}_{qa}")
                p.op("vector", lambda e, nm=nm, nms=nms: e.tensor_scalar_mul(
                        nms, nm, MIX_SCALE), waits=[mx.done],
                     incs=(mxs := dv.inc()))
                p.op("scalar", lambda e, qil=qil, nms=nms, sums=sums, nkc=nkc:
                     e.activation(
                        PRB[qil][:, 0:512 * nkc], kb.PSB[:, 0:512 * nkc],
                        AF.Exp, bias=nms, scale=MIX_SCALE, accum_out=sums),
                     waits=[mxs.done] + (_as_waits(prb_free) if qil == 0 else []),
                     incs=(ex := ev.inc()), tag=f"mx_ex{h}_{qa}")
                ex_prev = [ex.done]
                p.op("vector", lambda e, sums=sums, rec=rec: e.reciprocal(
                        rec, sums), waits=[ex.done], incs=(rc2 := dv.inc()))
                p.op("vector", lambda e, qil=qil, rec=rec: e.tensor_scalar_mul(
                        kb.SMB[:, 128 * qil:128 * (qil + 1)], kb.ident[:], rec),
                     waits=[rc2.done], incs=(dg := dv.inc()),
                     tag=f"mx_dg{h}_{qa}")
                if qil == 3:
                    psb_scores_free = [ex.done]
            diag_done = dg.done
            tr_ev = None
            for kt in range(nkt):
                ps = kb.PSS[kt % 2][:]
                for qil in range(4):
                    p.op("tensor", lambda e, kt=kt, qil=qil, ps=ps: e.matmul(
                            ps[:, 128 * qil:128 * (qil + 1)],
                            PRB[qil][:, 128 * kt:128 * (kt + 1)],
                            kb.SMB[:, 128 * qil:128 * (qil + 1)],
                            start=True, stop=True),
                         waits=([diag_done] + _as_waits(atn_evac) +
                                (_as_waits(ctx_free) if kt < 2 else []))
                         if qil == 0 else None,
                         incs=(tmm := mm.inc()) if qil == 3 else None,
                         tag=f"mx_tr{h}_{kt}")
                p.op("scalar", lambda e, kt=kt, ps=ps: e.activation(
                        ATN[kt], ps, AF.Identity),
                     waits=[tmm.done], incs=(tr_ev := ev.inc()),
                     tag=f"mx_tre{h}_{kt}")
                atn_evac = [tr_ev.done]
            atn_done = tr_ev.done
            prb_free = [tmm.done]
            for kt in range(nkt):
                p.op("tensor", lambda e, kt=kt, b=b, nkt=nkt: e.matmul(
                        kb.PSS[2][:], VTM[b][:, 128 * kt:128 * (kt + 1)],
                        ATN[kt], start=(kt == 0), stop=(kt == nkt - 1)),
                     waits=([atn_done, lv.done] + _as_waits(ctx_free))
                           if kt == 0 else None,
                     incs=(cm2 := mm.inc()) if kt == nkt - 1 else None,
                     tag=f"mx_cx{h}_{qc}")
            p.op("scalar", lambda e, qc=qc: e.activation(
                    CXH[:, 512 * qc:512 * (qc + 1)], kb.PSS[2][:], AF.Identity),
                 waits=[cm2.done] + (_as_waits(cxh_free) if qc == 0 else []),
                 incs=(cxe := ev.inc()), tag=f"mx_cxe{h}_{qc}")
            ctx_free = [cxe.done]
        p.op("sync", lambda e, h=h: e.dma_start(
                out=ctx_hbm[128 * h:128 * (h + 1), :], in_=CXH),
             waits=[cxe.done], incs=(cxs := cst.inc(16)), tag=f"mx_cxs{h}")
        cxh_free = [cxs.done]
        hfree["k"][b] = [_Wait(mm, mm.count)]
        hfree["q"][b] = [_Wait(mm, mm.count)]
        hfree["v"][b] = [_Wait(mm, mm.count)]
    ctx_all = cst.at()

    WOT2 = [kb.carve(4096 * k, 2048, BF16) for k in range(16)]
    CVS = [kb.carve(65536 + 4096 * i, 2048, BF16) for i in range(2)]
    woc = p.chan(f"mx{ci}wo")
    cvl = [p.chan(f"mx{ci}cv0"), p.chan(f"mx{ci}cv1")]
    xnl = [p.chan(f"mx{ci}xn0"), p.chan(f"mx{ci}xn1")]
    xis = [p.chan(f"mx{ci}xi0"), p.chan(f"mx{ci}xi1")]
    x2s = [p.chan(f"mx{ci}x20"), p.chan(f"mx{ci}x21")]
    wo_bf = kb.scr["woT2_bf"]
    for k in range(16):
        p.op("sync", lambda e, k=k: e.dma_start(
                out=WOT2[k], in_=wo_bf[128 * k:128 * (k + 1), :]),
             waits=[ctx_all, kb._flags["woT2_bf"]] if k == 0 else None,
             incs=(wol := woc.inc(16)), tag=f"mx_wol{k}")
    wo_ready = wol.done
    XN2B = [kb.carve(73728 + 4096 * i, 2048, BF16) for i in range(2)]
    cvs_free = [None, None]
    wbx_free = [None, None]
    xn2b_free = [None, None]
    tmp0_free = None
    bank_free = [None] * 8
    banksA = [kb.PSS[i][:] for i in range(4)] + \
             [kb.PSB[:, 512 * i:512 * (i + 1)] for i in range(4)]
    for j in range(NT):
        ob = j % 2
        p.op("gpsimd", lambda e, j=j, ob=ob: e.dma_start(
                out=CVS[ob].rearrange("p (k t) -> p k t", t=128),
                in_=ctx_hbm[:, 128 * j:128 * (j + 1)].rearrange(
                    "(k p) t -> p k t", p=128)),
             waits=([ctx_all] if j < 2 else []) + _as_waits(cvs_free[ob]),
             incs=(cvld := cvl[ob].inc(16)), tag=f"mx_cvl{j}")
        wbuf = kb.WB[ob]
        p.op("sync", lambda e, j=j, wbuf=wbuf: e.dma_start(
                out=wbuf[:], in_=xn_tm[128 * j:128 * (j + 1), :]),
             waits=_as_waits(wbx_free[ob]) + (lnr["tm_done"] if j < 2 else []),
             incs=(xnr := xnl[ob].inc(16)),
             tag=f"mx_xnr{j}")
        for c in range(4):
            bk = (j % 2) * 4 + c
            for k in range(16):
                p.op("tensor", lambda e, k=k, c=c, ob=ob, bk=bk: e.matmul(
                        banksA[bk], CVS[ob][:, 128 * k:128 * (k + 1)],
                        WOT2[k][:, 512 * c:512 * (c + 1)],
                        start=(k == 0), stop=(k == 15)),
                     waits=([cvld.done, wo_ready] + _as_waits(bank_free[bk]) +
                            (_as_waits(psb_scores_free) if j < 2 else []) +
                            (_as_waits(ctx_free) if j < 2 else []))
                           if k == 0 else None,
                     incs=(wm := mm.inc()) if k == 15 else None,
                     tag=f"mx_wo{j}_{c}")
            p.op("vector", lambda e, c=c, ob=ob, bk=bk, wbuf=wbuf: e.tensor_add(
                    kb.TMP[0][:, 512 * c:512 * (c + 1)],
                    wbuf[:, 512 * c:512 * (c + 1)], banksA[bk]),
                 waits=([wm.done, xnr.done] + _as_waits(tmp0_free))
                       if c == 0 else [wm.done],
                 incs=(xia := dv.inc()), tag=f"mx_xia{j}_{c}")
            bank_free[bk] = [xia.done]
        p.op("sync", lambda e, j=j: e.dma_start(
                out=xi_hbm[128 * j:128 * (j + 1), :], in_=kb.TMP[0][:]),
             waits=[xia.done], incs=(xist := xis[ob].inc(16)),
             tag=f"mx_xis{j}")
        wbx_free[ob] = [xia.done]
        cvs_free[ob] = [wm.done]
        s_c = SM[:, 240 + j:241 + j]
        q_c = SM[:, 256 + j:257 + j]
        m_c = SM[:, 464 + j:465 + j]
        v_c = SM[:, 468 + j:469 + j]
        r_c = SM[:, 272 + j:273 + j]
        n_c = SM[:, 288 + j:289 + j]
        p.op("vector", lambda e, o=s_c: e.reduce_sum(o, kb.TMP[0][:], AX.X),
             waits=[xia.done], incs=(b1 := dv.inc()))
        p.op("scalar", lambda e, o=q_c, ob=ob: e.activation(
                kb.XT[ob][:], kb.TMP[0][:], AF.Square, accum_out=o),
             waits=[xia.done], incs=(b2 := ev.inc()))
        p.op("vector", lambda e, s=s_c, o=m_c: e.tensor_scalar_mul(o, s, 1.0 / D),
             waits=[b1.done], incs=(b3 := dv.inc()))
        p.op("scalar", lambda e, m=m_c, o=v_c: e.activation(o, m, AF.Square),
             waits=[b3.done], incs=(b4 := ev.inc()))
        p.op("vector", lambda e, q=q_c, m2=v_c, o=v_c: e.scalar_tensor_tensor(
                o, q, 1.0 / D, m2, ALU.mult, ALU.subtract),
             waits=[b2.done, b4.done], incs=(b5 := dv.inc()))
        p.op("vector", lambda e, o=v_c: e.tensor_scalar_add(o, o, EPS_LN),
             waits=[b5.done], incs=(b6 := dv.inc()))
        p.op("scalar", lambda e, v=v_c, o=r_c: e.activation(o, v, AF.Ln),
             waits=[b6.done], incs=(b7 := ev.inc()))
        p.op("scalar", lambda e, r=r_c: e.activation(r, r, AF.Exp, scale=-0.5),
             incs=(b8 := ev.inc()))
        p.op("vector", lambda e, m=m_c, r=r_c, o=n_c: e.scalar_tensor_tensor(
                o, m, -1.0, r, ALU.mult, ALU.mult),
             waits=[b8.done], incs=(b9 := dv.inc()))
        p.op("scalar", lambda e, ob=ob, r=r_c, n=n_c: e.activation(
                XN2B[ob], kb.TMP[0][:], AF.Identity, bias=n, scale=r),
             waits=[b9.done] + _as_waits(xn2b_free[ob]),
             incs=(x2n := ev.inc()), tag=f"mx_x2n{j}")
        tmp0_free = [x2n.done, xist.done]
        p.op("sync", lambda e, j=j, ob=ob: e.dma_start(
                out=xn2_tm[128 * j:128 * (j + 1), :], in_=XN2B[ob]),
             waits=[x2n.done], incs=(x2st := x2s[ob].inc(16)),
             tag=f"mx_x2s{j}")
        xn2b_free[ob] = [x2st.done]
    xn2_all = [x2s[0].at(), x2s[1].at()]
    wo_mm_all = _Wait(mm, mm.count)

    if "ff" not in stages:
        return dict(done=xn2_all[0])
    tr2 = p.chan(f"mx{ci}tr2")
    for dt in range(ND):
        p.op("sync", lambda e, dt=dt: e.dma_start_transpose(
                kb.xfm(dt), xn2_tm[:, 128 * dt:128 * (dt + 1)]),
             waits=(xn2_all + [wo_mm_all]) if dt == 0 else None,
             incs=(t2 := tr2.inc(16)), tag=f"mx_x2t{dt}")
    tr2_all = t2.done
    for dt in range(ND):
        p.op("vector", lambda e, dt=dt: e.tensor_scalar(
                kb.xfm(dt), kb.xfm(dt), kb.LN2S[:, dt:dt + 1],
                kb.LN2B[:, dt:dt + 1], ALU.mult, ALU.add),
             waits=[tr2_all] if dt == 0 else None,
             incs=(sb2 := dv.inc()), tag=f"mx_s2b{dt}")
    xn2_ready = sb2.done

    kb.act_set(SET_GELU_T)
    UPW = [kb.carve(65536 + 16384 * i, 8192, BF16) for i in range(2)]
    GF = [kb.carve(98304 + 4096 * i, 2048, BF16) for i in range(2)]
    f1c = [p.chan(f"mx{ci}f10"), p.chan(f"mx{ci}f11")]
    gsc = [p.chan(f"mx{ci}gs0"), p.chan(f"mx{ci}gs1")]
    ff1_bf = kb.scr["wff1T_bf"]
    ff2_bf = kb.scr["wff2T_bf"]
    upw_free = [None, None]
    gf_free = [None, None]
    pbank_free = {0: _as_waits(bank_free[0]) + _as_waits(bank_free[1]) +
                     _as_waits(bank_free[2]) + _as_waits(bank_free[3]),
                  1: _as_waits(bank_free[4]) + _as_waits(bank_free[5]) +
                     _as_waits(bank_free[6]) + _as_waits(bank_free[7])}
    for mq in range(NH // 4):
        wb2 = mq % 2
        for k in range(ND):
            p.op("sync", lambda e, mq=mq, k=k, wb2=wb2: e.dma_start(
                    out=UPW[wb2][:, 512 * k:512 * (k + 1)],
                    in_=ff1_bf[128 * k:128 * (k + 1),
                               512 * mq:512 * (mq + 1)]),
                 waits=(([kb._flags["wff1T_bf"], wo_mm_all] + xn2_all)
                        if mq < 2 and k == 0
                        else []) + (_as_waits(upw_free[wb2]) if k == 0 else []),
                 incs=(f1l := f1c[wb2].inc(16)), tag=f"mx_f1w{mq}_{k}")
        f1_ready = f1l.done
        for mm2 in range(4):
            m = 4 * mq + mm2
            pb = m % 2
            banks = ([kb.PSS[c][:] for c in range(4)] if pb == 0 else
                     [kb.PSB[:, 512 * c:512 * (c + 1)] for c in range(4)])
            for k in range(ND):
                for c in range(4):
                    w = None
                    if k == 0 and c == 0:
                        w = [f1_ready, xn2_ready] + _as_waits(pbank_free[pb])
                    p.op("tensor", lambda e, k=k, c=c, m=m, wb2=wb2, banks=banks:
                         e.matmul(
                            banks[c],
                            UPW[wb2][:, 512 * k + 128 * (m % 4):
                                     512 * k + 128 * (m % 4) + 128],
                            kb.xfm(k)[:, 512 * c:512 * (c + 1)],
                            start=(k == 0), stop=(k == ND - 1)),
                         waits=w,
                         incs=(f1m := mm.inc()) if (k == ND - 1 and c == 3)
                              else None,
                         tag=f"mx_f1{m}")
            gfb = m % 2
            for c in range(4):
                p.op("scalar", lambda e, c=c, gfb=gfb, banks=banks: e.activation(
                        GF[gfb][:, 512 * c:512 * (c + 1)], banks[c],
                        AF.Gelu_apprx_tanh),
                     waits=([f1m.done] + _as_waits(gf_free[gfb])) if c == 0
                           else None,
                     incs=(ge := ev.inc()), tag=f"mx_gev{m}")
            pbank_free[pb] = [ge.done]
            p.op("sync", lambda e, m=m, gfb=gfb: e.dma_start(
                    out=g2_hbm[128 * m:128 * (m + 1), :], in_=GF[gfb]),
                 waits=[ge.done], incs=(gst := gsc[gfb].inc(16)),
                 tag=f"mx_gst{m}")
            gf_free[gfb] = [gst.done]
        upw_free[wb2] = [_Wait(mm, mm.count)]
    up2_mm_all = _Wait(mm, mm.count)
    g2_all = [gsc[0].at(), gsc[1].at()]
    kb.act_set(SET_NL_EXP)

    WRESQ = kb.carve(0, 64 * 512, BF16)
    GSB = [kb.carve(65536 + 16384 * i, 8192, BF16) for i in range(2)]
    wrc2 = p.chan(f"mx{ci}wr2")
    glc2 = [p.chan(f"mx{ci}gl0"), p.chan(f"mx{ci}gl1")]
    xoc = p.chan(f"mx{ci}xoc")
    xac = [p.chan(f"mx{ci}xa0"), p.chan(f"mx{ci}xa1")]
    xbc = [p.chan(f"mx{ci}xb0"), p.chan(f"mx{ci}xb1")]
    xcc = [p.chan(f"mx{ci}xc0"), p.chan(f"mx{ci}xc1")]
    gsb_free = [None, None]
    wres_free = None
    zb_free = [pbank_free[0], pbank_free[0]]
    fb_free = [pbank_free[0], pbank_free[0]]
    xt_free = [None, None]
    smb_free = None
    blend_prev = None
    for qtr in range(4):
        for k in range(NH):
            p.op("sync", lambda e, qtr=qtr, k=k: e.dma_start(
                    out=WRESQ[:, 512 * k:512 * (k + 1)],
                    in_=ff2_bf[128 * k:128 * (k + 1),
                               512 * qtr:512 * (qtr + 1)]),
                 waits=([up2_mm_all, kb._flags["wff2T_bf"]] +
                        _as_waits(wres_free)) if k == 0 else None,
                 incs=(wr2 := wrc2.inc(16)), tag=f"mx_f2w{qtr}_{k}")
        wres_ready = wr2.done
        for j in range(NT):
            gb = j % 2
            p.op("gpsimd", lambda e, j=j, gb=gb: e.dma_start(
                    out=GSB[gb].rearrange("p (k t) -> p k t", t=128),
                    in_=g2_hbm[:, 128 * j:128 * (j + 1)].rearrange(
                        "(k p) t -> p k t", p=128)),
                 waits=(g2_all if (qtr == 0 and j < 2) else []) +
                       _as_waits(gsb_free[gb]),
                 incs=(gl2 := glc2[gb].inc(16)), tag=f"mx_gsl{qtr}_{j}")
            fbk = kb.PSS[j % 2][:]
            zbk = kb.PSS[2 + j % 2][:]
            for k in range(NH):
                w = None
                if k == 0:
                    w = [gl2.done, wres_ready] + _as_waits(fb_free[j % 2]) + \
                        ([up2_mm_all] if (qtr == 0 and j < 2) else [])
                p.op("tensor", lambda e, k=k, gb=gb, fbk=fbk: e.matmul(
                        fbk, GSB[gb][:, 128 * k:128 * (k + 1)],
                        WRESQ[:, 512 * k:512 * (k + 1)],
                        start=(k == 0), stop=(k == NH - 1)),
                     waits=w,
                     incs=(fmm := mm.inc()) if k == NH - 1 else None,
                     tag=f"mx_f2{qtr}_{j}")
            p.op("tensor", lambda e, j=j, qtr=qtr, zbk=zbk: e.matmul(
                    zbk, kb.T1B[:, 128 * j:128 * (j + 1)],
                    kb.WUPT1[:, 512 * qtr:512 * (qtr + 1)],
                    start=True, stop=True),
                 waits=([t1_done] if (qtr == 0 and j < 2) else []) +
                       _as_waits(zb_free[j % 2]),
                 incs=(zmm := mm.inc()), tag=f"mx_z{qtr}_{j}")
            gsb_free[gb] = [_Wait(mm, mm.count)]
            xb2 = j % 2
            p.op("sync", lambda e, j=j, qtr=qtr, xb2=xb2: e.dma_start(
                    out=kb.XT[xb2][:, 0:512],
                    in_=x_in[128 * j:128 * (j + 1), 512 * qtr:512 * (qtr + 1)]),
                 waits=_as_waits(xt_free[xb2]), incs=(lxo := xac[xb2].inc(16)),
                 tag=f"mx_bxl{qtr}_{j}")
            p.op("sync", lambda e, j=j, qtr=qtr, xb2=xb2: e.dma_start(
                    out=kb.XT[xb2][:, 512:1024],
                    in_=xi_hbm[128 * j:128 * (j + 1),
                               512 * qtr:512 * (qtr + 1)]),
                 waits=[xis[xb2].at()] if qtr == 0 and j < 2 else None,
                 incs=(lxi := xbc[xb2].inc(16)), tag=f"mx_bxi{qtr}_{j}")
            p.op("sync", lambda e, j=j, qtr=qtr, xb2=xb2: e.dma_start(
                    out=kb.XT[xb2][:, 1024:1280].bitcast(BF16),
                    in_=xn_tm[128 * j:128 * (j + 1),
                              512 * qtr:512 * (qtr + 1)]),
                 incs=(lxn := xcc[xb2].inc(16)), tag=f"mx_bxn{qtr}_{j}")
            is0 = SM[:, 304 + j:305 + j]
            is1 = SM[:, 320 + j:321 + j]
            is2 = SM[:, 336 + j:337 + j]
            is01 = SM[:, 352 + j:353 + j]
            acc = kb.TMP[0][:, 0:512]
            fin = kb.TMP[0][:, 512:1024]
            xnbf = kb.XT[xb2][:, 1024:1280].bitcast(BF16)
            p.op("vector", lambda e, xb2=xb2, is2=is2, acc=acc: e.tensor_scalar(
                    acc, kb.XT[xb2][:, 512:1024], is2, None, ALU.mult),
                 waits=[lxi.done, masks_done] + _as_waits(blend_prev),
                 incs=(c1 := dv.inc()), tag=f"mx_b1{qtr}_{j}")
            if f"mx{ci}_F" in kb.dbg:
                dbgF = kb.dram(f"mx{ci}_F", (S, D), F32)
                FD = kb.carve(98304, 512, F32)
                p.op("scalar", lambda e, fbk=fbk, FD=FD: e.activation(
                        FD, fbk, AF.Identity),
                     waits=[fmm.done], incs=(fde := ev.inc()), tag="dbgFe")
                p.op("sync", lambda e, j=j, qtr=qtr, FD=FD: e.dma_start(
                        out=dbgF[128 * j:128 * (j + 1),
                                 512 * qtr:512 * (qtr + 1)], in_=FD),
                     waits=[fde.done], incs=(fds := xoc.inc(16)), tag="dbgFs")
                fwait = [fmm.done, fds.done]
            else:
                fwait = [fmm.done]
            p.op("vector", lambda e, fbk=fbk, is2=is2, acc=acc:
                 e.scalar_tensor_tensor(acc, fbk, is2, acc, ALU.mult, ALU.add),
                 waits=fwait + [c1.done], incs=(c2 := dv.inc()), tag=f"mx_b2{qtr}_{j}")
            fb_free[j % 2] = [c2.done]
            p.op("vector", lambda e, zbk=zbk, is1=is1, acc=acc:
                 e.scalar_tensor_tensor(acc, zbk, is1, acc, ALU.mult, ALU.add),
                 waits=[zmm.done, c2.done], incs=(c3 := dv.inc()), tag=f"mx_b3{qtr}_{j}")
            zb_free[j % 2] = [c3.done]
            p.op("vector", lambda e, xnbf=xnbf, is01=is01, acc=acc:
                 e.scalar_tensor_tensor(acc, xnbf, is01, acc, ALU.mult, ALU.add),
                 waits=[lxn.done, c3.done], incs=(c4 := dv.inc()), tag=f"mx_b4{qtr}_{j}")
            p.op("vector", lambda e, qtr=qtr, is0=is0, acc=acc:
                 e.scalar_tensor_tensor(
                    acc, kb.BIAS_BC[:, 512 * qtr:512 * (qtr + 1)], is0, acc,
                    ALU.mult, ALU.add),
                 waits=[c4.done], incs=(c5 := dv.inc()), tag=f"mx_b5{qtr}_{j}")
            p.op("vector", lambda e, xb2=xb2, acc=acc, fin=fin:
                 e.scalar_tensor_tensor(fin, acc, 0.5, kb.XT[xb2][:, 0:512],
                                        ALU.mult, ALU.add),
                 waits=[lxo.done, c5.done], incs=(c6 := dv.inc()), tag=f"mx_b6{qtr}_{j}")
            p.op("sync", lambda e, j=j, qtr=qtr, fin=fin: e.dma_start(
                    out=x_out[128 * j:128 * (j + 1),
                              512 * qtr:512 * (qtr + 1)], in_=fin),
                 waits=[c6.done], incs=(xo := xoc.inc(16)),
                 tag=f"mx_xo{qtr}_{j}")
            xt_free[xb2] = [c6.done]
            blend_prev = [xo.done]
        wres_free = [_Wait(mm, mm.count)]
    if f"mx{ci}_msk" in kb.dbg:
        dbgM = kb.dram(f"mx{ci}_msk", (128, 128), F32)
        p.op("sync", lambda e: e.dma_start(out=dbgM[:, 0:64],
                                           in_=SM[:, 304:368]),
             waits=[xoc.at()], incs=xoc.inc(16), tag="dbgMsk")
        p.op("sync", lambda e: e.dma_start(out=dbgM[:, 64:128],
                                           in_=kb.LGT[:]),
             incs=xoc.inc(16), tag="dbgLgt")
    return dict(done=xoc.at())


from concourse.bass_utils import run_bass_kernel_spmd

_SEQ = [("ffn0", 0), ("mamba", None), ("ffn0", 0), ("mix1", 0), ("mla", None),
        ("mix1", 0)]


def _build_one(which, ci, shapes):
    nc = bass.Bass("TRN2", target_bir_lowering=False,
                   detect_race_conditions=False)
    kbb = KB(nc, shapes)
    x_in = kbb.din["x_in"]
    x_out = kbb.dram_out("x_out", (S, D), F32)
    if which == "ffn0":
        r = emit_ffn0(kbb, x_in, x_out, ci, start=[kbb.const_ready])
    elif which == "mamba":
        r = emit_mamba(kbb, x_in, x_out, start=[kbb.const_ready])
    elif which == "mla":
        r = emit_mla(kbb, x_in, x_out, start=[kbb.const_ready])
    else:
        r = emit_mix1(kbb, x_in, x_out, ci, start=[kbb.const_ready])
    kbb.p.op("sync", lambda e: None, waits=[r["done"]], tag="final")
    kbb.p.build()
    return nc


def kernel(**inputs):
    g, xs = prepare_inputs(inputs)
    shapes = shapes_of(g)
    shapes["x_in"] = ((S, D), F32)
    ncores = len(xs)
    cur = xs
    ncs = {}
    for which, ci in _SEQ:
        if which not in ncs:
            ncs[which] = _build_one(which, ci, shapes)
        nc = ncs[which]
        in_maps = [dict(g, x_in=np.ascontiguousarray(cur[c]))
                   for c in range(ncores)]
        res = run_bass_kernel_spmd(nc, in_maps, core_ids=list(range(ncores)))
        cur = [np.asarray(res.results[c]["x_out"]) for c in range(ncores)]
    return np.stack(cur, 0)


# version 02
# speedup vs baseline: 1.0235x; 1.0235x over previous
import numpy as np
import ml_dtypes

import concourse.bass as bass
import concourse.mybir as mybir

F32 = mybir.dt.float32
BF16 = mybir.dt.bfloat16
I32 = mybir.dt.int32
U8 = mybir.dt.uint8
AF = mybir.ActivationFunctionType
ALU = mybir.AluOpType
AX = mybir.AxisListType
BF = ml_dtypes.bfloat16

ENGINES = ("sync", "tensor", "vector", "scalar", "gpsimd")

S = 2048
D = 2048
HID = 8192
NT = S // 128
ND = D // 128
NH = HID // 128
EPS_LN = 1e-5
EPS_RMS = 1e-6

SET_NL_EXP = "natural_log_exp_and_others"
SET_GELU_T = "gelu_apprx_tanh_and_others"

ARENA_B = 131072


class Chan:
    __slots__ = ("sem", "count", "name", "owner")

    def __init__(self, sem, name):
        self.sem = sem
        self.count = 0
        self.name = name
        self.owner = None

    def inc(self, amt=1):
        self.count += amt
        return _Inc(self, self.count, amt)

    def at(self, val=None):
        return _Wait(self, self.count if val is None else val)


class _Inc:
    __slots__ = ("chan", "after", "amt")

    def __init__(self, chan, after, amt):
        self.chan, self.after, self.amt = chan, after, amt

    @property
    def done(self):
        return _Wait(self.chan, self.after)


class _Wait:
    __slots__ = ("chan", "val")

    def __init__(self, chan, val):
        self.chan, self.val = chan, val


class _Item:
    __slots__ = ("fn", "waits", "incs", "tag")

    def __init__(self, fn, waits, incs, tag):
        self.fn, self.waits, self.incs, self.tag = fn, waits, incs, tag


def _as_waits(w):
    if w is None:
        return []
    if isinstance(w, (_Wait, _Inc)):
        w = [w]
    out = []
    for x in w:
        if x is None:
            continue
        if isinstance(x, _Inc):
            x = x.done
        assert isinstance(x, _Wait), f"bad wait token {x!r}"
        out.append(x)
    return out


class Prog:
    def __init__(self, nc):
        self.nc = nc
        self.items = {e: [] for e in ENGINES}
        self.n_ops = 0
        self._chans = {}

    def chan(self, name):
        if name not in self._chans:
            self._chans[name] = Chan(self.nc.alloc_semaphore(name=name), name)
        return self._chans[name]

    def op(self, engine, fn, waits=None, incs=None, tag=""):
        waits = _as_waits(waits)
        if incs is None:
            incs = []
        elif isinstance(incs, _Inc):
            incs = [incs]
        for w in waits:
            assert w.val <= w.chan.count, (
                f"op {tag!r} on {engine} waits {w.chan.name}>={w.val} "
                f"but only {w.chan.count} planned")
        for x in incs:
            ch = x.chan
            if ch.owner is None:
                ch.owner = engine
            assert ch.owner == engine, (
                f"chan {ch.name} inc'd from {engine} but owned by {ch.owner} "
                f"(op {tag!r}) — single-engine channels only")
        self.items[engine].append(_Item(fn, waits, incs, tag))
        self.n_ops += 1

    def _emit(self, eng, items):
        for it in items:
            for w in it.waits:
                eng.wait_ge(w.chan.sem, w.val)
            inst = it.fn(eng)
            if it.incs:
                if inst is not None:
                    inst.then_inc(it.incs[0].chan.sem, it.incs[0].amt)
                    rest = it.incs[1:]
                else:
                    rest = it.incs
                for x in rest:
                    eng.sem_inc(x.chan.sem, x.amt)

    def build(self):
        with self.nc.Block() as block:
            for ename in ENGINES:
                items = self.items[ename]
                if not items:
                    continue

                def make(items):
                    def f(eng):
                        self._emit(eng, items)
                    return f

                getattr(block, ename)(make(items))
        return self.nc


def _bf(a):
    return np.ascontiguousarray(np.asarray(a)).astype(BF)


def _c(a):
    return np.ascontiguousarray(np.asarray(a))


def prepare_inputs(inp):
    g = {}
    f32 = np.float32

    g["ident_bf"] = _bf(np.eye(128, dtype=f32))
    g["ones_row_bf"] = _bf(np.ones((1, 128), dtype=f32))
    g["ones_col_bf"] = _bf(np.ones((128, 1), dtype=f32))
    tri = np.where(np.arange(128)[None, :] > np.arange(128)[:, None],
                   np.float32(-1e30), np.float32(0.0))
    g["mask_tri_bf"] = _bf(tri)
    P = np.zeros((128, 128), dtype=f32)
    for i in range(0, 128, 2):
        P[i, i + 1] = 1.0
        P[i + 1, i] = 1.0
    g["pswap_bf"] = _bf(P)
    selb = np.zeros((16, 16 * 128), dtype=f32)
    for n in range(16):
        selb[n, 128 * n:128 * (n + 1)] = 1.0
    g["sel16_bf"] = _bf(selb)
    ROPE = 64
    freqs = 1.0 / 10000.0 ** (np.arange(0, ROPE, 2, dtype=f32) / ROPE)
    ang = np.outer(np.arange(S, dtype=f32), freqs)
    cos64 = np.repeat(np.cos(ang), 2, axis=1).T
    sin64 = np.repeat(np.sin(ang), 2, axis=1).T
    sgn = np.tile(np.array([-1.0, 1.0], dtype=f32), 32)[:, None]
    g["ropecos_bf"] = _bf(np.concatenate([cos64, cos64], 0))
    g["ropesin_bf"] = _bf(np.concatenate([sin64 * sgn, sin64 * sgn], 0))
    m4 = np.zeros((128, 4 * 512), dtype=f32)
    rr = np.arange(128)[:, None]
    cc = np.arange(512)[None, :]
    for q in range(4):
        m4[:, 512 * q:512 * (q + 1)] = np.where(cc > q * 128 + rr,
                                                np.float32(-1e30), 0.0)
    g["mask4_bf"] = _bf(m4)

    g["wupq"] = _c(inp["L0_ffn_w_up_q"]).astype(np.int32)
    g["wupsc"] = _c(inp["L0_ffn_w_up_scale"]).astype(f32)
    g["bup_pd"] = _c(np.asarray(inp["L0_ffn_b_up"], f32).reshape(NH, 128).T)
    g["l0ua"] = _c(inp["L0_ffn_w_up_lora_a"]).astype(f32)
    g["l0ub"] = _c(inp["L0_ffn_w_up_lora_b"]).astype(f32)
    g["wdnq"] = _c(inp["L0_ffn_w_down_q"]).astype(np.int32)
    g["wdnsc"] = _c(inp["L0_ffn_w_down_scale"]).astype(f32)
    g["bdn_row"] = _c(np.asarray(inp["L0_ffn_b_down"], f32)[None, :])
    g["l0da"] = _c(inp["L0_ffn_w_down_lora_a"]).astype(f32)
    g["l0db"] = _c(inp["L0_ffn_w_down_lora_b"]).astype(f32)

    W_x = np.asarray(inp["L0_attn_W_x"], f32)
    g["wx3T"] = _c(W_x.T)
    g["wdtT"] = _c(np.asarray(inp["L0_attn_W_dt"], f32).T)
    g["bdt_pd"] = _c(np.asarray(inp["L0_attn_b_dt"], f32).reshape(ND, 128).T)
    g["alog"] = _c(inp["L0_attn_A_log"]).astype(f32)
    g["d_pd"] = _c(np.asarray(inp["L0_attn_D"], f32).reshape(ND, 128).T)

    wq = np.asarray(inp["L1_attn_wq"], f32)
    QKH, NOPE = 192, 128
    perm = [h * QKH + i for h in range(16) for i in range(NOPE)]
    perm += [h * QKH + NOPE + i for h in range(16) for i in range(64)]
    g["wqTr"] = _c(wq[perm].T)
    wkva = np.asarray(inp["L1_attn_wkv_a"], f32)
    g["wkvaTx"] = _c(np.concatenate([wkva[:512], wkva[512:], wkva[512:]], 0).T)
    wkvb = np.asarray(inp["L1_attn_wkv_b"], f32).reshape(16, 256, 512)
    g["wkvbn"] = _c(wkvb[:, :128, :].reshape(2048, 512))
    g["wkvbvT"] = _c(wkvb[:, 128:, :].transpose(2, 0, 1).reshape(512, 2048))
    g["woT"] = _c(np.asarray(inp["L1_attn_wo"], f32).T)
    g["kvw_pd"] = _c(np.asarray(inp["L1_attn_kv_norm_w"], f32).reshape(4, 128).T)

    g["sbias_row"] = _c(np.asarray(inp["L1_ffn_simple_bias"], f32)[None, :])
    g["wdT1"] = _c(np.asarray(inp["L1_ffn_w_down"], f32).T)
    g["wupT1"] = _c(np.asarray(inp["L1_ffn_w_up"], f32).T)
    g["wrT"] = _c(np.asarray(inp["L1_ffn_w_router"], f32).T)
    for nm, key in [("ln1s", "L1_ffn_ln1_scale"), ("ln1b", "L1_ffn_ln1_bias"),
                    ("ln2s", "L1_ffn_ln2_scale"), ("ln2b", "L1_ffn_ln2_bias")]:
        g[nm + "_pd"] = _c(np.asarray(inp[key], f32).reshape(ND, 128).T)
    g["wqT2"] = _c(np.asarray(inp["L1_ffn_wq"], f32).T)
    g["wkT2"] = _c(np.asarray(inp["L1_ffn_wk"], f32).T)
    g["wvT2"] = _c(np.asarray(inp["L1_ffn_wv"], f32).T)
    g["woT2"] = _c(np.asarray(inp["L1_ffn_wo"], f32).T)
    g["wff1T"] = _c(np.asarray(inp["L1_ffn_w_ff1"], f32).T)
    g["wff2T"] = _c(np.asarray(inp["L1_ffn_w_ff2"], f32).T)

    xs = [np.ascontiguousarray(np.asarray(inp["L0_attn_x"][c], f32))
          for c in range(np.asarray(inp["L0_attn_x"]).shape[0])]
    return g, xs


DT_OF = {np.dtype(np.float32): F32, np.dtype(BF): BF16,
         np.dtype(np.int32): I32}


def shapes_of(d):
    return {k: (v.shape, DT_OF[v.dtype]) for k, v in d.items()}


class KB:
    def __init__(self, nc, shapes):
        self.nc = nc
        self.p = Prog(nc)
        self.din = {}
        for name, (shape, dt) in shapes.items():
            self.din[name] = nc.dram_tensor(
                name, list(shape), dt, kind="ExternalInput").ap()
        self.scr = {}
        self.cur_act_set = None
        self._flags = {}
        self._mk()

    dbg = frozenset()

    def dram(self, name, shape, dt):
        if name not in self.scr:
            kind = "ExternalOutput" if name in self.dbg else "Internal"
            self.scr[name] = self.nc.dram_tensor(
                name, list(shape), dt, kind=kind).ap()
        return self.scr[name]

    def dram_out(self, name, shape, dt):
        self.scr[name] = self.nc.dram_tensor(
            name, list(shape), dt, kind="ExternalOutput").ap()
        return self.scr[name]

    def _mk(self):
        nc, p = self.nc, self.p
        self.PSB = nc.alloc_psum_tensor("psb", [128, 2048], F32)
        self.PSS = [nc.alloc_psum_tensor(f"pss{i}", [128, 512], F32)
                    for i in range(4)]
        self.ARENA = nc.alloc_sbuf_tensor("ARENA", [128, ARENA_B], U8,
                                          is_transpose_dest=True)
        self.XT = [nc.alloc_sbuf_tensor(f"XT{i}", [128, 2048], F32)
                   for i in range(2)]
        self.WB = [nc.alloc_sbuf_tensor(f"WB{i}", [128, 2048], BF16,
                                        is_transpose_dest=True)
                   for i in range(2)]
        self.TMP = [nc.alloc_sbuf_tensor("TMP0", [128, 2048], F32)]
        self.SM = nc.alloc_sbuf_tensor("SM", [128, 512], F32)
        self.SMB = nc.alloc_sbuf_tensor("SMB", [128, 512], BF16)
        self.TUP = nc.alloc_sbuf_tensor("TUP", [16, 2048], BF16)
        self.TDN = nc.alloc_sbuf_tensor("TDN", [16, 2048], BF16)
        self.ROW2 = nc.alloc_sbuf_tensor("ROW2", [1, 2048], BF16)
        self.ident = nc.alloc_sbuf_tensor("identc", [128, 128], BF16)
        self.onesr = nc.alloc_sbuf_tensor("onesr", [1, 128], BF16)
        self.onesc = nc.alloc_sbuf_tensor("onesc", [128, 1], BF16)
        self.ldc = p.chan("ldconst")
        for nm, t in [("ident_bf", self.ident), ("ones_row_bf", self.onesr),
                      ("ones_col_bf", self.onesc)]:
            p.op("sync", lambda e, nm=nm, t=t: e.dma_start(
                    out=t[:], in_=self.din[nm]),
                 incs=self.ldc.inc(16), tag=f"ldc_{nm}")
        self.const_ready = self.ldc.at()

    def carve(self, off_b, n, dt):
        size = {F32: 4, BF16: 2, I32: 4}[dt]
        assert off_b % 32 == 0 and off_b + n * size <= ARENA_B, (off_b, n, dt)
        return self.ARENA[:, off_b:off_b + n * size].bitcast(dt)

    def xfm(self, i):
        return self.carve(4096 * i, 2048, BF16)


    def ensure_msk4(self):
        if hasattr(self, "MSK4"):
            return self._flags["msk4_ready"]
        nc, p = self.nc, self.p
        self.MSK4 = nc.alloc_sbuf_tensor("MSK4", [128, 2048], BF16)
        ch = p.chan("msk4")
        p.op("sync", lambda e: e.dma_start(out=self.MSK4[:],
                                           in_=self.din["mask4_bf"]),
             incs=ch.inc(16), tag="ld_msk4")
        self._flags["msk4_ready"] = ch.at()
        return self._flags["msk4_ready"]

    def act_set(self, set_name, waits=None):
        if self.cur_act_set == set_name:
            return
        self.cur_act_set = set_name
        from concourse.hw_specs import get_activation_tables
        tabs = list(get_activation_tables(self.nc.m.arch).keys())
        idx = tabs.index(set_name)

        def fn(e, idx=idx):
            return e.add_instruction(mybir.InstLoadActFuncSet(
                name=e.bass.get_next_instruction_name(),
                act_func_set_id=idx, ins=[], outs=[]))
        self.p.op("scalar", fn, waits=waits, tag=f"actset_{set_name}")

    def cast_weight(self, name, grp="g", waits=None):
        key = name + "_bf"
        if key in self.scr:
            return self.scr[key], self._flags.get(key)
        src = self.din[name]
        dst = self.dram(key, src.shape, BF16)
        ch = self.p.chan("cast_" + grp)
        self.p.op("gpsimd", lambda e: e.dma_start(out=dst, in_=src),
                  waits=waits, incs=ch.inc(16), tag=f"cast_{name}")
        self._flags.setdefault("_castgrp", {}).setdefault(grp, []).append(key)
        return dst, None

    def cast_group_done(self, grp):
        ch = self.p.chan("cast_" + grp)
        tok = ch.at()
        for key in self._flags.get("_castgrp", {}).get(grp, []):
            self._flags[key] = tok
        return tok

    def once(self, key):
        if key in self._flags:
            return False
        self._flags[key] = True
        return True


def emit_ln(kb, x_in, xn_tm, tag, start=None, inner_stats=None):
    p = kb.p
    ld = [p.chan(tag + "_ld0"), p.chan(tag + "_ld1")]
    stc = [p.chan(tag + "_st0"), p.chan(tag + "_st1")]
    trc = p.chan(tag + "_tr")
    cs = p.chan(tag + "_cs")
    cv = p.chan(tag + "_cv")
    SM = kb.SM
    start = _as_waits(start)
    xt_free = [None, None]
    wb_free = [None, None]
    inner_done = []
    for i in range(NT):
        b = i % 2
        xt = kb.XT[b]
        p.op("sync", lambda e, xt=xt, i=i: e.dma_start(
                out=xt[:], in_=x_in[128 * i:128 * (i + 1), :]),
             waits=(start if i < 2 else []) + _as_waits(xt_free[b]),
             incs=(ldi := ld[b].inc(16)), tag=f"{tag}_ld{i}")
        sum_c = SM[:, i:i + 1]
        sq_c = SM[:, 16 + i:17 + i]
        p.op("vector", lambda e, xt=xt, o=sum_c: e.reduce_sum(o, xt[:], AX.X),
             waits=[ldi.done], incs=(esum := cv.inc()), tag=f"{tag}_sum{i}")
        p.op("scalar", lambda e, xt=xt, o=sq_c: e.activation(
                kb.TMP[0][:], xt[:], AF.Square, accum_out=o),
             waits=[ldi.done], incs=(esq := cs.inc()), tag=f"{tag}_sq{i}")
        mean_c = SM[:, 32 + i:33 + i]
        m2_c = SM[:, 96 + i:97 + i]
        var_c = SM[:, 112 + i:113 + i]
        lnv_c = SM[:, 128 + i:129 + i]
        rstd_c = SM[:, 48 + i:49 + i]
        nb_c = SM[:, 64 + i:65 + i]
        p.op("vector", lambda e, a=sum_c, o=mean_c: e.tensor_scalar_mul(
                o, a, 1.0 / D), waits=[esum.done], incs=(em := cv.inc()))
        p.op("scalar", lambda e, a=mean_c, o=m2_c: e.activation(
                o, a, AF.Square), waits=[em.done], incs=(em2 := cs.inc()))
        p.op("vector", lambda e, a=sq_c, b2=m2_c, o=var_c: e.scalar_tensor_tensor(
                o, a, 1.0 / D, b2, ALU.mult, ALU.subtract),
             waits=[esq.done, em2.done], incs=(ev0 := cv.inc()))
        p.op("vector", lambda e, o=var_c: e.tensor_scalar_add(o, o, EPS_LN),
             waits=[ev0.done], incs=(ev := cv.inc()))
        p.op("scalar", lambda e, a=var_c, o=lnv_c: e.activation(
                o, a, AF.Ln), waits=[ev.done],
             incs=(el := cs.inc()))
        p.op("scalar", lambda e, a=lnv_c, o=rstd_c: e.activation(
                o, a, AF.Exp, scale=-0.5), waits=[el.done],
             incs=(er := cs.inc()))
        p.op("vector", lambda e, a=mean_c, r=rstd_c, o=nb_c: e.scalar_tensor_tensor(
                o, a, -1.0, r, ALU.mult, ALU.mult),
             waits=[er.done], incs=(en := cv.inc()))
        wbuf = kb.WB[b]
        p.op("scalar", lambda e, xt=xt, r=rstd_c, nb=nb_c, o=wbuf: e.activation(
                o[:], xt[:], AF.Identity, bias=nb, scale=r),
             waits=[en.done] + _as_waits(wb_free[b]),
             incs=(ecp := cs.inc()), tag=f"{tag}_norm{i}")
        xt_free[b] = [ecp.done]
        extra = []
        if inner_stats is not None:
            s_c, q_c = inner_stats(i)
            p.op("vector", lambda e, wb=wbuf, o=s_c: e.reduce_sum(o, wb[:], AX.X),
                 waits=[ecp.done], incs=(eis := cv.inc()))
            p.op("scalar", lambda e, wb=wbuf, o=q_c: e.activation(
                    kb.TMP[0][:], wb[:], AF.Square, accum_out=o),
                 waits=[ecp.done], incs=(eiq := cs.inc()))
            extra = [eis.done, eiq.done]
            inner_done.append([eis.done, eiq.done])
        p.op("sync", lambda e, wbuf=wbuf, i=i: e.dma_start(
                out=xn_tm[128 * i:128 * (i + 1), :], in_=wbuf[:]),
             waits=[ecp.done], incs=(est := stc[b].inc(16)), tag=f"{tag}_st{i}")
        wb_free[b] = [est.done] + extra
    all_st = [stc[0].at(), stc[1].at()]
    fm_ready = []
    for dt in range(ND):
        p.op("sync", lambda e, dt=dt: e.dma_start_transpose(
                kb.xfm(dt), xn_tm[:, 128 * dt:128 * (dt + 1)]),
             waits=all_st if dt == 0 else None,
             incs=(etr := trc.inc(16)), tag=f"{tag}_tr{dt}")
    fm_all = trc.at()
    fm_ready = [fm_all] * ND
    return dict(fm_ready=fm_ready, tm_done=all_st, inner_done=inner_done,
                rstd=lambda i: kb.SM[:, 48 + i:49 + i],
                negb=lambda i: kb.SM[:, 64 + i:65 + i])


def emit_dequant(kb, start=None):
    p = kb.p
    wup_bf = kb.dram("wup_bf", (HID, D), BF16)
    wdn_bf = kb.dram("wdn_bf", (D, HID), BF16)
    cd = p.chan("dq_d")
    cqi = [p.chan("dq_i0"), p.chan("dq_i1")]
    cst = [p.chan("dq_s0"), p.chan("dq_s1")]
    cq = p.chan("dq_v")
    QI = [kb.carve(65536 + 8192 * i, 2048, I32) for i in range(2)]
    QF = kb.carve(65536 + 16384, 2048, F32)
    QS = kb.carve(65536 + 24576, 128, F32)
    QO = [kb.carve(65536 + 25088 + 4096 * i, 2048, BF16) for i in range(2)]
    qi_free = [None, None]
    qo_free = [None, None]
    sc_free = [None]

    def one(mat_q, mat_sc, out_bf, rows, cols, scdim, tagp):
        ncolt = cols // 2048
        scblk = scdim // ncolt
        for r in range(rows // 128):
            p.op("sync", lambda e, r=r: e.dma_start(
                    out=QS[:, 0:scdim],
                    in_=mat_sc[128 * r:128 * (r + 1), :]),
                 waits=_as_waits(sc_free[0]) + (_as_waits(start) if r == 0 else []),
                 incs=(lsc := cd.inc(16)), tag=f"{tagp}_sc{r}")
            last_mul = None
            for c in range(ncolt):
                b = (r * ncolt + c) % 2
                p.op("sync", lambda e, r=r, c=c, b=b: e.dma_start(
                        out=QI[b], in_=mat_q[128 * r:128 * (r + 1),
                                             2048 * c:2048 * (c + 1)]),
                     waits=qi_free[b], incs=(lqi := cqi[b].inc(16)),
                     tag=f"{tagp}_qi")
                p.op("vector", lambda e, b=b: e.tensor_scalar_add(
                        QF, QI[b], -7.5),
                     waits=[lqi.done], incs=(e1 := cq.inc()), tag=f"{tagp}_sub")
                qi_free[b] = [e1.done]

                def mulfn(e, c=c, b=b, scblk=scblk):
                    in0 = QF.rearrange("p (a v) -> p a v", v=64)
                    in1 = QS[:, scblk * c:scblk * (c + 1)].unsqueeze(
                        2).broadcast_to((128, 32, 64))
                    out = QO[b].rearrange("p (a v) -> p a v", v=64)
                    return e.tensor_tensor(out, in0, in1, ALU.mult)
                p.op("vector", mulfn,
                     waits=[e1.done, lsc.done] + _as_waits(qo_free[b]),
                     incs=(e2 := cq.inc()), tag=f"{tagp}_mul")
                last_mul = e2
                p.op("sync", lambda e, r=r, c=c, b=b: e.dma_start(
                        out=out_bf[128 * r:128 * (r + 1),
                                   2048 * c:2048 * (c + 1)], in_=QO[b]),
                     waits=[e2.done], incs=(e3 := cst[b].inc(16)),
                     tag=f"{tagp}_st")
                qo_free[b] = [e3.done]
            sc_free[0] = [last_mul.done]

    one(kb.din["wupq"], kb.din["wupsc"], wup_bf, HID, D, 32, "dqu")
    one(kb.din["wdnq"], kb.din["wdnsc"], wdn_bf, D, HID, 128, "dqd")
    return dict(wup_bf=wup_bf, wdn_bf=wdn_bf,
                done=[cst[0].at(), cst[1].at()])


def emit_ffn0(kb, x_in, x_out, ci, start=None):
    p = kb.p
    kb.act_set(SET_NL_EXP, waits=None)

    if kb.once("ffn0_setup"):
        dq = emit_dequant(kb, start=None)
        kb._flags["dq"] = dq
        sld = p.chan("ffn0_sld")
        lua_bf, _ = kb.cast_weight("l0ua", "f")
        lub_bf, _ = kb.cast_weight("l0ub", "f")
        dla_bf, _ = kb.cast_weight("l0da", "f")
        dlb_bf, _ = kb.cast_weight("l0db", "f")
        bdn_bf, _ = kb.cast_weight("bdn_row", "f")
        w1 = w3 = w4 = kb.cast_group_done("f")
        nc = kb.nc
        kb.LUA = nc.alloc_sbuf_tensor("LUA", [128, 256], BF16)
        kb.DLA = nc.alloc_sbuf_tensor("DLA", [128, 1024], BF16)
        kb.DLB = nc.alloc_sbuf_tensor("DLB", [16, 2048], BF16)
        kb.BUP = nc.alloc_sbuf_tensor("BUP", [128, 64], F32)
        p.op("sync", lambda e: e.dma_start(
                out=kb.LUA[:].rearrange("p (k r) -> p k r", r=16),
                in_=lua_bf.rearrange("(k p) r -> p k r", p=128)),
             waits=[w1], incs=sld.inc(16))
        p.op("sync", lambda e: e.dma_start(
                out=kb.DLA[:].rearrange("p (k r) -> p k r", r=16),
                in_=dla_bf.rearrange("(k p) r -> p k r", p=128)),
             waits=[w3], incs=sld.inc(16))
        p.op("sync", lambda e: e.dma_start(out=kb.DLB[:], in_=dlb_bf),
             waits=[w4], incs=sld.inc(16))
        p.op("sync", lambda e: e.dma_start(out=kb.BUP[:], in_=kb.din["bup_pd"]),
             incs=sld.inc(16))
        kb._flags["ffn0_small"] = sld.at()
    dq = kb._flags["dq"]
    small_ready = kb._flags["ffn0_small"]

    xn_tm = kb.dram(f"xn_tm", (S, D), BF16)
    g_nat = kb.dram(f"ffn_g", (HID, D), BF16)
    ln = emit_ln(kb, x_in, xn_tm, "LN", start=start)
    LUB = kb.carve(106496, 8192, BF16)[0:16, :]
    lubc = p.chan("f_lub")
    p.op("sync", lambda e: e.dma_start(out=LUB, in_=kb.scr["l0ub_bf"]),
         waits=[kb._flags["l0ub_bf"]] + _as_waits(start),
         incs=(lub_ld := lubc.inc(16)), tag=f"f{ci}lubld")

    mmc = p.chan("g_mm")
    evc = p.chan("g_ev")
    wc = [p.chan("f_wl0"), p.chan("f_wl1")]
    wrc = p.chan("f_wr")
    gc = [p.chan("f_g0"), p.chan("f_g1")]
    gl_c = [p.chan("f_gl0"), p.chan("f_gl1")]
    tdc = [p.chan("f_td0"), p.chan("f_td1")]
    dnc = [p.chan("f_dn0"), p.chan("f_dn1")]
    dnv = p.chan("g_dv")
    xoc = p.chan("f_xo")

    for k in range(ND):
        for c in range(4):
            p.op("tensor", lambda e, k=k, c=c: e.matmul(
                    kb.PSB[0:16, 512 * c:512 * (c + 1)],
                    kb.LUA[:, 16 * k:16 * (k + 1)],
                    kb.xfm(k)[:, 512 * c:512 * (c + 1)],
                    start=(k == 0), stop=(k == ND - 1)),
                 waits=[ln["fm_ready"][k], small_ready] if c == 0 else None,
                 incs=(tup_mm := mmc.inc()) if (k == ND - 1 and c == 3) else None,
                 tag=f"f{ci}tup")
    for c in range(4):
        p.op("scalar", lambda e, c=c: e.activation(
                kb.TUP[:, 512 * c:512 * (c + 1)],
                kb.PSB[0:16, 512 * c:512 * (c + 1)], AF.Copy),
             waits=[tup_mm.done] if c == 0 else None,
             incs=(tup_ev := evc.inc()), tag=f"f{ci}tupev")
    tup_done = tup_ev.done

    UPW = [kb.carve(65536 + 16384 * i, 8192, BF16) for i in range(2)]
    GF = [kb.carve(98304 + 4096 * i, 2048, BF16) for i in range(2)]
    upw_free = [None, None]
    gf_free = [None, None]
    bank_free = {0: None, 1: [tup_done]}
    g_done_all = []
    for mq in range(NH // 4):
        wb = mq % 2
        for k in range(ND):
            p.op("sync", lambda e, mq=mq, k=k, wb=wb: e.dma_start_transpose(
                    UPW[wb][:, 512 * k:512 * (k + 1)],
                    dq["wup_bf"][512 * mq:512 * (mq + 1),
                                 128 * k:128 * (k + 1)]),
                 waits=(dq["done"] if mq < 2 and k == 0 else []) +
                       (_as_waits(upw_free[wb]) if k == 0 else []),
                 incs=(wl := wc[wb].inc(16)), tag=f"f{ci}upw{mq}_{k}")
        wl_done = wl.done
        for mm in range(4):
            m = 4 * mq + mm
            pb = m % 2
            banks = ([kb.PSS[c][:, 0:512] for c in range(4)] if pb == 0 else
                     [kb.PSB[:, 512 * c:512 * (c + 1)] for c in range(4)])
            for k in range(ND):
                for c in range(4):
                    w = []
                    if k == 0 and c == 0:
                        w = [wl_done] + _as_waits(bank_free[pb])
                        if mq == 0 and mm == 0:
                            w += [ln["fm_ready"][ND - 1], tup_done]
                    p.op("tensor", lambda e, k=k, c=c, m=m, wb=wb, banks=banks: e.matmul(
                            banks[c],
                            UPW[wb][:, 512 * k + 128 * (m % 4):512 * k + 128 * (m % 4) + 128],
                            kb.xfm(k)[:, 512 * c:512 * (c + 1)],
                            start=(k == 0), stop=False),
                         waits=w, tag=f"f{ci}up{m}")
            for c in range(4):
                p.op("tensor", lambda e, c=c, m=m, banks=banks, LUB=LUB: e.matmul(
                        banks[c], LUB[:, 128 * m:128 * (m + 1)],
                        kb.TUP[:, 512 * c:512 * (c + 1)],
                        start=False, stop=True),
                     waits=[lub_ld.done] if (c == 0 and m == 0) else None,
                     incs=(ul := mmc.inc()) if c == 3 else None,
                     tag=f"f{ci}uplora{m}")
            gfb = m % 2
            for c in range(4):
                p.op("scalar", lambda e, c=c, m=m, gfb=gfb, banks=banks: e.activation(
                        GF[gfb][:, 512 * c:512 * (c + 1)], banks[c],
                        AF.Relu, bias=kb.BUP[:, m:m + 1], scale=1.0),
                     waits=([ul.done] + _as_waits(gf_free[gfb])) if c == 0 else None,
                     incs=(ge := evc.inc()), tag=f"f{ci}gev{m}")
            bank_free[pb] = [ge.done]
            p.op("sync", lambda e, m=m, gfb=gfb: e.dma_start(
                    out=g_nat[128 * m:128 * (m + 1), :], in_=GF[gfb]),
                 waits=[ge.done], incs=(gs := gc[gfb].inc(16)), tag=f"f{ci}gst{m}")
            gf_free[gfb] = [gs.done]

        upw_free[wb] = [_Wait(mmc, mmc.count)]
    up_all_mm = _Wait(mmc, mmc.count)
    g_done_all = [gc[0].at(), gc[1].at()]

    td_mm_done = [None, None]
    for m in range(NH):
        b = m % 2
        p.op("sync", lambda e, m=m, b=b: e.dma_start(
                out=kb.WB[b][:], in_=g_nat[128 * m:128 * (m + 1), :]),
             waits=(g_done_all + _as_waits(bank_free[1]) if m == 0 else
                    _as_waits(td_mm_done[b])),
             incs=(gl := tdc[b].inc(16)), tag=f"f{ci}tdl{m}")
        for c in range(4):
            p.op("tensor", lambda e, m=m, b=b, c=c: e.matmul(
                    kb.PSB[0:16, 512 * c:512 * (c + 1)],
                    kb.DLA[:, 16 * m:16 * (m + 1)],
                    kb.WB[b][:, 512 * c:512 * (c + 1)],
                    start=(m == 0), stop=(m == NH - 1)),
                 waits=[gl.done, up_all_mm] if c == 0 and m == 0 else
                       ([gl.done] if c == 0 else None),
                 incs=(tdm := mmc.inc()) if c == 3 else None,
                 tag=f"f{ci}tdmm")
        td_mm_done[b] = [tdm.done]
    for c in range(4):
        p.op("scalar", lambda e, c=c: e.activation(
                kb.TDN[:, 512 * c:512 * (c + 1)],
                kb.PSB[0:16, 512 * c:512 * (c + 1)], AF.Copy),
             waits=[tdm.done] if c == 0 else None,
             incs=(tdn_ev := evc.inc()), tag=f"f{ci}tdnev")
    tdn_done = tdn_ev.done

    WRESQ = kb.carve(0, 64 * 512, BF16)
    GSB = [kb.carve(65536 + 16384 * i, 8192, BF16) for i in range(2)]
    ROW1 = kb.carve(98304, 2048, BF16)[0:1, :]
    rowc = p.chan("f_row")
    p.op("sync", lambda e: e.dma_start(out=ROW1, in_=kb.scr["bdn_row_bf"]),
         waits=[kb._flags["bdn_row_bf"]] + g_done_all,
         incs=(row_ld := rowc.inc(16)), tag=f"f{ci}rowld")
    banks8 = [kb.PSS[i][:, 0:512] for i in range(4)] + \
             [kb.PSB[:, 512 * i:512 * (i + 1)] for i in range(4)]
    bank8_free = [bank_free[0]] * 4 + [[tdn_done]] * 4
    gsb_free = [None, None]
    xt_free = [None, None]
    wres_free = None
    for qtr in range(4):
        for k in range(NH):
            p.op("sync", lambda e, qtr=qtr, k=k: e.dma_start_transpose(
                    WRESQ[:, 512 * k:512 * (k + 1)],
                    dq["wdn_bf"][512 * qtr:512 * (qtr + 1),
                                 128 * k:128 * (k + 1)]),
                 waits=([up_all_mm, tdn_done] + _as_waits(wres_free))
                       if k == 0 else None,
                 incs=(wr := wrc.inc(16)), tag=f"f{ci}dw{qtr}_{k}")
        wres_ready = wr.done
        for j in range(NT):
            gb = j % 2
            p.op("gpsimd", lambda e, j=j, gb=gb: e.dma_start(
                    out=GSB[gb].rearrange("p (k t) -> p k t", t=128),
                    in_=g_nat[:, 128 * j:128 * (j + 1)].rearrange(
                        "(k p) t -> p k t", p=128)),
                 waits=(g_done_all if (qtr == 0 and j < 2) else []) +
                       _as_waits(gsb_free[gb]),
                 incs=(gl2 := gl_c[gb].inc(16)), tag=f"f{ci}gsl{qtr}_{j}")
            bk = (qtr * NT + j) % 8
            for k in range(NH):
                w = []
                if k == 0:
                    w = [gl2.done, wres_ready] + _as_waits(bank8_free[bk])
                p.op("tensor", lambda e, k=k, gb=gb, bk=bk: e.matmul(
                        banks8[bk], GSB[gb][:, 128 * k:128 * (k + 1)],
                        WRESQ[:, 512 * k:512 * (k + 1)],
                        start=(k == 0), stop=False),
                     waits=w, tag=f"f{ci}dn{qtr}_{j}")
            p.op("tensor", lambda e, j=j, qtr=qtr, bk=bk: e.matmul(
                    banks8[bk], kb.TDN[:, 128 * j:128 * (j + 1)],
                    kb.DLB[:, 512 * qtr:512 * (qtr + 1)],
                    start=False, stop=False),
                 waits=[tdn_done] if (qtr == 0 and j == 0) else None,
                 tag=f"f{ci}dnlora")
            p.op("tensor", lambda e, qtr=qtr, bk=bk, ROW1=ROW1: e.matmul(
                    banks8[bk], kb.onesr[:],
                    ROW1[:, 512 * qtr:512 * (qtr + 1)],
                    start=False, stop=True),
                 waits=[row_ld.done] if (qtr == 0 and j == 0) else None,
                 incs=(dmm := mmc.inc()), tag=f"f{ci}dnbias")
            gsb_free[gb] = [_Wait(mmc, mmc.count)]
            xb = j % 2
            p.op("sync", lambda e, j=j, qtr=qtr, xb=xb: e.dma_start(
                    out=kb.XT[xb][:, 0:512],
                    in_=x_in[128 * j:128 * (j + 1),
                             512 * qtr:512 * (qtr + 1)]),
                 waits=_as_waits(xt_free[xb]), incs=(xl := dnc[xb].inc(16)),
                 tag=f"f{ci}xl{qtr}_{j}")
            p.op("vector", lambda e, xb=xb, bk=bk: e.scalar_tensor_tensor(
                    kb.XT[xb][:, 512:1024], banks8[bk], 0.5,
                    kb.XT[xb][:, 0:512], ALU.mult, ALU.add),
                 waits=[dmm.done, xl.done], incs=(add := dnv.inc()),
                 tag=f"f{ci}add{qtr}_{j}")
            bank8_free[bk] = [add.done]
            p.op("sync", lambda e, j=j, qtr=qtr, xb=xb: e.dma_start(
                    out=x_out[128 * j:128 * (j + 1),
                              512 * qtr:512 * (qtr + 1)],
                    in_=kb.XT[xb][:, 512:1024]),
                 waits=[add.done], incs=(xo := xoc.inc(16)),
                 tag=f"f{ci}xo{qtr}_{j}")
            xt_free[xb] = [add.done]
        wres_free = [_Wait(mmc, mmc.count)]
    return dict(done=xoc.at(), xoc=xoc)


def emit_mamba(kb, x_in, x_out, start=None):
    p, nc = kb.p, kb.nc
    kb.act_set(SET_NL_EXP)

    if kb.once("mamba_setup"):
        sld = p.chan("f_sld")
        sldv = p.chan("g_dv")
        slds = p.chan("g_ev")
        wx3_bf, _ = kb.cast_weight("wx3T", "mb")
        wdt_bf, _ = kb.cast_weight("wdtT", "mb")
        w1 = w2 = kb.cast_group_done("mb")
        kb.MWX = nc.alloc_sbuf_tensor("MWX", [128, 16 * 96], BF16)
        kb.MWDT = nc.alloc_sbuf_tensor("MWDT", [64, 2048], BF16)
        kb.MBDT = nc.alloc_sbuf_tensor("MBDT", [128, 16], F32)
        kb.MA = nc.alloc_sbuf_tensor("MA", [128, 256], F32)
        kb.MD = nc.alloc_sbuf_tensor("MD", [128, 16], F32)
        kb.SEL = nc.alloc_sbuf_tensor("SEL", [16, 2048], BF16)
        p.op("sync", lambda e: e.dma_start(
                out=kb.MWX[:].rearrange("p (k m) -> p k m", m=96),
                in_=wx3_bf.rearrange("(k p) m -> p k m", p=128)),
             waits=[w1], incs=sld.inc(16))
        p.op("sync", lambda e: e.dma_start(out=kb.MWDT[:], in_=wdt_bf),
             waits=[w2], incs=sld.inc(16))
        p.op("sync", lambda e: e.dma_start(out=kb.MBDT[:], in_=kb.din["bdt_pd"]),
             incs=sld.inc(16))
        p.op("sync", lambda e: e.dma_start(
                out=kb.MA[:].rearrange("p (k n) -> p k n", n=16),
                in_=kb.din["alog"].rearrange("(k p) n -> p k n", p=128)),
             incs=(la := sld.inc(16)))
        p.op("sync", lambda e: e.dma_start(out=kb.MD[:], in_=kb.din["d_pd"]),
             incs=sld.inc(16))
        p.op("sync", lambda e: e.dma_start(out=kb.SEL[:], in_=kb.din["sel16_bf"]),
             incs=sld.inc(16))
        p.op("scalar", lambda e: e.activation(kb.MA[:], kb.MA[:], AF.Exp),
             waits=[sld.at()], incs=(ea := slds.inc()))
        p.op("vector", lambda e: e.tensor_scalar_mul(kb.MA[:], kb.MA[:], -1.0),
             waits=[ea.done], incs=(eneg := sldv.inc()))
        kb._flags["mb_small"] = [sld.at(), sldv.at()]
    small = kb._flags["mb_small"]

    xn_tm = kb.dram("mb_xn_tm", (S, D), BF16)
    y_fm = kb.dram("mb_y_fm", (D, S), BF16)
    ln = emit_ln(kb, x_in, xn_tm, "LN", start=start)

    DPB = kb.carve(65536, 2048, BF16)[0:64, :]
    BCT = kb.carve(69632, 2048, BF16)[0:16, :]
    CCT = kb.carve(73728, 2048, BF16)[0:16, :]
    DELTA = kb.carve(77824, 2048, F32)
    DU = kb.carve(86016, 2048, F32)
    AEXP = kb.carve(94208, 2048, F32)
    UBUF = kb.carve(102400, 2048, F32)
    HBUF = kb.carve(110592, 2048, BF16)
    YM = kb.carve(114688, 2048, BF16)
    YACC = kb.carve(118784, 2048, F32)
    YTB = kb.carve(126976, 2048, BF16)

    mm = p.chan("g_mm")
    ev = p.chan("g_ev")
    dv = p.chan("g_dv")
    yst = p.chan("f_g0")
    xoc = p.chan("f_xo")

    banksets = [(kb.PSS[0][0:64, :], kb.PSS[1][0:16, :], kb.PSS[2][0:16, :]),
                (kb.PSS[3][0:64, :], kb.PSB[0:16, 0:512], kb.PSB[0:16, 512:1024])]
    bs_free = [None, None]
    for c in range(4):
        bs = banksets[c % 2]
        for k in range(ND):
            w = []
            if k == 0:
                w = [ln["fm_ready"][k]] + small + _as_waits(bs_free[c % 2])
            else:
                w = [ln["fm_ready"][k]] if c == 0 else []
            for j, (lo, hi) in enumerate([(0, 64), (64, 80), (80, 96)]):
                p.op("tensor", lambda e, k=k, c=c, j=j, lo=lo, hi=hi, bs=bs: e.matmul(
                        bs[j], kb.MWX[:, 96 * k + lo:96 * k + hi],
                        kb.xfm(k)[:, 512 * c:512 * (c + 1)],
                        start=(k == 0), stop=(k == ND - 1)),
                     waits=w if j == 0 else None,
                     incs=(xmm := mm.inc()) if (k == ND - 1 and j == 2) else None,
                     tag=f"mb_xp{c}")
        for j, dst in enumerate([DPB, BCT, CCT]):
            p.op("scalar", lambda e, j=j, c=c, dst=dst, bs=bs: e.activation(
                    dst[:, 512 * c:512 * (c + 1)], bs[j], AF.Identity),
                 waits=[xmm.done] if j == 0 else None,
                 incs=(xev := ev.inc()), tag=f"mb_xpev{c}")
        bs_free[c % 2] = [xev.done]
    xp_done = xev.done

    delta_mm_free = _Wait(ev, ev.count)
    ymul_prev = None
    umul_prev = None
    scan_prev = None
    tmp0_free = None
    delta_free = None
    ytb_free = None
    kb._mb_wb_free = [None, None]
    kb._mb_xt_free = [None, None]
    for dt in range(ND):
        for c in range(4):
            p.op("tensor", lambda e, dt=dt, c=c: e.matmul(
                    kb.PSB[:, 512 * c:512 * (c + 1)],
                    kb.MWDT[:, 128 * dt:128 * (dt + 1)],
                    DPB[:, 512 * c:512 * (c + 1)],
                    start=True, stop=True),
                 waits=([xp_done, delta_mm_free] if c == 0 else None),
                 incs=(dmm := mm.inc()) if c == 3 else None, tag=f"mb_dl{dt}")
        p.op("scalar", lambda e, dt=dt: e.activation(
                kb.TMP[0][:], kb.PSB[:], AF.Identity,
                bias=kb.MBDT[:, dt:dt + 1], scale=1.0),
             waits=[dmm.done] + _as_waits(scan_prev) + _as_waits(tmp0_free),
             incs=(p0 := ev.inc()), tag=f"mb_sp0_{dt}")
        delta_mm_free = p0.done
        p.op("scalar", lambda e: e.activation(AEXP, kb.TMP[0][:], AF.Abs),
             incs=(p1 := ev.inc()))
        p.op("scalar", lambda e: e.activation(AEXP, AEXP, AF.Exp, scale=-1.0),
             incs=(p2 := ev.inc()))
        p.op("scalar", lambda e: e.activation(AEXP, AEXP, AF.Ln, bias=1.0),
             incs=(p3 := ev.inc()))
        p.op("vector", lambda e: e.tensor_scalar_max(UBUF, kb.TMP[0][:], 0.0),
             waits=[p0.done] + _as_waits(scan_prev), incs=(p4 := dv.inc()))
        tmp0_free = [p4.done]
        p.op("vector", lambda e: e.tensor_add(DELTA, UBUF, AEXP),
             waits=[p3.done, p4.done] + _as_waits(delta_free),
             incs=(p5 := dv.inc()), tag=f"mb_sp5_{dt}")
        p.op("vector", lambda e, dt=dt: e.tensor_mul(DU, DELTA, kb.xfm(dt)),
             waits=[p5.done], incs=(pdu := dv.inc()), tag=f"mb_du{dt}")
        p.op("vector", lambda e, dt=dt: e.tensor_scalar_mul(
                YACC, kb.xfm(dt), kb.MD[:, dt:dt + 1]),
             waits=[pdu.done], incs=(py0 := dv.inc()), tag=f"mb_xd{dt}")
        yadd_prev = py0
        for n in range(16):
            for c in range(4):
                p.op("tensor", lambda e, n=n, c=c: e.matmul(
                        kb.PSB[:, 512 * c:512 * (c + 1)],
                        kb.SEL[:, 128 * n:128 * (n + 1)],
                        BCT[:, 512 * c:512 * (c + 1)],
                        start=True, stop=True),
                     waits=([delta_mm_free] + _as_waits(umul_prev))
                           if c == 0 else None,
                     incs=(bmm := mm.inc()) if c == 3 else None,
                     tag=f"mb_bb{dt}_{n}")
            for c in range(4):
                p.op("tensor", lambda e, n=n, c=c: e.matmul(
                        kb.PSS[c][:], kb.SEL[:, 128 * n:128 * (n + 1)],
                        CCT[:, 512 * c:512 * (c + 1)],
                        start=True, stop=True),
                     waits=(_as_waits(ymul_prev) + ([xp_done] if dt == 0 and n == 0 else []))
                           if c == 0 else None,
                     incs=(cmm := mm.inc()) if c == 3 else None,
                     tag=f"mb_cb{dt}_{n}")
            p.op("scalar", lambda e, dt=dt, n=n: e.activation(
                    AEXP, DELTA, AF.Exp, scale=kb.MA[:, 16 * dt + n:16 * dt + n + 1]),
                 waits=[p5.done] if n == 0 else _as_waits(scan_prev),
                 incs=(pa := ev.inc()), tag=f"mb_a{dt}_{n}")
            if n == 15:
                delta_free = [pa.done]
            p.op("vector", lambda e: e.tensor_mul(UBUF, DU, kb.PSB[:]),
                 waits=[bmm.done, pdu.done] + _as_waits(scan_prev),
                 incs=(pu := dv.inc()), tag=f"mb_u{dt}_{n}")
            umul_prev = [pu.done]
            delta_mm_free = pu.done
            p.op("vector", lambda e: e.tensor_tensor_scan(
                    HBUF, AEXP, UBUF, 0.0, ALU.mult, ALU.add),
                 waits=[pa.done] + _as_waits(ymul_prev),
                 incs=(ph := dv.inc()), tag=f"mb_h{dt}_{n}")
            scan_prev = [ph.done]
            for c in range(4):
                p.op("vector", lambda e, c=c: e.tensor_mul(
                        YM[:, 512 * c:512 * (c + 1)], HBUF[:, 512 * c:512 * (c + 1)],
                        kb.PSS[c][:]),
                     waits=[ph.done, cmm.done] if c == 0 else None,
                     incs=(pm := dv.inc()), tag=f"mb_ym{dt}_{n}")
            ymul_prev = [pm.done]
            p.op("vector", lambda e: e.tensor_add(YACC, YACC, YM),
                 waits=[pm.done, yadd_prev.done], incs=(pya := dv.inc()),
                 tag=f"mb_ya{dt}_{n}")
            yadd_prev = pya
        p.op("vector", lambda e: e.tensor_copy(YTB, YACC),
             waits=[pya.done] + _as_waits(ytb_free),
             incs=(pyt := dv.inc()), tag=f"mb_yt{dt}")
        p.op("sync", lambda e, dt=dt: e.dma_start(
                out=y_fm[128 * dt:128 * (dt + 1), :], in_=YTB),
             waits=[pyt.done], incs=(ys := yst.inc(16)), tag=f"mb_yst{dt}")
        ytb_free = [ys.done]
    y_all = yst.at()

    trl = [p.chan("LN_tr"), p.chan("f_td1")]
    xlc = [p.chan("f_dn0"), p.chan("f_dn1")]
    add_prev = None
    for j in range(NT):
        wbuf = kb.WB[j % 2]
        p.op("sync", lambda e, j=j, wbuf=wbuf: e.dma_start_transpose(
                wbuf[:], y_fm[:, 128 * j:128 * (j + 1)]),
             waits=[y_all] if j < 2 else _as_waits(kb._mb_wb_free[j % 2]),
             incs=(ty := trl[j % 2].inc(16)), tag=f"mb_ytr{j}")
        p.op("sync", lambda e, j=j: e.dma_start(
                out=kb.XT[j % 2][:], in_=x_in[128 * j:128 * (j + 1), :]),
             waits=_as_waits(kb._mb_xt_free[j % 2]) if j >= 2 else [],
             incs=(xl := xlc[j % 2].inc(16)), tag=f"mb_xl{j}")
        p.op("vector", lambda e, j=j, wbuf=wbuf: e.tensor_add(
                kb.TMP[0][:], kb.XT[j % 2][:], wbuf[:]),
             waits=[ty.done, xl.done] + _as_waits(add_prev),
             incs=(pad := dv.inc()), tag=f"mb_add{j}")
        p.op("sync", lambda e, j=j: e.dma_start(
                out=x_out[128 * j:128 * (j + 1), :], in_=kb.TMP[0][:]),
             waits=[pad.done], incs=(xo := xoc.inc(16)), tag=f"mb_xo{j}")
        add_prev = [xo.done]
        kb._mb_wb_free[j % 2] = [pad.done]
        kb._mb_xt_free[j % 2] = [pad.done]
    return dict(done=xoc.at())


MLA_SCALE = float((128 + 64) ** -0.5)


def emit_mla(kb, x_in, x_out, start=None, stages=("proj", "heads", "wo"),
             nhead=16):
    p, nc = kb.p, kb.nc
    kb.act_set(SET_NL_EXP)

    if kb.once("mla_setup"):
        sld = p.chan("f_sld")
        for w in ["wqTr", "wkvaTx", "wkvbn", "wkvbvT", "woT"]:
            kb.cast_weight(w, "ml")
        kb.cast_group_done("ml")
        kb.KVW = nc.alloc_sbuf_tensor("KVW", [128, 4], F32)
        kb.PSW = nc.alloc_sbuf_tensor("PSW", [128, 128], BF16)
        mskr = kb.ensure_msk4()
        p.op("sync", lambda e: e.dma_start(out=kb.KVW[:], in_=kb.din["kvw_pd"]),
             incs=sld.inc(16))
        p.op("sync", lambda e: e.dma_start(out=kb.PSW[:], in_=kb.din["pswap_bf"]),
             incs=sld.inc(16))
        kb._flags["ml_small"] = [sld.at(), mskr]
    small = kb._flags["ml_small"]

    xn_tm = kb.dram("ml_xn_tm", (S, D), BF16)
    qn_hbm = kb.dram("ml_qn", (2048, S), BF16)
    qp_hbm = kb.dram("ml_qp", (1024, S), BF16)
    kvc_hbm = kb.dram("ml_kvc", (512, S), BF16)
    ov_hbm = kb.dram("ml_ov", (D, S), BF16)
    wq_bf = kb.scr["wqTr_bf"]
    wkva_bf = kb.scr["wkvaTx_bf"]
    wkvbn_bf = kb.scr["wkvbn_bf"]
    wkvbv_bf = kb.scr["wkvbvT_bf"]
    wo_bf = kb.scr["woT_bf"]
    cwaits = [kb._flags[k + "_bf"] for k in
              ["wqTr", "wkvaTx", "wkvbn", "wkvbvT", "woT"]]

    ln = emit_ln(kb, x_in, xn_tm, "LN", start=start)

    mm = p.chan("g_mm")
    ev = p.chan("g_ev")
    dv = p.chan("g_dv")
    wl = [p.chan("ml_wl0"), p.chan("ml_wl1")]
    rt = p.chan("f_lub")
    kvtc = p.chan("ml_kvt")
    qst = p.chan("ml_qst")
    xoc = p.chan("f_xo")

    ROPC = kb.carve(65536, 2048, BF16)
    ROPS = kb.carve(69632, 2048, BF16)
    WQB = [kb.carve(73728 + 4096 * i, 2048, BF16) for i in range(2)]
    QNB = kb.carve(81920, 2048, BF16)
    RPB = kb.carve(86016, 2048, BF16)
    RT2 = kb.carve(94208, 2048, BF16)
    QPB = kb.carve(90112, 2048, BF16)
    SQB = kb.carve(98304, 2048, BF16)
    KVL = [kb.carve(102400 + 4096 * i, 2048, BF16) for i in range(4)]
    KPE = kb.carve(118784, 2048, BF16)
    RSR = kb.carve(122880, 2048, F32)[0:1, :]
    p.op("sync", lambda e: e.dma_start(out=ROPC, in_=kb.din["ropecos_bf"]),
         incs=(rc := rt.inc(16)), tag="ml_ropec")
    p.op("sync", lambda e: e.dma_start(out=ROPS, in_=kb.din["ropesin_bf"]),
         incs=(rs := rt.inc(16)), tag="ml_ropes")
    rope_ready = [rt.at()]

    def rope_pair(src_bf, swap_psum_set, out_bf, waits_src, tagp):
        toks = []
        for c in range(4):
            p.op("tensor", lambda e, c=c: e.matmul(
                    swap_psum_set[c], kb.PSW[:],
                    src_bf[:, 512 * c:512 * (c + 1)], start=True, stop=True),
                 waits=(waits_src + small) if c == 0 else None,
                 incs=(sw := mm.inc()) if c == 3 else None, tag=tagp + "sw")
        for c in range(4):
            p.op("vector", lambda e, c=c: e.tensor_mul(
                    out_bf[:, 512 * c:512 * (c + 1)],
                    src_bf[:, 512 * c:512 * (c + 1)],
                    ROPC[:, 512 * c:512 * (c + 1)]),
                 waits=([sw.done] + rope_ready) if c == 0 else None,
                 incs=dv.inc(), tag=tagp + "c")
            p.op("vector", lambda e, c=c: e.tensor_mul(
                    RT2[:, 512 * c:512 * (c + 1)], swap_psum_set[c],
                    ROPS[:, 512 * c:512 * (c + 1)]),
                 incs=dv.inc(), tag=tagp + "s")
            p.op("vector", lambda e, c=c: e.tensor_add(
                    out_bf[:, 512 * c:512 * (c + 1)],
                    out_bf[:, 512 * c:512 * (c + 1)],
                    RT2[:, 512 * c:512 * (c + 1)]),
                 incs=(tk := dv.inc()), tag=tagp + "a")
            toks.append(tk)
        return toks[-1]

    wqb_free = [None, None]
    rope_busy = None
    qnb_free = None
    rpb_free = None
    pss_free = None
    for m in range(24):
        b = m % 2
        p.op("sync", lambda e, m=m, b=b: e.dma_start(
                out=WQB[b].rearrange("p (k m) -> p k m", m=128),
                in_=wq_bf[:, 128 * m:128 * (m + 1)].rearrange(
                    "(k p) m -> p k m", p=128)),
             waits=[cwaits[0]] + _as_waits(wqb_free[b]) if m < 2 else
                   _as_waits(wqb_free[b]),
             incs=(qw := wl[b].inc(16)), tag=f"ml_qw{m}")
        for k in range(ND):
            for c in range(4):
                w = None
                if k == 0 and c == 0:
                    w = [qw.done] + _as_waits(pss_free)
                    if m == 0:
                        w += [ln["fm_ready"][ND - 1]]
                elif c == 0 and m == 0:
                    w = [ln["fm_ready"][k]]
                p.op("tensor", lambda e, k=k, c=c, b=b: e.matmul(
                        kb.PSS[c][:], WQB[b][:, 128 * k:128 * (k + 1)],
                        kb.xfm(k)[:, 512 * c:512 * (c + 1)],
                        start=(k == 0), stop=(k == ND - 1)),
                     waits=w,
                     incs=(qm := mm.inc()) if (k == ND - 1 and c == 3) else None,
                     tag=f"ml_q{m}")
        wqb_free[b] = [qm.done]
        if m < 16:
            for c in range(4):
                p.op("scalar", lambda e, c=c: e.activation(
                        QNB[:, 512 * c:512 * (c + 1)], kb.PSS[c][:], AF.Identity),
                     waits=([qm.done] + _as_waits(qnb_free)) if c == 0 else None,
                     incs=(qe := ev.inc()), tag=f"ml_qev{m}")
            pss_free = [qe.done]
            p.op("sync", lambda e, m=m: e.dma_start(
                    out=qn_hbm[128 * m:128 * (m + 1), :], in_=QNB),
                 waits=[qe.done], incs=(qs := qst.inc(16)), tag=f"ml_qnst{m}")
            qnb_free = [qs.done]
        else:
            for c in range(4):
                p.op("scalar", lambda e, c=c: e.activation(
                        RPB[:, 512 * c:512 * (c + 1)], kb.PSS[c][:], AF.Identity),
                     waits=([qm.done] + _as_waits(rpb_free)) if c == 0 else None,
                     incs=(qe := ev.inc()), tag=f"ml_qpev{m}")
            tk = rope_pair(RPB, [kb.PSS[c][:] for c in range(4)], QPB,
                           [qe.done] + _as_waits(rope_busy), f"ml_rp{m}")
            rope_busy = [tk.done]
            pss_free = [tk.done]
            rpb_free = [tk.done]
            pr = m - 16
            p.op("sync", lambda e, pr=pr: e.dma_start(
                    out=qp_hbm[128 * pr:128 * (pr + 1), :], in_=QPB),
                 waits=[tk.done], incs=(qs := qst.inc(16)), tag=f"ml_qpst{pr}")
            rope_busy = [qs.done]
    q_all = qst.at()

    sq_mm = None
    for m in range(5):
        b = m % 2
        p.op("sync", lambda e, m=m, b=b: e.dma_start(
                out=WQB[b].rearrange("p (k m) -> p k m", m=128),
                in_=wkva_bf[:, 128 * m:128 * (m + 1)].rearrange(
                    "(k p) m -> p k m", p=128)),
             waits=[cwaits[1]] + _as_waits(wqb_free[b]),
             incs=(kw := wl[b].inc(16)), tag=f"ml_kw{m}")
        for k in range(ND):
            for c in range(4):
                p.op("tensor", lambda e, k=k, c=c, b=b: e.matmul(
                        kb.PSS[c][:], WQB[b][:, 128 * k:128 * (k + 1)],
                        kb.xfm(k)[:, 512 * c:512 * (c + 1)],
                        start=(k == 0), stop=(k == ND - 1)),
                     waits=([kw.done] + _as_waits(pss_free))
                           if (k == 0 and c == 0) else None,
                     incs=(km := mm.inc()) if (k == ND - 1 and c == 3) else None,
                     tag=f"ml_kv{m}")
        wqb_free[b] = [km.done]
        if m < 4:
            for c in range(4):
                p.op("scalar", lambda e, c=c: e.activation(
                        kb.TMP[0][:, 512 * c:512 * (c + 1)], kb.PSS[c][:],
                        AF.Identity),
                     waits=([km.done] + (_as_waits([kc.done]) if m > 0 else []))
                           if c == 0 else None,
                     incs=(ke := ev.inc()), tag=f"ml_kvev{m}")
            p.op("vector", lambda e, m=m: e.tensor_copy(KVL[m], kb.TMP[0][:]),
                 waits=[ke.done], incs=(kc := dv.inc()), tag=f"ml_kvcp{m}")
            p.op("scalar", lambda e: e.activation(SQB, kb.TMP[0][:], AF.Square),
                 waits=([ke.done] + ([sq_mm.done] if m > 0 else [])),
                 incs=(ksq := ev.inc()), tag=f"ml_sq{m}")
            pss_free = [ke.done]
            for c in range(4):
                p.op("tensor", lambda e, c=c, m=m: e.matmul(
                        kb.PSB[0:1, 512 * c:512 * (c + 1)], kb.onesc[:],
                        SQB[:, 512 * c:512 * (c + 1)],
                        start=(m == 0), stop=(m == 3)),
                     waits=[ksq.done] if c == 0 else None,
                     incs=(sq_mm := mm.inc()) if c == 3 else None,
                     tag=f"ml_sqmm{m}")
        else:
            for c in range(4):
                p.op("scalar", lambda e, c=c: e.activation(
                        RPB[:, 512 * c:512 * (c + 1)], kb.PSS[c][:], AF.Identity),
                     waits=([km.done] + _as_waits(rpb_free)) if c == 0 else None,
                     incs=(ke := ev.inc()), tag=f"ml_kpev")
            tk = rope_pair(RPB, [kb.PSS[c][:] for c in range(4)], KPE,
                           [ke.done] + _as_waits(rope_busy), "ml_kpe")
            pss_free = [tk.done]
    kpe_done = tk.done

    p.op("vector", lambda e: e.tensor_scalar(
            RSR, kb.PSB[0:1, 0:2048], 1.0 / 512, EPS_RMS, ALU.mult, ALU.add),
         waits=[sq_mm.done], incs=(r1 := dv.inc()), tag="ml_rs1")
    p.op("scalar", lambda e: e.activation(RSR, RSR, AF.Ln),
         waits=[r1.done], incs=(r2 := ev.inc()))
    p.op("scalar", lambda e: e.activation(RSR, RSR, AF.Exp, scale=-0.5),
         incs=(r3 := ev.inc()))
    p.op("vector", lambda e: e.tensor_copy(kb.ROW2[:], RSR),
         waits=[r3.done], incs=(r4 := dv.inc()), tag="ml_rs4")
    for c in range(4):
        p.op("tensor", lambda e, c=c: e.matmul(
                kb.PSS[c][:], kb.onesr[:], kb.ROW2[0:1, 512 * c:512 * (c + 1)],
                start=True, stop=True),
             waits=([r4.done] + _as_waits(pss_free)) if c == 0 else None,
             incs=(rb := mm.inc()) if c == 3 else None, tag="ml_rbc")
    for ct in range(4):
        for c in range(4):
            p.op("vector", lambda e, ct=ct, c=c: e.scalar_tensor_tensor(
                    KVL[ct][:, 512 * c:512 * (c + 1)],
                    KVL[ct][:, 512 * c:512 * (c + 1)],
                    kb.KVW[:, ct:ct + 1], kb.PSS[c][:], ALU.mult, ALU.mult),
                 waits=[rb.done, small[0]] if (ct == 0 and c == 0) else None,
                 incs=(kvn := dv.inc()), tag=f"ml_kvn{ct}")
    kvc_done = kvn.done
    pss_free = [kvn.done]
    for ct in range(4):
        p.op("sync", lambda e, ct=ct: e.dma_start(
                out=kvc_hbm[128 * ct:128 * (ct + 1), :], in_=KVL[ct]),
             waits=[kvc_done], incs=(ks2 := qst.inc(16)), tag=f"ml_kvst{ct}")
    kv_st = ks2.done
    if "heads" not in stages:
        return dict(done=_Wait(qst, qst.count))

    KVT = [kb.carve(1024 * j, 512, BF16) for j in range(16)]
    ATN = [kb.carve(16384 + 1024 * j, 512, BF16) for j in range(16)]
    PRB = [kb.carve(32768 + 4096 * i, 2048, BF16) for i in range(4)]
    QLT = [kb.carve(49152 + 4096 * i, 2048, BF16) for i in range(4)]
    QNH = [kb.carve(65536 + 4096 * i, 2048, BF16) for i in range(2)]
    QPH = [kb.carve(73728 + 4096 * i, 2048, BF16) for i in range(2)]
    OLB = kb.carve(81920, 2048, BF16)
    OVH = kb.carve(86016, 2048, BF16)
    WKB = [kb.carve(90112 + 1024 * i, 512, BF16) for i in range(2)]
    WVB = [kb.carve(92160 + 1024 * i, 512, BF16) for i in range(2)]

    hlc = {k: [p.chan(f"ml_h{k}0"), p.chan(f"ml_h{k}1")]
           for k in ("qn", "wk", "wv", "qp")}
    for j in range(16):
        p.op("sync", lambda e, j=j: e.dma_start_transpose(
                KVT[j], kvc_hbm[:, 128 * j:128 * (j + 1)]),
             waits=[kv_st, q_all] if j == 0 else None,
             incs=(kt := kvtc.inc(16)), tag=f"ml_kvt{j}")
    kvt_ready = kt.done

    hload_free = {"qn": [None, None], "qp": [None, None],
                  "wk": [None, None], "wv": [None, None]}
    sc = p.chan("g_dv")
    atn_evac = None
    olb_free = None
    ovh_free = None
    ctx_free = None
    prb_free = None
    psb_scores_free = None
    for h in range(nhead):
        b = h % 2
        p.op("sync", lambda e, h=h, b=b: e.dma_start(
                out=QNH[b], in_=qn_hbm[128 * h:128 * (h + 1), :]),
             waits=([q_all, kpe_done] if h < 2 else []) +
                   _as_waits(hload_free["qn"][b]),
             incs=(lqn := hlc["qn"][b].inc(16)), tag=f"ml_hqn{h}")
        p.op("sync", lambda e, h=h, b=b: e.dma_start(
                out=WKB[b], in_=wkvbn_bf[128 * h:128 * (h + 1), :]),
             waits=([cwaits[2]] if h < 2 else []) + _as_waits(hload_free["wk"][b]),
             incs=(lwk := hlc["wk"][b].inc(16)), tag=f"ml_hwk{h}")
        p.op("sync", lambda e, h=h, b=b: e.dma_start(
                out=WVB[b].rearrange("p (ct m) -> p ct m", m=128),
                in_=wkvbv_bf[:, 128 * h:128 * (h + 1)].rearrange(
                    "(ct p) m -> p ct m", p=128)),
             waits=([cwaits[3]] if h < 2 else []) + _as_waits(hload_free["wv"][b]),
             incs=(lwv := hlc["wv"][b].inc(16)), tag=f"ml_hwv{h}")
        if h % 2 == 0:
            pr = h // 2
            p.op("sync", lambda e, pr=pr: e.dma_start(
                    out=QPH[pr % 2], in_=qp_hbm[128 * pr:128 * (pr + 1), :]),
                 waits=([q_all, kpe_done] if pr < 2 else []) +
                       _as_waits(hload_free["qp"][pr % 2]),
                 incs=(lqp := hlc["qp"][pr % 2].inc(16)), tag=f"ml_hqp{pr}")
            hload_qp = lqp.done
        qph_buf = QPH[(h // 2) % 2]
        qpe_lo = 64 * (h % 2)

        qle = None
        for ct in range(4):
            for c in range(4):
                if ct == 0 and c == 0:
                    w = ([lqn.done, lwk.done] + _as_waits(ctx_free) +
                         (_as_waits(pss_free) if h == 0 else []))
                elif c == 0:
                    w = [qle.done]
                else:
                    w = None
                p.op("tensor", lambda e, ct=ct, c=c, b=b: e.matmul(
                        kb.PSS[c][:], WKB[b][:, 128 * ct:128 * (ct + 1)],
                        QNH[b][:, 512 * c:512 * (c + 1)],
                        start=True, stop=True),
                     waits=w,
                     incs=(qlm := mm.inc()) if c == 3 else None,
                     tag=f"ml_ql{h}_{ct}")
            for c in range(4):
                p.op("scalar", lambda e, ct=ct, c=c: e.activation(
                        QLT[ct][:, 512 * c:512 * (c + 1)], kb.PSS[c][:],
                        AF.Identity),
                     waits=[qlm.done] if c == 0 else None,
                     incs=(qle := ev.inc()), tag=f"ml_qle{h}_{ct}")
        qlat_done = qle.done
        if h == 0 and "ml_dbg_qlat" in kb.dbg:
            dbg_ql = kb.dram("ml_dbg_qlat", (512, S), BF16)
            for ct in range(4):
                p.op("sync", lambda e, ct=ct: e.dma_start(
                        out=dbg_ql[128 * ct:128 * (ct + 1), :], in_=QLT[ct]),
                     waits=[qlat_done], incs=qst.inc(16), tag="dbg_ql")

        for qc in range(4):
            nkc = qc + 1
            nkt = 4 * (qc + 1)
            ex_prev = None
            for qil in range(4):
                qa = 4 * qc + qil
                for kc in range(nkc):
                    for ct in range(4):
                        w = None
                        if ct == 0 and kc == 0 and qil == 0:
                            w = [qlat_done, kvt_ready] + \
                                _as_waits(psb_scores_free)
                            if h == 0 and qc == 0:
                                w += [kpe_done]
                        elif ct == 0 and kc == 0:
                            w = _as_waits(ex_prev)
                        p.op("tensor", lambda e, qa=qa, kc=kc, ct=ct: e.matmul(
                                kb.PSB[:, 512 * kc:512 * (kc + 1)],
                                QLT[ct][:, 128 * qa:128 * (qa + 1)],
                                KVL[ct][:, 512 * kc:512 * (kc + 1)],
                                start=(ct == 0), stop=False),
                             waits=w, tag=f"ml_s{h}_{qa}")
                    p.op("tensor", lambda e, qa=qa, kc=kc, qph_buf=qph_buf,
                         qpe_lo=qpe_lo, nkc=nkc: e.matmul(
                                kb.PSB[:, 512 * kc:512 * (kc + 1)],
                                qph_buf[qpe_lo:qpe_lo + 64,
                                        128 * qa:128 * (qa + 1)],
                                KPE[qpe_lo:qpe_lo + 64,
                                    512 * kc:512 * (kc + 1)],
                                start=False, stop=(kc != nkc - 1)),
                         waits=[hload_qp] if (qc == 0 and qil == 0 and kc == 0)
                               else None,
                         tag=f"ml_sp{h}_{qa}")
                p.op("tensor", lambda e, qa=qa, qil=qil, qc=qc: e.matmul(
                        kb.PSB[:, 512 * qc:512 * (qc + 1)], kb.ident[:],
                        kb.MSK4[:, 512 * qil:512 * (qil + 1)],
                        start=False, stop=True),
                     incs=(smm := mm.inc()), tag=f"ml_sm{h}_{qa}")
                nm = kb.SM[:, 160 + qil:161 + qil]
                nms = kb.SM[:, 164 + qil:165 + qil]
                sums = kb.SM[:, 168 + qil:169 + qil]
                rec = kb.SM[:, 172 + qil:173 + qil]
                p.op("vector", lambda e, nm=nm, nkc=nkc: e.reduce_max(
                        nm, kb.PSB[:, 0:512 * nkc], AX.X, negate=True),
                     waits=[smm.done], incs=(mx := dv.inc()),
                     tag=f"ml_mx{h}_{qa}")
                p.op("vector", lambda e, nm=nm, nms=nms: e.tensor_scalar_mul(
                        nms, nm, MLA_SCALE),
                     waits=[mx.done], incs=(mxs := dv.inc()))
                p.op("scalar", lambda e, qil=qil, nms=nms, sums=sums, nkc=nkc:
                     e.activation(
                        PRB[qil][:, 0:512 * nkc], kb.PSB[:, 0:512 * nkc],
                        AF.Exp, bias=nms, scale=MLA_SCALE, accum_out=sums),
                     waits=[mxs.done] + (_as_waits(prb_free) if qil == 0 else []),
                     incs=(ex := ev.inc()), tag=f"ml_ex{h}_{qa}")
                ex_prev = [ex.done]
                p.op("vector", lambda e, sums=sums, rec=rec: e.reciprocal(
                        rec, sums),
                     waits=[ex.done], incs=(rc2 := dv.inc()))
                p.op("vector", lambda e, qil=qil, rec=rec: e.tensor_scalar_mul(
                        kb.SMB[:, 128 * qil:128 * (qil + 1)], kb.ident[:], rec),
                     waits=[rc2.done], incs=(dg := dv.inc()),
                     tag=f"ml_dg{h}_{qa}")
                if qil == 3:
                    psb_scores_free = [ex.done]
            diag_done = dg.done
            tr_ev = None
            for kt in range(nkt):
                ps = kb.PSS[kt % 2][:]
                for qil in range(4):
                    p.op("tensor", lambda e, kt=kt, qil=qil, ps=ps: e.matmul(
                            ps[:, 128 * qil:128 * (qil + 1)],
                            PRB[qil][:, 128 * kt:128 * (kt + 1)],
                            kb.SMB[:, 128 * qil:128 * (qil + 1)],
                            start=True, stop=True),
                         waits=([diag_done] + _as_waits(atn_evac)
                                + (_as_waits(pss_free) if (h == 0 and qc == 0
                                                           and kt < 2) else []))
                         if qil == 0 else None,
                         incs=(tmm := mm.inc()) if qil == 3 else None,
                         tag=f"ml_tr{h}_{kt}")
                p.op("scalar", lambda e, kt=kt, ps=ps: e.activation(
                        ATN[kt], ps, AF.Identity),
                     waits=[tmm.done], incs=(tr_ev := ev.inc()),
                     tag=f"ml_tre{h}_{kt}")
                if kt >= 1:
                    atn_evac = [tr_ev.done]
            atn_done = tr_ev.done
            prb_free = [tmm.done]
            if h == 0 and "ml_dbg_atn" in kb.dbg:
                dbg_at = kb.dram("ml_dbg_atn", (4, 2048, 512), BF16)
                for kt in range(nkt):
                    p.op("sync", lambda e, kt=kt, qc=qc: e.dma_start(
                            out=dbg_at[qc, 128 * kt:128 * (kt + 1), :],
                            in_=ATN[kt]),
                         waits=[atn_done], incs=qst.inc(16), tag="dbg_at")
            for ct in range(4):
                for kt in range(nkt):
                    p.op("tensor", lambda e, ct=ct, kt=kt, nkt=nkt: e.matmul(
                            kb.PSS[ct][:], KVT[kt][:, 128 * ct:128 * (ct + 1)],
                            ATN[kt], start=(kt == 0), stop=(kt == nkt - 1)),
                         waits=[atn_done] + _as_waits(olb_free)
                               if (ct == 0 and kt == 0) else None,
                         incs=(cm2 := mm.inc()) if kt == nkt - 1 else None,
                         tag=f"ml_ol{h}_{ct}")
                p.op("scalar", lambda e, ct=ct: e.activation(
                        OLB[:, 512 * ct:512 * (ct + 1)], kb.PSS[ct][:],
                        AF.Identity),
                     waits=[cm2.done], incs=(ole := ev.inc()),
                     tag=f"ml_ole{h}_{ct}")
            ctx_free = [ole.done]
            for ct in range(4):
                p.op("tensor", lambda e, ct=ct, b=b: e.matmul(
                        kb.PSB[:, 0:512], WVB[b][:, 128 * ct:128 * (ct + 1)],
                        OLB[:, 512 * ct:512 * (ct + 1)],
                        start=(ct == 0), stop=(ct == 3)),
                     waits=([lwv.done, ole.done] + _as_waits(psb_scores_free))
                           if ct == 0 else None,
                     incs=(ovm := mm.inc()) if ct == 3 else None,
                     tag=f"ml_ov{h}_{qc}")
            p.op("scalar", lambda e, qc=qc: e.activation(
                    OVH[:, 512 * qc:512 * (qc + 1)], kb.PSB[:, 0:512],
                    AF.Identity),
                 waits=[ovm.done] + (_as_waits(ovh_free) if qc == 0 else []),
                 incs=(ove := ev.inc()), tag=f"ml_ove{h}_{qc}")
            psb_scores_free = [ove.done]
            olb_free = [ovm.done]
        p.op("sync", lambda e, h=h: e.dma_start(
                out=ov_hbm[128 * h:128 * (h + 1), :], in_=OVH),
             waits=[ove.done], incs=(ovs := qst.inc(16)), tag=f"ml_ovst{h}")
        ovh_free = [ovs.done]
        hload_free["qn"][b] = [qlm.done]
        hload_free["wk"][b] = [qlm.done]
        hload_free["wv"][b] = [ovm.done]
        if h % 2 == 1:
            hload_free["qp"][(h // 2) % 2] = [smm.done]
    ov_all = qst.at()
    if "wo" not in stages:
        return dict(done=ov_all)

    WOT = [kb.carve(4096 * k, 2048, BF16) for k in range(16)]
    OVS = [kb.carve(65536 + 4096 * i, 2048, BF16) for i in range(2)]
    wo_c = p.chan("f_wr")
    wog = [p.chan("f_gl0"), p.chan("f_gl1")]
    mxl = [p.chan("f_dn0"), p.chan("f_dn1")]
    for k in range(16):
        p.op("sync", lambda e, k=k: e.dma_start(
                out=WOT[k], in_=wo_bf[128 * k:128 * (k + 1), :]),
             waits=[ov_all, cwaits[4]] if k == 0 else None,
             incs=(wol := wo_c.inc(16)), tag=f"ml_wol{k}")
    wo_ready = wol.done
    ovs_free = [None, None]
    bank_free2 = [None] * 8
    add_prev = None
    xt_free = [None, None]
    for j in range(NT):
        ob = j % 2
        p.op("gpsimd", lambda e, j=j, ob=ob: e.dma_start(
                out=OVS[ob].rearrange("p (k t) -> p k t", t=128),
                in_=ov_hbm[:, 128 * j:128 * (j + 1)].rearrange(
                    "(k p) t -> p k t", p=128)),
             waits=([ov_all] if j < 2 else []) + _as_waits(ovs_free[ob]),
             incs=(ols := wog[ob].inc(16)), tag=f"ml_ovsl{j}")
        p.op("sync", lambda e, j=j, ob=ob: e.dma_start(
                out=kb.XT[ob][:], in_=x_in[128 * j:128 * (j + 1), :]),
             waits=_as_waits(xt_free[ob]), incs=(xl := mxl[ob].inc(16)),
             tag=f"ml_xl{j}")
        bks = [j % 2 * 4 + c for c in range(4)]
        for c in range(4):
            bkid = bks[c]
            bank = (kb.PSS[bkid][:] if bkid < 4 else
                    kb.PSB[:, 512 * (bkid - 4):512 * (bkid - 3)])
            for k in range(16):
                p.op("tensor", lambda e, k=k, c=c, ob=ob, bank=bank: e.matmul(
                        bank, OVS[ob][:, 128 * k:128 * (k + 1)],
                        WOT[k][:, 512 * c:512 * (c + 1)],
                        start=(k == 0), stop=(k == 15)),
                     waits=([ols.done, wo_ready] + _as_waits(bank_free2[bkid]))
                           if k == 0 else None,
                     incs=(wm := mm.inc()) if k == 15 else None,
                     tag=f"ml_wo{j}_{c}")
            p.op("vector", lambda e, c=c, ob=ob, bank=bank: e.tensor_add(
                    kb.TMP[0][:, 512 * c:512 * (c + 1)],
                    kb.XT[ob][:, 512 * c:512 * (c + 1)], bank),
                 waits=[wm.done, xl.done] + (_as_waits(add_prev) if c == 0 else []),
                 incs=(ad := dv.inc()), tag=f"ml_add{j}_{c}")
            bank_free2[bkid] = [ad.done]
        ovs_free[ob] = [wm.done]
        p.op("sync", lambda e, j=j: e.dma_start(
                out=x_out[128 * j:128 * (j + 1), :], in_=kb.TMP[0][:]),
             waits=[ad.done], incs=(xo := xoc.inc(16)), tag=f"ml_xo{j}")
        add_prev = [xo.done]
        xt_free[ob] = [ad.done]
    return dict(done=xoc.at())


MIX_SCALE = float(128 ** -0.5)


def emit_mix1(kb, x_in, x_out, ci, start=None,
              stages=("pre", "qkv", "heads", "wo", "ff")):
    p, nc = kb.p, kb.nc
    kb.act_set(SET_NL_EXP)

    start = _as_waits(start)
    if kb.once("mx_setup"):
        sld = p.chan("mx_sld")
        for w in ["wrT", "wdT1", "wupT1", "wqT2", "wkT2", "wvT2", "woT2",
                  "wff1T", "wff2T", "sbias_row"]:
            kb.cast_weight(w, "mx")
        kb.cast_group_done("mx")
        kb.LN1S = nc.alloc_sbuf_tensor("LN1S", [128, 16], F32)
        kb.LN1B = nc.alloc_sbuf_tensor("LN1B", [128, 16], F32)
        kb.LN2S = nc.alloc_sbuf_tensor("LN2S", [128, 16], F32)
        kb.LN2B = nc.alloc_sbuf_tensor("LN2B", [128, 16], F32)
        kb.LGT = nc.alloc_sbuf_tensor("LGT", [128, 64], F32)
        p.op("vector", lambda e: e.memset(kb.LGT[:], 0.0), tag="mx_lgt0")
        kb.BIAS_BC = nc.alloc_sbuf_tensor("BIAS_BC", [128, 2048], BF16)
        kb.WDT1 = nc.alloc_sbuf_tensor("WDT1", [128, 512], BF16)
        kb.WRT = nc.alloc_sbuf_tensor("WRT", [128, 48], BF16)
        for nm, t in [("ln1s_pd", kb.LN1S), ("ln1b_pd", kb.LN1B),
                      ("ln2s_pd", kb.LN2S), ("ln2b_pd", kb.LN2B)]:
            p.op("sync", lambda e, nm=nm, t=t: e.dma_start(
                    out=t[:], in_=kb.din[nm]), incs=sld.inc(16))
        sb_bf = kb.scr["sbias_row_bf"]; wsb = kb._flags["sbias_row_bf"]
        p.op("sync", lambda e: e.dma_start(out=kb.ROW2[:], in_=sb_bf),
             waits=[wsb], incs=sld.inc(16))
        wd1_bf = kb.scr["wdT1_bf"]; wd1w = kb._flags["wdT1_bf"]
        p.op("sync", lambda e: e.dma_start(
                out=kb.WDT1[:].rearrange("p (k r) -> p k r", r=32),
                in_=wd1_bf.rearrange("(k p) r -> p k r", p=128)),
             waits=[wd1w], incs=sld.inc(16))
        kb.cast_weight("wupT1")
        wr_bf = kb.scr["wrT_bf"]; wrw = kb._flags["wrT_bf"]
        p.op("sync", lambda e: e.dma_start(
                out=kb.WRT[:].rearrange("p (k r) -> p k r", r=3),
                in_=wr_bf.rearrange("(k p) r -> p k r", p=128)),
             waits=[wrw], incs=sld.inc(16))
        small_ready = sld.at()
        bmm = p.chan("mx_bb")
        for c in range(4):
            p.op("tensor", lambda e, c=c: e.matmul(
                    kb.PSS[c][:], kb.onesr[:],
                    kb.ROW2[0:1, 512 * c:512 * (c + 1)], start=True, stop=True),
                 waits=([small_ready] + start) if c == 0 else None,
                 incs=(bm := bmm.inc()) if c == 3 else None, tag="mx_bbc")
        bevc = p.chan("g_ev")
        for c in range(4):
            p.op("scalar", lambda e, c=c: e.activation(
                    kb.BIAS_BC[:, 512 * c:512 * (c + 1)], kb.PSS[c][:],
                    AF.Identity),
                 waits=[bm.done] if c == 0 else None,
                 incs=(bfin := bevc.inc()), tag="mx_bbe")
        kb._flags["mx_small"] = [small_ready, bfin.done, kb.ensure_msk4()]
        kb._flags["mx_bias_pss_free"] = [bfin.done]
    small = kb._flags["mx_small"]

    xn_tm = kb.dram(f"mx{ci}_xn", (S, D), BF16)
    xn1_tm = kb.dram(f"mx{ci}_xn1", (S, D), BF16)
    xn2_tm = kb.dram(f"mx{ci}_xn2", (S, D), BF16)
    xi_hbm = kb.dram(f"mx{ci}_xi", (S, D), F32)
    q_hbm = kb.dram(f"mx{ci}_q", (D, S), BF16)
    k_hbm = kb.dram(f"mx{ci}_k", (D, S), BF16)
    v_hbm = kb.dram(f"mx{ci}_v", (D, S), BF16)
    ctx_hbm = kb.dram(f"mx{ci}_ctx", (D, S), BF16)
    g2_hbm = kb.dram(f"mx{ci}_g", (HID, D), BF16)

    SM = kb.SM
    T1B = kb.carve(106496, 2048, BF16)[0:32, :]
    WUPT1 = kb.carve(110592, 2048, BF16)[0:32, :]
    wupc = p.chan("mx_wup")
    p.op("sync", lambda e: e.dma_start(out=WUPT1, in_=kb.scr["wupT1_bf"]),
         waits=[kb._flags["wupT1_bf"]] + start, incs=(wup_ld := wupc.inc(16)),
         tag=f"mx{ci}wupld")
    lnr = emit_ln(kb, x_in, xn_tm, "LN", start=start,
                  inner_stats=lambda i: (SM[:, 176 + i:177 + i],
                                         SM[:, 192 + i:193 + i]))
    mm = p.chan("g_mm")
    ev = p.chan("g_ev")
    dv = p.chan("g_dv")
    st1 = p.chan("mx_st")

    ln1_toks = []
    for i in range(NT):
        s_c = SM[:, 176 + i:177 + i]
        q_c = SM[:, 192 + i:193 + i]
        m_c = SM[:, 368 + i:369 + i]
        v_c = SM[:, 384 + i:385 + i]
        r_c = SM[:, 208 + i:209 + i]
        n_c = SM[:, 224 + i:225 + i]
        p.op("vector", lambda e, s=s_c, o=m_c: e.tensor_scalar_mul(o, s, 1.0 / D),
             waits=[lnr["inner_done"][i][0]], incs=(a1 := dv.inc()))
        p.op("scalar", lambda e, m=m_c, o=v_c: e.activation(o, m, AF.Square),
             waits=[a1.done, lnr["inner_done"][i][1]], incs=(a2 := ev.inc()))
        p.op("vector", lambda e, q=q_c, m2=v_c, o=v_c: e.scalar_tensor_tensor(
                o, q, 1.0 / D, m2, ALU.mult, ALU.subtract),
             waits=[a2.done], incs=(a3 := dv.inc()))
        p.op("vector", lambda e, o=v_c: e.tensor_scalar_add(o, o, EPS_LN),
             waits=[a3.done], incs=(a4 := dv.inc()))
        p.op("scalar", lambda e, v=v_c, o=r_c: e.activation(o, v, AF.Ln),
             waits=[a4.done], incs=(a5 := ev.inc()))
        p.op("scalar", lambda e, r=r_c: e.activation(r, r, AF.Exp, scale=-0.5),
             incs=(a6 := ev.inc()))
        p.op("vector", lambda e, m=m_c, r=r_c, o=n_c: e.scalar_tensor_tensor(
                o, m, -1.0, r, ALU.mult, ALU.mult),
             waits=[a6.done], incs=(a7 := dv.inc()))
        ln1_toks.append(a7.done)

    msk_toks = []
    lg_prev = None
    for j in range(NT):
        bk = j % 2
        for k in range(ND):
            p.op("tensor", lambda e, j=j, k=k, bk=bk: e.matmul(
                    kb.PSB[:, 512 * bk:512 * bk + 3],
                    kb.xfm(k)[:, 128 * j:128 * (j + 1)],
                    kb.WRT[:, 3 * k:3 * (k + 1)],
                    start=(k == 0), stop=(k == ND - 1)),
                 waits=(lnr["fm_ready"][k] + [] if False else
                        ([lnr["fm_ready"][k]] + small +
                         _as_waits(kb._flags.get("mx_bias_pss_free")) +
                         _as_waits(lg_prev)))
                       if k == 0 else None,
                 incs=(lgm := mm.inc()) if k == ND - 1 else None,
                 tag=f"mx_rt{j}")
        p.op("scalar", lambda e, j=j, bk=bk: e.activation(
                kb.LGT[:, 4 * j:4 * j + 3], kb.PSB[:, 512 * bk:512 * bk + 3],
                AF.Identity),
             waits=[lgm.done], incs=(lge := ev.inc()), tag=f"mx_rte{j}")
        lg_prev = [lge.done] if j % 2 == 1 else lg_prev
        l0 = kb.LGT[:, 4 * j:4 * j + 1]
        l1 = kb.LGT[:, 4 * j + 1:4 * j + 2]
        l2 = kb.LGT[:, 4 * j + 2:4 * j + 3]
        g01 = SM[:, 400 + j:401 + j]
        g02 = SM[:, 416 + j:417 + j]
        g12 = SM[:, 432 + j:433 + j]
        is0 = SM[:, 304 + j:305 + j]
        is1 = SM[:, 320 + j:321 + j]
        is2 = SM[:, 336 + j:337 + j]
        is01 = SM[:, 352 + j:353 + j]
        p.op("vector", lambda e, a=l0, b2=l1, o=g01: e.tensor_tensor(
                o, a, b2, ALU.is_ge), waits=[lge.done], incs=(w1 := dv.inc()))
        p.op("vector", lambda e, a=l0, b2=l2, o=g02: e.tensor_tensor(
                o, a, b2, ALU.is_ge), incs=(w2 := dv.inc()))
        p.op("vector", lambda e, a=l1, b2=l2, o=g12: e.tensor_tensor(
                o, a, b2, ALU.is_ge), incs=(w3 := dv.inc()))
        p.op("vector", lambda e, a=g01, b2=g02, o=is0: e.tensor_mul(o, a, b2),
             waits=[w3.done], incs=(w4 := dv.inc()))
        p.op("vector", lambda e, a=is0, o=is1: e.tensor_scalar(
                o, a, -1.0, 1.0, ALU.mult, ALU.add),
             waits=[w4.done], incs=(w5 := dv.inc()))
        p.op("vector", lambda e, a=is1, b2=g12, o=is1: e.tensor_mul(o, a, b2),
             waits=[w5.done], incs=(w6 := dv.inc()))
        p.op("vector", lambda e, a=is0, b2=is1, o=is01: e.tensor_add(o, a, b2),
             waits=[w6.done], incs=(w7 := dv.inc()))
        p.op("vector", lambda e, a=is01, o=is2: e.tensor_scalar(
                o, a, -1.0, 1.0, ALU.mult, ALU.add),
             waits=[w7.done], incs=(mk := dv.inc()),
             tag=f"mx_msk{j}")
        msk_toks.append(mk.done)
    masks_done = mk.done
    if f"mx{ci}_msk2" in kb.dbg:
        dbgM2 = kb.dram(f"mx{ci}_msk2", (128, 576), F32)
        p.op("sync", lambda e: e.dma_start(out=dbgM2[:, 0:512], in_=SM[:]),
             waits=[masks_done], incs=st1.inc(16), tag="dbgMsk2")
        p.op("sync", lambda e: e.dma_start(out=dbgM2[:, 512:576], in_=kb.LGT[:]),
             incs=st1.inc(16), tag="dbgLgt2")

    for c in range(4):
        for k in range(ND):
            p.op("tensor", lambda e, c=c, k=k: e.matmul(
                    kb.PSB[0:32, 1024 + 512 * (c % 2):1024 + 512 * (c % 2) + 512]
                    if False else kb.PSS[c][0:32, :],
                    kb.WDT1[:, 32 * k:32 * (k + 1)],
                    kb.xfm(k)[:, 512 * c:512 * (c + 1)],
                    start=(k == 0), stop=(k == ND - 1)),
                 waits=([lnr["fm_ready"][k]] + small +
                        _as_waits(kb._flags.get("mx_bias_pss_free")))
                       if k == 0 else None,
                 incs=(t1m := mm.inc()) if k == ND - 1 else None,
                 tag=f"mx_t1{c}")
    for c in range(4):
        p.op("scalar", lambda e, c=c: e.activation(
                T1B[:, 512 * c:512 * (c + 1)], kb.PSS[c][0:32, :],
                AF.Identity),
             waits=[t1m.done] if c == 0 else None, incs=(t1e := ev.inc()),
             tag=f"mx_t1e{c}")
    t1_done = t1e.done

    x1st = [p.chan("mx_x1a"), p.chan("mx_x1b")]
    xld = [p.chan("mx_xla"), p.chan("mx_xlb")]
    wb_free = [None, None]
    for i in range(NT):
        b = i % 2
        wbuf = kb.WB[b]
        p.op("sync", lambda e, i=i, b=b: e.dma_start(
                out=kb.XT[b][:, 0:1024].bitcast(BF16),
                in_=xn_tm[128 * i:128 * (i + 1), :]),
             waits=([lnr["tm_done"][0], lnr["tm_done"][1]] if i < 2 else []) +
                   _as_waits(wb_free[b]),
             incs=(xr := xld[b].inc(16)), tag=f"mx_x1l{i}")
        r_c = SM[:, 208 + i:209 + i]
        n_c = SM[:, 224 + i:225 + i]
        p.op("scalar", lambda e, b=b, r=r_c, n=n_c, wbuf=wbuf: e.activation(
                wbuf[:], kb.XT[b][:, 0:1024].bitcast(BF16), AF.Identity,
                bias=n, scale=r),
             waits=[xr.done, ln1_toks[i]], incs=(xno := ev.inc()),
             tag=f"mx_x1n{i}")
        p.op("sync", lambda e, i=i, wbuf=wbuf: e.dma_start(
                out=xn1_tm[128 * i:128 * (i + 1), :], in_=wbuf[:]),
             waits=[xno.done], incs=(x1s := x1st[b].inc(16)),
             tag=f"mx_x1s{i}")
        wb_free[b] = [x1s.done]
    xn1_all = [x1st[0].at(), x1st[1].at()]

    trc = p.chan("mx_tr")
    s1ap = p.chan("mx_s1")
    for dt in range(ND):
        p.op("sync", lambda e, dt=dt: e.dma_start_transpose(
                kb.xfm(dt), xn1_tm[:, 128 * dt:128 * (dt + 1)]),
             waits=(xn1_all + [t1m.done, lgm.done]) if dt == 0 else None,
             incs=(tq := trc.inc(16)), tag=f"mx_x1t{dt}")
    tr_all = tq.done
    for dt in range(ND):
        p.op("vector", lambda e, dt=dt: e.tensor_scalar(
                kb.xfm(dt), kb.xfm(dt), kb.LN1S[:, dt:dt + 1],
                kb.LN1B[:, dt:dt + 1], ALU.mult, ALU.add),
             waits=[tr_all, masks_done] if dt == 0 else None,
             incs=(sb1 := dv.inc()), tag=f"mx_s1b{dt}")
    xn1_ready = sb1.done

    if "qkv" not in stages:
        return dict(done=_Wait(dv, dv.count))
    WQB = [kb.carve(65536 + 4096 * i, 2048, BF16) for i in range(2)]
    QNB = [kb.carve(73728 + 4096 * i, 2048, BF16) for i in range(2)]
    wlc = [p.chan("mx_wl0"), p.chan("mx_wl1")]
    pst = [p.chan("mx_ps0"), p.chan("mx_ps1")]
    wqb_free = [None, None]
    qnb_free = [None, None]
    pss_free = kb._flags.get("mx_bias_pss_free")
    mi = 0
    for wname, out_h in [("wqT2", q_hbm), ("wkT2", k_hbm), ("wvT2", v_hbm)]:
        w_bf = kb.scr[wname + "_bf"]
        cw = kb._flags[wname + "_bf"]
        for m2 in range(16):
            b = mi % 2
            p.op("sync", lambda e, m2=m2, b=b, w_bf=w_bf: e.dma_start(
                    out=WQB[b].rearrange("p (k m) -> p k m", m=128),
                    in_=w_bf[:, 128 * m2:128 * (m2 + 1)].rearrange(
                        "(k p) m -> p k m", p=128)),
                 waits=([cw] if mi < 2 else []) + _as_waits(wqb_free[b]),
                 incs=(qw := wlc[b].inc(16)), tag=f"mx_w{wname}{m2}")
            for k in range(ND):
                for c in range(4):
                    if k == 0 and c == 0:
                        w = [qw.done, xn1_ready] + _as_waits(pss_free)
                    else:
                        w = None
                    p.op("tensor", lambda e, k=k, c=c, b=b: e.matmul(
                            kb.PSS[c][:], WQB[b][:, 128 * k:128 * (k + 1)],
                            kb.xfm(k)[:, 512 * c:512 * (c + 1)],
                            start=(k == 0), stop=(k == ND - 1)),
                         waits=w,
                         incs=(qm := mm.inc()) if (k == ND - 1 and c == 3) else None,
                         tag=f"mx_p{wname}{m2}")
            wqb_free[b] = [qm.done]
            for c in range(4):
                p.op("scalar", lambda e, c=c, b=b: e.activation(
                        QNB[b][:, 512 * c:512 * (c + 1)], kb.PSS[c][:],
                        AF.Identity),
                     waits=([qm.done] + _as_waits(qnb_free[b])) if c == 0 else None,
                     incs=(qe := ev.inc()), tag=f"mx_e{wname}{m2}")
            pss_free = [qe.done]
            p.op("sync", lambda e, m2=m2, b=b, out_h=out_h: e.dma_start(
                    out=out_h[128 * m2:128 * (m2 + 1), :], in_=QNB[b]),
                 waits=[qe.done], incs=(qs := pst[b].inc(16)),
                 tag=f"mx_s{wname}{m2}")
            qnb_free[b] = [qs.done]
            mi += 1
    qkv_all = [pst[0].at(), pst[1].at()]

    if "heads" not in stages:
        return dict(done=qkv_all[0])
    KH = [kb.carve(4096 * i, 2048, BF16) for i in range(2)]
    QH = [kb.carve(8192 + 4096 * i, 2048, BF16) for i in range(2)]
    VTM = [kb.carve(16384 + 4096 * i, 2048, BF16) for i in range(2)]
    PRB = [kb.carve(24576 + 4096 * i, 2048, BF16) for i in range(4)]
    ATN = [kb.carve(40960 + 1024 * j, 512, BF16) for j in range(16)]
    CXH = kb.carve(57344, 2048, BF16)
    hlc = {"k": [p.chan("ml_hwk0"), p.chan("ml_hwk1")],
           "q": [p.chan("ml_hqn0"), p.chan("ml_hqn1")],
           "v": [p.chan("ml_hwv0"), p.chan("ml_hwv1")]}
    cst = p.chan("mx_cst")
    hfree = {k2: [None, None] for k2 in ("k", "q", "v")}
    atn_evac = None
    ctx_free = None
    cxh_free = None
    prb_free = None
    psb_scores_free = None
    for h in range(16):
        b = h % 2
        p.op("sync", lambda e, h=h, b=b: e.dma_start(
                out=KH[b], in_=k_hbm[128 * h:128 * (h + 1), :]),
             waits=(qkv_all if h < 2 else []) + _as_waits(hfree["k"][b]),
             incs=(lk := hlc["k"][b].inc(16)), tag=f"mx_hk{h}")
        p.op("sync", lambda e, h=h, b=b: e.dma_start(
                out=QH[b], in_=q_hbm[128 * h:128 * (h + 1), :]),
             waits=(qkv_all if h < 2 else []) + _as_waits(hfree["q"][b]),
             incs=(lq := hlc["q"][b].inc(16)), tag=f"mx_hq{h}")
        for kt in range(16):
            p.op("sync", lambda e, h=h, b=b, kt=kt: e.dma_start_transpose(
                    VTM[b][:, 128 * kt:128 * (kt + 1)],
                    v_hbm[128 * h:128 * (h + 1), 128 * kt:128 * (kt + 1)]),
                 waits=((qkv_all if h < 2 else []) + _as_waits(hfree["v"][b]))
                       if kt == 0 else None,
                 incs=(lv := hlc["v"][b].inc(16)), tag=f"mx_hv{h}")
        for qc in range(4):
            nkc = qc + 1
            nkt = 4 * (qc + 1)
            ex_prev = None
            for qil in range(4):
                qa = 4 * qc + qil
                for kc in range(nkc):
                    p.op("tensor", lambda e, qa=qa, kc=kc, b=b, qc=qc: e.matmul(
                            kb.PSB[:, 512 * kc:512 * (kc + 1)],
                            QH[b][:, 128 * qa:128 * (qa + 1)],
                            KH[b][:, 512 * kc:512 * (kc + 1)],
                            start=True, stop=(kc != qc)),
                         waits=([lq.done, lk.done] + _as_waits(psb_scores_free))
                               if (kc == 0 and qil == 0) else
                               (_as_waits(ex_prev) if kc == 0 else None),
                         tag=f"mx_s{h}_{qa}")
                p.op("tensor", lambda e, qa=qa, qil=qil, qc=qc: e.matmul(
                        kb.PSB[:, 512 * qc:512 * (qc + 1)], kb.ident[:],
                        kb.MSK4[:, 512 * qil:512 * (qil + 1)],
                        start=False, stop=True),
                     incs=(smm := mm.inc()), tag=f"mx_m{h}_{qa}")
                nm = SM[:, 448 + qil:449 + qil]
                nms = SM[:, 452 + qil:453 + qil]
                sums = SM[:, 456 + qil:457 + qil]
                rec = SM[:, 460 + qil:461 + qil]
                p.op("vector", lambda e, nm=nm, nkc=nkc: e.reduce_max(
                        nm, kb.PSB[:, 0:512 * nkc], AX.X, negate=True),
                     waits=[smm.done], incs=(mx := dv.inc()),
                     tag=f"mx_mx{h}_{qa}")
                p.op("vector", lambda e, nm=nm, nms=nms: e.tensor_scalar_mul(
                        nms, nm, MIX_SCALE), waits=[mx.done],
                     incs=(mxs := dv.inc()))
                p.op("scalar", lambda e, qil=qil, nms=nms, sums=sums, nkc=nkc:
                     e.activation(
                        PRB[qil][:, 0:512 * nkc], kb.PSB[:, 0:512 * nkc],
                        AF.Exp, bias=nms, scale=MIX_SCALE, accum_out=sums),
                     waits=[mxs.done] + (_as_waits(prb_free) if qil == 0 else []),
                     incs=(ex := ev.inc()), tag=f"mx_ex{h}_{qa}")
                ex_prev = [ex.done]
                p.op("vector", lambda e, sums=sums, rec=rec: e.reciprocal(
                        rec, sums), waits=[ex.done], incs=(rc2 := dv.inc()))
                p.op("vector", lambda e, qil=qil, rec=rec: e.tensor_scalar_mul(
                        kb.SMB[:, 128 * qil:128 * (qil + 1)], kb.ident[:], rec),
                     waits=[rc2.done], incs=(dg := dv.inc()),
                     tag=f"mx_dg{h}_{qa}")
                if qil == 3:
                    psb_scores_free = [ex.done]
            diag_done = dg.done
            tr_ev = None
            for kt in range(nkt):
                ps = kb.PSS[kt % 2][:]
                for qil in range(4):
                    p.op("tensor", lambda e, kt=kt, qil=qil, ps=ps: e.matmul(
                            ps[:, 128 * qil:128 * (qil + 1)],
                            PRB[qil][:, 128 * kt:128 * (kt + 1)],
                            kb.SMB[:, 128 * qil:128 * (qil + 1)],
                            start=True, stop=True),
                         waits=([diag_done] + _as_waits(atn_evac) +
                                (_as_waits(ctx_free) if kt < 2 else []))
                         if qil == 0 else None,
                         incs=(tmm := mm.inc()) if qil == 3 else None,
                         tag=f"mx_tr{h}_{kt}")
                p.op("scalar", lambda e, kt=kt, ps=ps: e.activation(
                        ATN[kt], ps, AF.Identity),
                     waits=[tmm.done], incs=(tr_ev := ev.inc()),
                     tag=f"mx_tre{h}_{kt}")
                atn_evac = [tr_ev.done]
            atn_done = tr_ev.done
            prb_free = [tmm.done]
            for kt in range(nkt):
                p.op("tensor", lambda e, kt=kt, b=b, nkt=nkt: e.matmul(
                        kb.PSS[2][:], VTM[b][:, 128 * kt:128 * (kt + 1)],
                        ATN[kt], start=(kt == 0), stop=(kt == nkt - 1)),
                     waits=([atn_done, lv.done] + _as_waits(ctx_free))
                           if kt == 0 else None,
                     incs=(cm2 := mm.inc()) if kt == nkt - 1 else None,
                     tag=f"mx_cx{h}_{qc}")
            p.op("scalar", lambda e, qc=qc: e.activation(
                    CXH[:, 512 * qc:512 * (qc + 1)], kb.PSS[2][:], AF.Identity),
                 waits=[cm2.done] + (_as_waits(cxh_free) if qc == 0 else []),
                 incs=(cxe := ev.inc()), tag=f"mx_cxe{h}_{qc}")
            ctx_free = [cxe.done]
        p.op("sync", lambda e, h=h: e.dma_start(
                out=ctx_hbm[128 * h:128 * (h + 1), :], in_=CXH),
             waits=[cxe.done], incs=(cxs := cst.inc(16)), tag=f"mx_cxs{h}")
        cxh_free = [cxs.done]
        hfree["k"][b] = [_Wait(mm, mm.count)]
        hfree["q"][b] = [_Wait(mm, mm.count)]
        hfree["v"][b] = [_Wait(mm, mm.count)]
    ctx_all = cst.at()

    WOT2 = [kb.carve(4096 * k, 2048, BF16) for k in range(16)]
    CVS = [kb.carve(65536 + 4096 * i, 2048, BF16) for i in range(2)]
    woc = p.chan("mx_wo")
    cvl = [p.chan("mx_cv0"), p.chan("mx_cv1")]
    xnl = [p.chan("mx_xn0"), p.chan("mx_xn1")]
    xis = [p.chan("mx_xi0"), p.chan("mx_xi1")]
    x2s = [p.chan("mx_x20"), p.chan("mx_x21")]
    wo_bf = kb.scr["woT2_bf"]
    for k in range(16):
        p.op("sync", lambda e, k=k: e.dma_start(
                out=WOT2[k], in_=wo_bf[128 * k:128 * (k + 1), :]),
             waits=[ctx_all, kb._flags["woT2_bf"]] if k == 0 else None,
             incs=(wol := woc.inc(16)), tag=f"mx_wol{k}")
    wo_ready = wol.done
    XN2B = [kb.carve(73728 + 4096 * i, 2048, BF16) for i in range(2)]
    cvs_free = [None, None]
    wbx_free = [None, None]
    xn2b_free = [None, None]
    tmp0_free = None
    bank_free = [None] * 8
    banksA = [kb.PSS[i][:] for i in range(4)] + \
             [kb.PSB[:, 512 * i:512 * (i + 1)] for i in range(4)]
    for j in range(NT):
        ob = j % 2
        p.op("gpsimd", lambda e, j=j, ob=ob: e.dma_start(
                out=CVS[ob].rearrange("p (k t) -> p k t", t=128),
                in_=ctx_hbm[:, 128 * j:128 * (j + 1)].rearrange(
                    "(k p) t -> p k t", p=128)),
             waits=([ctx_all] if j < 2 else []) + _as_waits(cvs_free[ob]),
             incs=(cvld := cvl[ob].inc(16)), tag=f"mx_cvl{j}")
        wbuf = kb.WB[ob]
        p.op("sync", lambda e, j=j, wbuf=wbuf: e.dma_start(
                out=wbuf[:], in_=xn_tm[128 * j:128 * (j + 1), :]),
             waits=_as_waits(wbx_free[ob]) + (lnr["tm_done"] if j < 2 else []),
             incs=(xnr := xnl[ob].inc(16)),
             tag=f"mx_xnr{j}")
        for c in range(4):
            bk = (j % 2) * 4 + c
            for k in range(16):
                p.op("tensor", lambda e, k=k, c=c, ob=ob, bk=bk: e.matmul(
                        banksA[bk], CVS[ob][:, 128 * k:128 * (k + 1)],
                        WOT2[k][:, 512 * c:512 * (c + 1)],
                        start=(k == 0), stop=(k == 15)),
                     waits=([cvld.done, wo_ready] + _as_waits(bank_free[bk]) +
                            (_as_waits(psb_scores_free) if j < 2 else []) +
                            (_as_waits(ctx_free) if j < 2 else []))
                           if k == 0 else None,
                     incs=(wm := mm.inc()) if k == 15 else None,
                     tag=f"mx_wo{j}_{c}")
            p.op("vector", lambda e, c=c, ob=ob, bk=bk, wbuf=wbuf: e.tensor_add(
                    kb.TMP[0][:, 512 * c:512 * (c + 1)],
                    wbuf[:, 512 * c:512 * (c + 1)], banksA[bk]),
                 waits=([wm.done, xnr.done] + _as_waits(tmp0_free))
                       if c == 0 else [wm.done],
                 incs=(xia := dv.inc()), tag=f"mx_xia{j}_{c}")
            bank_free[bk] = [xia.done]
        p.op("sync", lambda e, j=j: e.dma_start(
                out=xi_hbm[128 * j:128 * (j + 1), :], in_=kb.TMP[0][:]),
             waits=[xia.done], incs=(xist := xis[ob].inc(16)),
             tag=f"mx_xis{j}")
        wbx_free[ob] = [xia.done]
        cvs_free[ob] = [wm.done]
        s_c = SM[:, 240 + j:241 + j]
        q_c = SM[:, 256 + j:257 + j]
        m_c = SM[:, 464 + j:465 + j]
        v_c = SM[:, 468 + j:469 + j]
        r_c = SM[:, 272 + j:273 + j]
        n_c = SM[:, 288 + j:289 + j]
        p.op("vector", lambda e, o=s_c: e.reduce_sum(o, kb.TMP[0][:], AX.X),
             waits=[xia.done], incs=(b1 := dv.inc()))
        p.op("scalar", lambda e, o=q_c, ob=ob: e.activation(
                kb.XT[ob][:], kb.TMP[0][:], AF.Square, accum_out=o),
             waits=[xia.done], incs=(b2 := ev.inc()))
        p.op("vector", lambda e, s=s_c, o=m_c: e.tensor_scalar_mul(o, s, 1.0 / D),
             waits=[b1.done], incs=(b3 := dv.inc()))
        p.op("scalar", lambda e, m=m_c, o=v_c: e.activation(o, m, AF.Square),
             waits=[b3.done], incs=(b4 := ev.inc()))
        p.op("vector", lambda e, q=q_c, m2=v_c, o=v_c: e.scalar_tensor_tensor(
                o, q, 1.0 / D, m2, ALU.mult, ALU.subtract),
             waits=[b2.done, b4.done], incs=(b5 := dv.inc()))
        p.op("vector", lambda e, o=v_c: e.tensor_scalar_add(o, o, EPS_LN),
             waits=[b5.done], incs=(b6 := dv.inc()))
        p.op("scalar", lambda e, v=v_c, o=r_c: e.activation(o, v, AF.Ln),
             waits=[b6.done], incs=(b7 := ev.inc()))
        p.op("scalar", lambda e, r=r_c: e.activation(r, r, AF.Exp, scale=-0.5),
             incs=(b8 := ev.inc()))
        p.op("vector", lambda e, m=m_c, r=r_c, o=n_c: e.scalar_tensor_tensor(
                o, m, -1.0, r, ALU.mult, ALU.mult),
             waits=[b8.done], incs=(b9 := dv.inc()))
        p.op("scalar", lambda e, ob=ob, r=r_c, n=n_c: e.activation(
                XN2B[ob], kb.TMP[0][:], AF.Identity, bias=n, scale=r),
             waits=[b9.done] + _as_waits(xn2b_free[ob]),
             incs=(x2n := ev.inc()), tag=f"mx_x2n{j}")
        tmp0_free = [x2n.done, xist.done]
        p.op("sync", lambda e, j=j, ob=ob: e.dma_start(
                out=xn2_tm[128 * j:128 * (j + 1), :], in_=XN2B[ob]),
             waits=[x2n.done], incs=(x2st := x2s[ob].inc(16)),
             tag=f"mx_x2s{j}")
        xn2b_free[ob] = [x2st.done]
    xn2_all = [x2s[0].at(), x2s[1].at()]
    wo_mm_all = _Wait(mm, mm.count)

    if "ff" not in stages:
        return dict(done=xn2_all[0])
    tr2 = p.chan("mx_tr2")
    for dt in range(ND):
        p.op("sync", lambda e, dt=dt: e.dma_start_transpose(
                kb.xfm(dt), xn2_tm[:, 128 * dt:128 * (dt + 1)]),
             waits=(xn2_all + [wo_mm_all]) if dt == 0 else None,
             incs=(t2 := tr2.inc(16)), tag=f"mx_x2t{dt}")
    tr2_all = t2.done
    for dt in range(ND):
        p.op("vector", lambda e, dt=dt: e.tensor_scalar(
                kb.xfm(dt), kb.xfm(dt), kb.LN2S[:, dt:dt + 1],
                kb.LN2B[:, dt:dt + 1], ALU.mult, ALU.add),
             waits=[tr2_all] if dt == 0 else None,
             incs=(sb2 := dv.inc()), tag=f"mx_s2b{dt}")
    xn2_ready = sb2.done

    kb.act_set(SET_GELU_T)
    UPW = [kb.carve(65536 + 16384 * i, 8192, BF16) for i in range(2)]
    GF = [kb.carve(98304 + 4096 * i, 2048, BF16) for i in range(2)]
    f1c = [p.chan("mx_f10"), p.chan("mx_f11")]
    gsc = [p.chan("mx_gs0"), p.chan("mx_gs1")]
    ff1_bf = kb.scr["wff1T_bf"]
    ff2_bf = kb.scr["wff2T_bf"]
    upw_free = [None, None]
    gf_free = [None, None]
    pbank_free = {0: _as_waits(bank_free[0]) + _as_waits(bank_free[1]) +
                     _as_waits(bank_free[2]) + _as_waits(bank_free[3]),
                  1: _as_waits(bank_free[4]) + _as_waits(bank_free[5]) +
                     _as_waits(bank_free[6]) + _as_waits(bank_free[7])}
    for mq in range(NH // 4):
        wb2 = mq % 2
        for k in range(ND):
            p.op("sync", lambda e, mq=mq, k=k, wb2=wb2: e.dma_start(
                    out=UPW[wb2][:, 512 * k:512 * (k + 1)],
                    in_=ff1_bf[128 * k:128 * (k + 1),
                               512 * mq:512 * (mq + 1)]),
                 waits=(([kb._flags["wff1T_bf"], wo_mm_all] + xn2_all)
                        if mq < 2 and k == 0
                        else []) + (_as_waits(upw_free[wb2]) if k == 0 else []),
                 incs=(f1l := f1c[wb2].inc(16)), tag=f"mx_f1w{mq}_{k}")
        f1_ready = f1l.done
        for mm2 in range(4):
            m = 4 * mq + mm2
            pb = m % 2
            banks = ([kb.PSS[c][:] for c in range(4)] if pb == 0 else
                     [kb.PSB[:, 512 * c:512 * (c + 1)] for c in range(4)])
            for k in range(ND):
                for c in range(4):
                    w = None
                    if k == 0 and c == 0:
                        w = [f1_ready, xn2_ready] + _as_waits(pbank_free[pb])
                    p.op("tensor", lambda e, k=k, c=c, m=m, wb2=wb2, banks=banks:
                         e.matmul(
                            banks[c],
                            UPW[wb2][:, 512 * k + 128 * (m % 4):
                                     512 * k + 128 * (m % 4) + 128],
                            kb.xfm(k)[:, 512 * c:512 * (c + 1)],
                            start=(k == 0), stop=(k == ND - 1)),
                         waits=w,
                         incs=(f1m := mm.inc()) if (k == ND - 1 and c == 3)
                              else None,
                         tag=f"mx_f1{m}")
            gfb = m % 2
            for c in range(4):
                p.op("scalar", lambda e, c=c, gfb=gfb, banks=banks: e.activation(
                        GF[gfb][:, 512 * c:512 * (c + 1)], banks[c],
                        AF.Gelu_apprx_tanh),
                     waits=([f1m.done] + _as_waits(gf_free[gfb])) if c == 0
                           else None,
                     incs=(ge := ev.inc()), tag=f"mx_gev{m}")
            pbank_free[pb] = [ge.done]
            p.op("sync", lambda e, m=m, gfb=gfb: e.dma_start(
                    out=g2_hbm[128 * m:128 * (m + 1), :], in_=GF[gfb]),
                 waits=[ge.done], incs=(gst := gsc[gfb].inc(16)),
                 tag=f"mx_gst{m}")
            gf_free[gfb] = [gst.done]
        upw_free[wb2] = [_Wait(mm, mm.count)]
    up2_mm_all = _Wait(mm, mm.count)
    g2_all = [gsc[0].at(), gsc[1].at()]
    kb.act_set(SET_NL_EXP)

    WRESQ = kb.carve(0, 64 * 512, BF16)
    GSB = [kb.carve(65536 + 16384 * i, 8192, BF16) for i in range(2)]
    wrc2 = p.chan("mx_wr2")
    glc2 = [p.chan("mx_gl0"), p.chan("mx_gl1")]
    xoc = p.chan("mx_xoc")
    xac = [p.chan("mx_xa0"), p.chan("mx_xa1")]
    xbc = [p.chan("mx_xb0"), p.chan("mx_xb1")]
    xcc = [p.chan("mx_xc0"), p.chan("mx_xc1")]
    gsb_free = [None, None]
    wres_free = None
    zb_free = [pbank_free[0], pbank_free[0]]
    fb_free = [pbank_free[0], pbank_free[0]]
    xt_free = [None, None]
    smb_free = None
    blend_prev = None
    for qtr in range(4):
        for k in range(NH):
            p.op("sync", lambda e, qtr=qtr, k=k: e.dma_start(
                    out=WRESQ[:, 512 * k:512 * (k + 1)],
                    in_=ff2_bf[128 * k:128 * (k + 1),
                               512 * qtr:512 * (qtr + 1)]),
                 waits=([up2_mm_all, kb._flags["wff2T_bf"]] +
                        _as_waits(wres_free)) if k == 0 else None,
                 incs=(wr2 := wrc2.inc(16)), tag=f"mx_f2w{qtr}_{k}")
        wres_ready = wr2.done
        for j in range(NT):
            gb = j % 2
            p.op("gpsimd", lambda e, j=j, gb=gb: e.dma_start(
                    out=GSB[gb].rearrange("p (k t) -> p k t", t=128),
                    in_=g2_hbm[:, 128 * j:128 * (j + 1)].rearrange(
                        "(k p) t -> p k t", p=128)),
                 waits=(g2_all if (qtr == 0 and j < 2) else []) +
                       _as_waits(gsb_free[gb]),
                 incs=(gl2 := glc2[gb].inc(16)), tag=f"mx_gsl{qtr}_{j}")
            fbk = kb.PSS[j % 2][:]
            zbk = kb.PSS[2 + j % 2][:]
            for k in range(NH):
                w = None
                if k == 0:
                    w = [gl2.done, wres_ready] + _as_waits(fb_free[j % 2]) + \
                        ([up2_mm_all] if (qtr == 0 and j < 2) else [])
                p.op("tensor", lambda e, k=k, gb=gb, fbk=fbk: e.matmul(
                        fbk, GSB[gb][:, 128 * k:128 * (k + 1)],
                        WRESQ[:, 512 * k:512 * (k + 1)],
                        start=(k == 0), stop=(k == NH - 1)),
                     waits=w,
                     incs=(fmm := mm.inc()) if k == NH - 1 else None,
                     tag=f"mx_f2{qtr}_{j}")
            p.op("tensor", lambda e, j=j, qtr=qtr, zbk=zbk: e.matmul(
                    zbk, T1B[:, 128 * j:128 * (j + 1)],
                    WUPT1[:, 512 * qtr:512 * (qtr + 1)],
                    start=True, stop=True),
                 waits=([t1_done, wup_ld.done] if (qtr == 0 and j < 2) else []) +
                       _as_waits(zb_free[j % 2]),
                 incs=(zmm := mm.inc()), tag=f"mx_z{qtr}_{j}")
            gsb_free[gb] = [_Wait(mm, mm.count)]
            xb2 = j % 2
            p.op("sync", lambda e, j=j, qtr=qtr, xb2=xb2: e.dma_start(
                    out=kb.XT[xb2][:, 0:512],
                    in_=x_in[128 * j:128 * (j + 1), 512 * qtr:512 * (qtr + 1)]),
                 waits=_as_waits(xt_free[xb2]), incs=(lxo := xac[xb2].inc(16)),
                 tag=f"mx_bxl{qtr}_{j}")
            p.op("sync", lambda e, j=j, qtr=qtr, xb2=xb2: e.dma_start(
                    out=kb.XT[xb2][:, 512:1024],
                    in_=xi_hbm[128 * j:128 * (j + 1),
                               512 * qtr:512 * (qtr + 1)]),
                 waits=[xis[xb2].at()] if qtr == 0 and j < 2 else None,
                 incs=(lxi := xbc[xb2].inc(16)), tag=f"mx_bxi{qtr}_{j}")
            p.op("sync", lambda e, j=j, qtr=qtr, xb2=xb2: e.dma_start(
                    out=kb.XT[xb2][:, 1024:1280].bitcast(BF16),
                    in_=xn_tm[128 * j:128 * (j + 1),
                              512 * qtr:512 * (qtr + 1)]),
                 incs=(lxn := xcc[xb2].inc(16)), tag=f"mx_bxn{qtr}_{j}")
            is0 = SM[:, 304 + j:305 + j]
            is1 = SM[:, 320 + j:321 + j]
            is2 = SM[:, 336 + j:337 + j]
            is01 = SM[:, 352 + j:353 + j]
            acc = kb.TMP[0][:, 0:512]
            fin = kb.TMP[0][:, 512:1024]
            xnbf = kb.XT[xb2][:, 1024:1280].bitcast(BF16)
            p.op("vector", lambda e, xb2=xb2, is2=is2, acc=acc: e.tensor_scalar(
                    acc, kb.XT[xb2][:, 512:1024], is2, None, ALU.mult),
                 waits=[lxi.done, masks_done] + _as_waits(blend_prev),
                 incs=(c1 := dv.inc()), tag=f"mx_b1{qtr}_{j}")
            if f"mx{ci}_F" in kb.dbg:
                dbgF = kb.dram(f"mx{ci}_F", (S, D), F32)
                FD = kb.carve(98304, 512, F32)
                p.op("scalar", lambda e, fbk=fbk, FD=FD: e.activation(
                        FD, fbk, AF.Identity),
                     waits=[fmm.done], incs=(fde := ev.inc()), tag="dbgFe")
                p.op("sync", lambda e, j=j, qtr=qtr, FD=FD: e.dma_start(
                        out=dbgF[128 * j:128 * (j + 1),
                                 512 * qtr:512 * (qtr + 1)], in_=FD),
                     waits=[fde.done], incs=(fds := xoc.inc(16)), tag="dbgFs")
                fwait = [fmm.done, fds.done]
            else:
                fwait = [fmm.done]
            p.op("vector", lambda e, fbk=fbk, is2=is2, acc=acc:
                 e.scalar_tensor_tensor(acc, fbk, is2, acc, ALU.mult, ALU.add),
                 waits=fwait + [c1.done], incs=(c2 := dv.inc()), tag=f"mx_b2{qtr}_{j}")
            fb_free[j % 2] = [c2.done]
            p.op("vector", lambda e, zbk=zbk, is1=is1, acc=acc:
                 e.scalar_tensor_tensor(acc, zbk, is1, acc, ALU.mult, ALU.add),
                 waits=[zmm.done, c2.done], incs=(c3 := dv.inc()), tag=f"mx_b3{qtr}_{j}")
            zb_free[j % 2] = [c3.done]
            p.op("vector", lambda e, xnbf=xnbf, is01=is01, acc=acc:
                 e.scalar_tensor_tensor(acc, xnbf, is01, acc, ALU.mult, ALU.add),
                 waits=[lxn.done, c3.done], incs=(c4 := dv.inc()), tag=f"mx_b4{qtr}_{j}")
            p.op("vector", lambda e, qtr=qtr, is0=is0, acc=acc:
                 e.scalar_tensor_tensor(
                    acc, kb.BIAS_BC[:, 512 * qtr:512 * (qtr + 1)], is0, acc,
                    ALU.mult, ALU.add),
                 waits=[c4.done], incs=(c5 := dv.inc()), tag=f"mx_b5{qtr}_{j}")
            p.op("vector", lambda e, xb2=xb2, acc=acc, fin=fin:
                 e.scalar_tensor_tensor(fin, acc, 0.5, kb.XT[xb2][:, 0:512],
                                        ALU.mult, ALU.add),
                 waits=[lxo.done, c5.done], incs=(c6 := dv.inc()), tag=f"mx_b6{qtr}_{j}")
            p.op("sync", lambda e, j=j, qtr=qtr, fin=fin: e.dma_start(
                    out=x_out[128 * j:128 * (j + 1),
                              512 * qtr:512 * (qtr + 1)], in_=fin),
                 waits=[c6.done], incs=(xo := xoc.inc(16)),
                 tag=f"mx_xo{qtr}_{j}")
            xt_free[xb2] = [c6.done]
            blend_prev = [xo.done]
        wres_free = [_Wait(mm, mm.count)]
    if f"mx{ci}_msk" in kb.dbg:
        dbgM = kb.dram(f"mx{ci}_msk", (128, 128), F32)
        p.op("sync", lambda e: e.dma_start(out=dbgM[:, 0:64],
                                           in_=SM[:, 304:368]),
             waits=[xoc.at()], incs=xoc.inc(16), tag="dbgMsk")
        p.op("sync", lambda e: e.dma_start(out=dbgM[:, 64:128],
                                           in_=kb.LGT[:]),
             incs=xoc.inc(16), tag="dbgLgt")
    return dict(done=xoc.at())
from concourse.bass_utils import run_bass_kernel_spmd

FUSED = True


def _build_fused(shapes):
    nc = bass.Bass("TRN2", target_bir_lowering=False,
                   detect_race_conditions=False)
    kbb = KB(nc, shapes)
    x_in = kbb.din["x_in"]
    xa = kbb.dram("x_a", (S, D), F32)
    xb = kbb.dram("x_b", (S, D), F32)
    y = kbb.dram_out("y", (S, D), F32)
    r1 = emit_ffn0(kbb, x_in, xa, 0, start=[kbb.const_ready])
    r2 = emit_mamba(kbb, xa, xb, start=[r1["done"]])
    r3 = emit_ffn0(kbb, xb, xa, 1, start=[r2["done"]])
    r4 = emit_mix1(kbb, xa, xb, 0, start=[r3["done"]])
    r5 = emit_mla(kbb, xb, xa, start=[r4["done"]])
    r6 = emit_mix1(kbb, xa, y, 1, start=[r5["done"]])
    kbb.p.op("sync", lambda e: None, waits=[r6["done"]], tag="final")
    kbb.p.build()
    return nc


def _build_one(which, ci, shapes):
    nc = bass.Bass("TRN2", target_bir_lowering=False,
                   detect_race_conditions=False)
    kbb = KB(nc, shapes)
    x_in = kbb.din["x_in"]
    x_out = kbb.dram_out("x_out", (S, D), F32)
    if which == "ffn0":
        r = emit_ffn0(kbb, x_in, x_out, ci, start=[kbb.const_ready])
    elif which == "mamba":
        r = emit_mamba(kbb, x_in, x_out, start=[kbb.const_ready])
    elif which == "mla":
        r = emit_mla(kbb, x_in, x_out, start=[kbb.const_ready])
    else:
        r = emit_mix1(kbb, x_in, x_out, ci, start=[kbb.const_ready])
    kbb.p.op("sync", lambda e: None, waits=[r["done"]], tag="final")
    kbb.p.build()
    return nc


def kernel(**inputs):
    g, xs = prepare_inputs(inputs)
    shapes = shapes_of(g)
    shapes["x_in"] = ((S, D), F32)
    ncores = len(xs)
    if FUSED:
        nc = _build_fused(shapes)
        in_maps = [dict(g, x_in=np.ascontiguousarray(xs[c]))
                   for c in range(ncores)]
        res = run_bass_kernel_spmd(nc, in_maps, core_ids=list(range(ncores)))
        return np.stack([np.asarray(res.results[c]["y"])
                         for c in range(ncores)], 0)
    cur = xs
    ncs = {}
    for which, ci in [("ffn0", 0), ("mamba", None), ("ffn0", 0), ("mix1", 0),
                      ("mla", None), ("mix1", 0)]:
        if which not in ncs:
            ncs[which] = _build_one(which, ci, shapes)
        nc = ncs[which]
        in_maps = [dict(g, x_in=np.ascontiguousarray(cur[c]))
                   for c in range(ncores)]
        res = run_bass_kernel_spmd(nc, in_maps, core_ids=list(range(ncores)))
        cur = [np.asarray(res.results[c]["x_out"]) for c in range(ncores)]
    return np.stack(cur, 0)


# version 03
# speedup vs baseline: 1.1162x; 1.0906x over previous
import numpy as np
import ml_dtypes

import concourse.bass as bass
import concourse.mybir as mybir

F32 = mybir.dt.float32
BF16 = mybir.dt.bfloat16
I32 = mybir.dt.int32
U8 = mybir.dt.uint8
AF = mybir.ActivationFunctionType
ALU = mybir.AluOpType
AX = mybir.AxisListType
BF = ml_dtypes.bfloat16

ENGINES = ("sync", "tensor", "vector", "scalar", "gpsimd")

S = 2048
D = 2048
HID = 8192
NT = S // 128
ND = D // 128
NH = HID // 128
EPS_LN = 1e-5
EPS_RMS = 1e-6

SET_NL_EXP = "natural_log_exp_and_others"
SET_GELU_T = "gelu_apprx_tanh_and_others"

ARENA_B = 131072


class Chan:
    __slots__ = ("sem", "count", "name", "owner")

    def __init__(self, sem, name):
        self.sem = sem
        self.count = 0
        self.name = name
        self.owner = None

    def inc(self, amt=1):
        self.count += amt
        return _Inc(self, self.count, amt)

    def at(self, val=None):
        return _Wait(self, self.count if val is None else val)


class _Inc:
    __slots__ = ("chan", "after", "amt")

    def __init__(self, chan, after, amt):
        self.chan, self.after, self.amt = chan, after, amt

    @property
    def done(self):
        return _Wait(self.chan, self.after)


class _Wait:
    __slots__ = ("chan", "val")

    def __init__(self, chan, val):
        self.chan, self.val = chan, val


class _Item:
    __slots__ = ("fn", "waits", "incs", "tag")

    def __init__(self, fn, waits, incs, tag):
        self.fn, self.waits, self.incs, self.tag = fn, waits, incs, tag


def _as_waits(w):
    if w is None:
        return []
    if isinstance(w, (_Wait, _Inc)):
        w = [w]
    out = []
    for x in w:
        if x is None:
            continue
        if isinstance(x, _Inc):
            x = x.done
        assert isinstance(x, _Wait), f"bad wait token {x!r}"
        out.append(x)
    return out


class Prog:
    def __init__(self, nc):
        self.nc = nc
        self.items = {e: [] for e in ENGINES}
        self.n_ops = 0
        self._chans = {}

    def chan(self, name):
        if name not in self._chans:
            self._chans[name] = Chan(self.nc.alloc_semaphore(name=name), name)
        return self._chans[name]

    def op(self, engine, fn, waits=None, incs=None, tag=""):
        waits = _as_waits(waits)
        if incs is None:
            incs = []
        elif isinstance(incs, _Inc):
            incs = [incs]
        for w in waits:
            assert w.val <= w.chan.count, (
                f"op {tag!r} on {engine} waits {w.chan.name}>={w.val} "
                f"but only {w.chan.count} planned")
        for x in incs:
            ch = x.chan
            if ch.owner is None:
                ch.owner = engine
            assert ch.owner == engine, (
                f"chan {ch.name} inc'd from {engine} but owned by {ch.owner} "
                f"(op {tag!r}) — single-engine channels only")
        self.items[engine].append(_Item(fn, waits, incs, tag))
        self.n_ops += 1

    def _emit(self, eng, items):
        for it in items:
            for w in it.waits:
                eng.wait_ge(w.chan.sem, w.val)
            inst = it.fn(eng)
            if it.incs:
                if inst is not None:
                    inst.then_inc(it.incs[0].chan.sem, it.incs[0].amt)
                    rest = it.incs[1:]
                else:
                    rest = it.incs
                for x in rest:
                    eng.sem_inc(x.chan.sem, x.amt)

    def build(self):
        with self.nc.Block() as block:
            for ename in ENGINES:
                items = self.items[ename]
                if not items:
                    continue

                def make(items):
                    def f(eng):
                        self._emit(eng, items)
                    return f

                getattr(block, ename)(make(items))
        return self.nc


def _bf(a):
    return np.ascontiguousarray(np.asarray(a)).astype(BF)


def _c(a):
    return np.ascontiguousarray(np.asarray(a))


def prepare_inputs(inp):
    g = {}
    f32 = np.float32

    g["ident_bf"] = _bf(np.eye(128, dtype=f32))
    g["ones_row_bf"] = _bf(np.ones((1, 128), dtype=f32))
    g["ones_col_bf"] = _bf(np.ones((128, 1), dtype=f32))
    tri = np.where(np.arange(128)[None, :] > np.arange(128)[:, None],
                   np.float32(-1e30), np.float32(0.0))
    g["mask_tri_bf"] = _bf(tri)
    P = np.zeros((128, 128), dtype=f32)
    for i in range(0, 128, 2):
        P[i, i + 1] = 1.0
        P[i + 1, i] = 1.0
    g["pswap_bf"] = _bf(P)
    selb = np.zeros((16, 16 * 128), dtype=f32)
    for n in range(16):
        selb[n, 128 * n:128 * (n + 1)] = 1.0
    g["sel16_bf"] = _bf(selb)
    ROPE = 64
    freqs = 1.0 / 10000.0 ** (np.arange(0, ROPE, 2, dtype=f32) / ROPE)
    ang = np.outer(np.arange(S, dtype=f32), freqs)
    cos64 = np.repeat(np.cos(ang), 2, axis=1).T
    sin64 = np.repeat(np.sin(ang), 2, axis=1).T
    sgn = np.tile(np.array([-1.0, 1.0], dtype=f32), 32)[:, None]
    g["ropecos_bf"] = _bf(np.concatenate([cos64, cos64], 0))
    g["ropesin_bf"] = _bf(np.concatenate([sin64 * sgn, sin64 * sgn], 0))
    m4 = np.zeros((128, 4 * 512), dtype=f32)
    rr = np.arange(128)[:, None]
    cc = np.arange(512)[None, :]
    for q in range(4):
        m4[:, 512 * q:512 * (q + 1)] = np.where(cc > q * 128 + rr,
                                                np.float32(-1e30), 0.0)
    g["mask4_bf"] = _bf(m4)

    g["wupq"] = _c(inp["L0_ffn_w_up_q"]).astype(np.int32)
    g["wupsc"] = _c(inp["L0_ffn_w_up_scale"]).astype(f32)
    g["bup_pd"] = _c(np.asarray(inp["L0_ffn_b_up"], f32).reshape(NH, 128).T)
    g["l0ua"] = _c(inp["L0_ffn_w_up_lora_a"]).astype(f32)
    g["l0ub"] = _c(inp["L0_ffn_w_up_lora_b"]).astype(f32)
    g["wdnq"] = _c(inp["L0_ffn_w_down_q"]).astype(np.int32)
    g["wdnsc"] = _c(inp["L0_ffn_w_down_scale"]).astype(f32)
    g["bdn_row"] = _c(np.asarray(inp["L0_ffn_b_down"], f32)[None, :])
    g["l0da"] = _c(inp["L0_ffn_w_down_lora_a"]).astype(f32)
    g["l0db"] = _c(inp["L0_ffn_w_down_lora_b"]).astype(f32)

    W_x = np.asarray(inp["L0_attn_W_x"], f32)
    g["wx3T"] = _c(W_x.T)
    g["wdtT"] = _c(np.asarray(inp["L0_attn_W_dt"], f32).T)
    g["bdt_pd"] = _c(np.asarray(inp["L0_attn_b_dt"], f32).reshape(ND, 128).T)
    g["alog"] = _c(inp["L0_attn_A_log"]).astype(f32)
    g["d_pd"] = _c(np.asarray(inp["L0_attn_D"], f32).reshape(ND, 128).T)

    wq = np.asarray(inp["L1_attn_wq"], f32)
    QKH, NOPE = 192, 128
    perm = [h * QKH + i for h in range(16) for i in range(NOPE)]
    perm += [h * QKH + NOPE + i for h in range(16) for i in range(64)]
    g["wqTr"] = _c(wq[perm].T)
    wkva = np.asarray(inp["L1_attn_wkv_a"], f32)
    g["wkvaTx"] = _c(np.concatenate([wkva[:512], wkva[512:], wkva[512:]], 0).T)
    wkvb = np.asarray(inp["L1_attn_wkv_b"], f32).reshape(16, 256, 512)
    g["wkvbn"] = _c(wkvb[:, :128, :].reshape(2048, 512))
    g["wkvbvT"] = _c(wkvb[:, 128:, :].transpose(2, 0, 1).reshape(512, 2048))
    g["woT"] = _c(np.asarray(inp["L1_attn_wo"], f32).T)
    g["kvw_pd"] = _c(np.asarray(inp["L1_attn_kv_norm_w"], f32).reshape(4, 128).T)

    g["sbias_row"] = _c(np.asarray(inp["L1_ffn_simple_bias"], f32)[None, :])
    g["wdT1"] = _c(np.asarray(inp["L1_ffn_w_down"], f32).T)
    g["wupT1"] = _c(np.asarray(inp["L1_ffn_w_up"], f32).T)
    g["wrT"] = _c(np.asarray(inp["L1_ffn_w_router"], f32).T)
    for nm, key in [("ln1s", "L1_ffn_ln1_scale"), ("ln1b", "L1_ffn_ln1_bias"),
                    ("ln2s", "L1_ffn_ln2_scale"), ("ln2b", "L1_ffn_ln2_bias")]:
        g[nm + "_pd"] = _c(np.asarray(inp[key], f32).reshape(ND, 128).T)
    g["wqT2"] = _c(np.asarray(inp["L1_ffn_wq"], f32).T)
    g["wkT2"] = _c(np.asarray(inp["L1_ffn_wk"], f32).T)
    g["wvT2"] = _c(np.asarray(inp["L1_ffn_wv"], f32).T)
    g["woT2"] = _c(np.asarray(inp["L1_ffn_wo"], f32).T)
    g["wff1T"] = _c(np.asarray(inp["L1_ffn_w_ff1"], f32).T)
    g["wff2T"] = _c(np.asarray(inp["L1_ffn_w_ff2"], f32).T)

    xs = [np.ascontiguousarray(np.asarray(inp["L0_attn_x"][c], f32))
          for c in range(np.asarray(inp["L0_attn_x"]).shape[0])]
    return g, xs


DT_OF = {np.dtype(np.float32): F32, np.dtype(BF): BF16,
         np.dtype(np.int32): I32}


def shapes_of(d):
    return {k: (v.shape, DT_OF[v.dtype]) for k, v in d.items()}


class KB:
    def __init__(self, nc, shapes):
        self.nc = nc
        self.p = Prog(nc)
        self.din = {}
        for name, (shape, dt) in shapes.items():
            self.din[name] = nc.dram_tensor(
                name, list(shape), dt, kind="ExternalInput").ap()
        self.scr = {}
        self.cur_act_set = None
        self._flags = {}
        self._mk()

    dbg = frozenset()

    def dram(self, name, shape, dt):
        if name not in self.scr:
            kind = "ExternalOutput" if name in self.dbg else "Internal"
            self.scr[name] = self.nc.dram_tensor(
                name, list(shape), dt, kind=kind).ap()
        return self.scr[name]

    def dram_out(self, name, shape, dt):
        self.scr[name] = self.nc.dram_tensor(
            name, list(shape), dt, kind="ExternalOutput").ap()
        return self.scr[name]

    def _mk(self):
        nc, p = self.nc, self.p
        self.PSB = nc.alloc_psum_tensor("psb", [128, 2048], F32)
        self.PSS = [nc.alloc_psum_tensor(f"pss{i}", [128, 512], F32)
                    for i in range(4)]
        self.ARENA = nc.alloc_sbuf_tensor("ARENA", [128, ARENA_B], U8,
                                          is_transpose_dest=True)
        self.XT = [nc.alloc_sbuf_tensor(f"XT{i}", [128, 2048], F32)
                   for i in range(2)]
        self.WB = [nc.alloc_sbuf_tensor(f"WB{i}", [128, 2048], BF16,
                                        is_transpose_dest=True)
                   for i in range(2)]
        self.TMP = [nc.alloc_sbuf_tensor("TMP0", [128, 2048], F32)]
        self.SM = nc.alloc_sbuf_tensor("SM", [128, 512], F32)
        self.SMB = nc.alloc_sbuf_tensor("SMB", [128, 512], BF16)
        self.TUP = nc.alloc_sbuf_tensor("TUP", [16, 2048], BF16)
        self.TDN = nc.alloc_sbuf_tensor("TDN", [16, 2048], BF16)
        self.ROW2 = nc.alloc_sbuf_tensor("ROW2", [1, 2048], BF16)
        self.ident = nc.alloc_sbuf_tensor("identc", [128, 128], BF16)
        self.onesr = nc.alloc_sbuf_tensor("onesr", [1, 128], BF16)
        self.onesc = nc.alloc_sbuf_tensor("onesc", [128, 1], BF16)
        self.ldc = p.chan("ldconst")
        for nm, t in [("ident_bf", self.ident), ("ones_row_bf", self.onesr),
                      ("ones_col_bf", self.onesc)]:
            p.op("sync", lambda e, nm=nm, t=t: e.dma_start(
                    out=t[:], in_=self.din[nm]),
                 incs=self.ldc.inc(16), tag=f"ldc_{nm}")
        self.const_ready = self.ldc.at()

    def carve(self, off_b, n, dt):
        size = {F32: 4, BF16: 2, I32: 4}[dt]
        assert off_b % 32 == 0 and off_b + n * size <= ARENA_B, (off_b, n, dt)
        return self.ARENA[:, off_b:off_b + n * size].bitcast(dt)

    def xfm(self, i):
        return self.carve(4096 * i, 2048, BF16)


    def ensure_msk4(self):
        if hasattr(self, "MSK4"):
            return self._flags["msk4_ready"]
        nc, p = self.nc, self.p
        self.MSK4 = nc.alloc_sbuf_tensor("MSK4", [128, 2048], BF16)
        ch = p.chan("msk4")
        p.op("sync", lambda e: e.dma_start(out=self.MSK4[:],
                                           in_=self.din["mask4_bf"]),
             incs=ch.inc(16), tag="ld_msk4")
        self._flags["msk4_ready"] = ch.at()
        return self._flags["msk4_ready"]

    def act_set(self, set_name, waits=None):
        if self.cur_act_set == set_name:
            return
        self.cur_act_set = set_name
        from concourse.hw_specs import get_activation_tables
        tabs = list(get_activation_tables(self.nc.m.arch).keys())
        idx = tabs.index(set_name)

        def fn(e, idx=idx):
            return e.add_instruction(mybir.InstLoadActFuncSet(
                name=e.bass.get_next_instruction_name(),
                act_func_set_id=idx, ins=[], outs=[]))
        self.p.op("scalar", fn, waits=waits, tag=f"actset_{set_name}")

    def cast_weight(self, name, grp="g", waits=None):
        key = name + "_bf"
        if key in self.scr:
            return self.scr[key], self._flags.get(key)
        src = self.din[name]
        dst = self.dram(key, src.shape, BF16)
        ch = self.p.chan("cast_" + grp)
        self.p.op("gpsimd", lambda e: e.dma_start(out=dst, in_=src),
                  waits=waits, incs=ch.inc(16), tag=f"cast_{name}")
        self._flags.setdefault("_castgrp", {}).setdefault(grp, []).append(key)
        return dst, None

    def cast_group_done(self, grp):
        ch = self.p.chan("cast_" + grp)
        tok = ch.at()
        for key in self._flags.get("_castgrp", {}).get(grp, []):
            self._flags[key] = tok
        return tok

    def once(self, key):
        if key in self._flags:
            return False
        self._flags[key] = True
        return True


def emit_ln(kb, x_in, xn_tm, tag, start=None, inner_stats=None):
    p = kb.p
    ld = [p.chan(tag + "_ld0"), p.chan(tag + "_ld1")]
    stc = [p.chan(tag + "_st0"), p.chan(tag + "_st1")]
    trc = p.chan(tag + "_tr")
    cs = p.chan(tag + "_cs")
    cv = p.chan(tag + "_cv")
    SM = kb.SM
    start = _as_waits(start)
    xt_free = [None, None]
    wb_free = [None, None]
    inner_done = []
    for i in range(NT):
        b = i % 2
        xt = kb.XT[b]
        p.op("sync", lambda e, xt=xt, i=i: e.dma_start(
                out=xt[:], in_=x_in[128 * i:128 * (i + 1), :]),
             waits=(start if i < 2 else []) + _as_waits(xt_free[b]),
             incs=(ldi := ld[b].inc(16)), tag=f"{tag}_ld{i}")
        sum_c = SM[:, i:i + 1]
        sq_c = SM[:, 16 + i:17 + i]
        p.op("vector", lambda e, xt=xt, o=sum_c: e.reduce_sum(o, xt[:], AX.X),
             waits=[ldi.done], incs=(esum := cv.inc()), tag=f"{tag}_sum{i}")
        p.op("scalar", lambda e, xt=xt, o=sq_c: e.activation(
                kb.TMP[0][:], xt[:], AF.Square, accum_out=o),
             waits=[ldi.done], incs=(esq := cs.inc()), tag=f"{tag}_sq{i}")
        mean_c = SM[:, 32 + i:33 + i]
        m2_c = SM[:, 96 + i:97 + i]
        var_c = SM[:, 112 + i:113 + i]
        lnv_c = SM[:, 128 + i:129 + i]
        rstd_c = SM[:, 48 + i:49 + i]
        nb_c = SM[:, 64 + i:65 + i]
        p.op("vector", lambda e, a=sum_c, o=mean_c: e.tensor_scalar_mul(
                o, a, 1.0 / D), waits=[esum.done], incs=(em := cv.inc()))
        p.op("scalar", lambda e, a=mean_c, o=m2_c: e.activation(
                o, a, AF.Square), waits=[em.done], incs=(em2 := cs.inc()))
        p.op("vector", lambda e, a=sq_c, b2=m2_c, o=var_c: e.scalar_tensor_tensor(
                o, a, 1.0 / D, b2, ALU.mult, ALU.subtract),
             waits=[esq.done, em2.done], incs=(ev0 := cv.inc()))
        p.op("vector", lambda e, o=var_c: e.tensor_scalar_add(o, o, EPS_LN),
             waits=[ev0.done], incs=(ev := cv.inc()))
        p.op("scalar", lambda e, a=var_c, o=lnv_c: e.activation(
                o, a, AF.Ln), waits=[ev.done],
             incs=(el := cs.inc()))
        p.op("scalar", lambda e, a=lnv_c, o=rstd_c: e.activation(
                o, a, AF.Exp, scale=-0.5), waits=[el.done],
             incs=(er := cs.inc()))
        p.op("vector", lambda e, a=mean_c, r=rstd_c, o=nb_c: e.scalar_tensor_tensor(
                o, a, -1.0, r, ALU.mult, ALU.mult),
             waits=[er.done], incs=(en := cv.inc()))
        wbuf = kb.WB[b]
        p.op("scalar", lambda e, xt=xt, r=rstd_c, nb=nb_c, o=wbuf: e.activation(
                o[:], xt[:], AF.Identity, bias=nb, scale=r),
             waits=[en.done] + _as_waits(wb_free[b]),
             incs=(ecp := cs.inc()), tag=f"{tag}_norm{i}")
        xt_free[b] = [ecp.done]
        extra = []
        if inner_stats is not None:
            s_c, q_c = inner_stats(i)
            p.op("vector", lambda e, wb=wbuf, o=s_c: e.reduce_sum(o, wb[:], AX.X),
                 waits=[ecp.done], incs=(eis := cv.inc()))
            p.op("scalar", lambda e, wb=wbuf, o=q_c: e.activation(
                    kb.TMP[0][:], wb[:], AF.Square, accum_out=o),
                 waits=[ecp.done], incs=(eiq := cs.inc()))
            extra = [eis.done, eiq.done]
            inner_done.append([eis.done, eiq.done])
        p.op("sync", lambda e, wbuf=wbuf, i=i: e.dma_start(
                out=xn_tm[128 * i:128 * (i + 1), :], in_=wbuf[:]),
             waits=[ecp.done], incs=(est := stc[b].inc(16)), tag=f"{tag}_st{i}")
        wb_free[b] = [est.done] + extra
    all_st = [stc[0].at(), stc[1].at()]
    fm_ready = []
    for dt in range(ND):
        p.op("sync", lambda e, dt=dt: e.dma_start_transpose(
                kb.xfm(dt), xn_tm[:, 128 * dt:128 * (dt + 1)]),
             waits=all_st if dt == 0 else None,
             incs=(etr := trc.inc(16)), tag=f"{tag}_tr{dt}")
    fm_all = trc.at()
    fm_ready = [fm_all] * ND
    return dict(fm_ready=fm_ready, tm_done=all_st, inner_done=inner_done,
                rstd=lambda i: kb.SM[:, 48 + i:49 + i],
                negb=lambda i: kb.SM[:, 64 + i:65 + i])


def emit_dequant(kb, start=None):
    p = kb.p
    wup_bf = kb.dram("wup_bf", (HID, D), BF16)
    wdn_bf = kb.dram("wdn_bf", (D, HID), BF16)
    cd = p.chan("dq_d")
    cqi = [p.chan("dq_i0"), p.chan("dq_i1")]
    cst = [p.chan("dq_s0"), p.chan("dq_s1")]
    cst2 = [p.chan("dq_s2"), p.chan("dq_s3")]
    cq = p.chan("dq_v")
    QI = [kb.carve(65536 + 8192 * i, 2048, I32) for i in range(2)]
    QF = kb.carve(65536 + 16384, 2048, F32)
    QS = kb.carve(65536 + 24576, 128, F32)
    QO = [kb.carve(65536 + 25088 + 4096 * i, 2048, BF16) for i in range(2)]
    qi_free = [None, None]
    qo_free = [None, None]
    sc_free = [None]

    def one(mat_q, mat_sc, out_bf, rows, cols, scdim, tagp, cst=cst):
        ncolt = cols // 2048
        scblk = scdim // ncolt
        for r in range(rows // 128):
            p.op("sync", lambda e, r=r: e.dma_start(
                    out=QS[:, 0:scdim],
                    in_=mat_sc[128 * r:128 * (r + 1), :]),
                 waits=_as_waits(sc_free[0]) + (_as_waits(start) if r == 0 else []),
                 incs=(lsc := cd.inc(16)), tag=f"{tagp}_sc{r}")
            last_mul = None
            for c in range(ncolt):
                b = (r * ncolt + c) % 2
                p.op("sync", lambda e, r=r, c=c, b=b: e.dma_start(
                        out=QI[b], in_=mat_q[128 * r:128 * (r + 1),
                                             2048 * c:2048 * (c + 1)]),
                     waits=qi_free[b], incs=(lqi := cqi[b].inc(16)),
                     tag=f"{tagp}_qi")
                p.op("vector", lambda e, b=b: e.tensor_scalar_add(
                        QF, QI[b], -7.5),
                     waits=[lqi.done], incs=(e1 := cq.inc()), tag=f"{tagp}_sub")
                qi_free[b] = [e1.done]

                def mulfn(e, c=c, b=b, scblk=scblk):
                    in0 = QF.rearrange("p (a v) -> p a v", v=64)
                    in1 = QS[:, scblk * c:scblk * (c + 1)].unsqueeze(
                        2).broadcast_to((128, 32, 64))
                    out = QO[b].rearrange("p (a v) -> p a v", v=64)
                    return e.tensor_tensor(out, in0, in1, ALU.mult)
                p.op("vector", mulfn,
                     waits=[e1.done, lsc.done] + _as_waits(qo_free[b]),
                     incs=(e2 := cq.inc()), tag=f"{tagp}_mul")
                last_mul = e2
                p.op("sync", lambda e, r=r, c=c, b=b: e.dma_start(
                        out=out_bf[128 * r:128 * (r + 1),
                                   2048 * c:2048 * (c + 1)], in_=QO[b]),
                     waits=[e2.done], incs=(e3 := cst[b].inc(16)),
                     tag=f"{tagp}_st")
                qo_free[b] = [e3.done]
            sc_free[0] = [last_mul.done]

    one(kb.din["wupq"], kb.din["wupsc"], wup_bf, HID, D, 32, "dqu", cst)
    up_done = [cst[0].at(), cst[1].at()]
    one(kb.din["wdnq"], kb.din["wdnsc"], wdn_bf, D, HID, 128, "dqd", cst2)
    return dict(wup_bf=wup_bf, wdn_bf=wdn_bf, done_up=up_done,
                done=up_done + [cst2[0].at(), cst2[1].at()])


def emit_ffn0(kb, x_in, x_out, ci, start=None):
    p = kb.p
    kb.act_set(SET_NL_EXP, waits=None)

    if kb.once("ffn0_setup"):
        dq = emit_dequant(kb, start=None)
        kb._flags["dq"] = dq
        sld = p.chan("ffn0_sld")
        lua_bf, _ = kb.cast_weight("l0ua", "f")
        lub_bf, _ = kb.cast_weight("l0ub", "f")
        dla_bf, _ = kb.cast_weight("l0da", "f")
        dlb_bf, _ = kb.cast_weight("l0db", "f")
        bdn_bf, _ = kb.cast_weight("bdn_row", "f")
        w1 = w3 = w4 = kb.cast_group_done("f")
        nc = kb.nc
        kb.LUA = nc.alloc_sbuf_tensor("LUA", [128, 256], BF16)
        kb.DLA = nc.alloc_sbuf_tensor("DLA", [128, 1024], BF16)
        kb.DLB = nc.alloc_sbuf_tensor("DLB", [16, 2048], BF16)
        kb.BUP = nc.alloc_sbuf_tensor("BUP", [128, 64], F32)
        p.op("sync", lambda e: e.dma_start(
                out=kb.LUA[:].rearrange("p (k r) -> p k r", r=16),
                in_=lua_bf.rearrange("(k p) r -> p k r", p=128)),
             waits=[w1], incs=sld.inc(16))
        p.op("sync", lambda e: e.dma_start(
                out=kb.DLA[:].rearrange("p (k r) -> p k r", r=16),
                in_=dla_bf.rearrange("(k p) r -> p k r", p=128)),
             waits=[w3], incs=sld.inc(16))
        p.op("sync", lambda e: e.dma_start(out=kb.DLB[:], in_=dlb_bf),
             waits=[w4], incs=sld.inc(16))
        p.op("sync", lambda e: e.dma_start(out=kb.BUP[:], in_=kb.din["bup_pd"]),
             incs=sld.inc(16))
        kb._flags["ffn0_small"] = sld.at()
    dq = kb._flags["dq"]
    small_ready = kb._flags["ffn0_small"]

    xn_tm = kb.dram(f"xn_tm", (S, D), BF16)
    g_nat = kb.dram(f"ffn_g", (HID, D), BF16)
    ln = emit_ln(kb, x_in, xn_tm, "LN", start=start)
    LUB = kb.carve(106496, 8192, BF16)[0:16, :]
    lubc = p.chan("f_lub")
    p.op("sync", lambda e: e.dma_start(out=LUB, in_=kb.scr["l0ub_bf"]),
         waits=[kb._flags["l0ub_bf"]] + _as_waits(start),
         incs=(lub_ld := lubc.inc(16)), tag=f"f{ci}lubld")

    mmc = p.chan("g_mm")
    evc = p.chan("g_ev")
    wc = [p.chan("f_wl0"), p.chan("f_wl1")]
    wrc = p.chan("f_wr")
    gc = [p.chan("f_g0"), p.chan("f_g1")]
    gl_c = [p.chan("f_gl0"), p.chan("f_gl1")]
    tdc = [p.chan("f_td0"), p.chan("f_td1")]
    dnc = [p.chan("f_dn0"), p.chan("f_dn1")]
    dnv = p.chan("g_dv")
    xoc = p.chan("f_xo")

    for k in range(ND):
        for c in range(4):
            p.op("tensor", lambda e, k=k, c=c: e.matmul(
                    kb.PSB[0:16, 512 * c:512 * (c + 1)],
                    kb.LUA[:, 16 * k:16 * (k + 1)],
                    kb.xfm(k)[:, 512 * c:512 * (c + 1)],
                    start=(k == 0), stop=(k == ND - 1)),
                 waits=[ln["fm_ready"][k], small_ready] if c == 0 else None,
                 incs=(tup_mm := mmc.inc()) if (k == ND - 1 and c == 3) else None,
                 tag=f"f{ci}tup")
    for c in range(4):
        p.op("scalar", lambda e, c=c: e.activation(
                kb.TUP[:, 512 * c:512 * (c + 1)],
                kb.PSB[0:16, 512 * c:512 * (c + 1)], AF.Copy),
             waits=[tup_mm.done] if c == 0 else None,
             incs=(tup_ev := evc.inc()), tag=f"f{ci}tupev")
    tup_done = tup_ev.done

    UPW = [kb.carve(65536 + 16384 * i, 8192, BF16) for i in range(2)]
    GF = [kb.carve(98304 + 4096 * i, 2048, BF16) for i in range(2)]
    upw_free = [None, None]
    gf_free = [None, None]
    bank_free = {0: None, 1: [tup_done]}
    g_done_all = []
    for mq in range(NH // 4):
        wb = mq % 2
        for k in range(ND):
            p.op("sync", lambda e, mq=mq, k=k, wb=wb: e.dma_start_transpose(
                    UPW[wb][:, 512 * k:512 * (k + 1)],
                    dq["wup_bf"][512 * mq:512 * (mq + 1),
                                 128 * k:128 * (k + 1)]),
                 waits=(dq["done_up"] if mq < 2 and k == 0 else []) +
                       (_as_waits(upw_free[wb]) if k == 0 else []),
                 incs=(wl := wc[wb].inc(16)), tag=f"f{ci}upw{mq}_{k}")
        wl_done = wl.done
        for mm in range(4):
            m = 4 * mq + mm
            pb = m % 2
            banks = ([kb.PSS[c][:, 0:512] for c in range(4)] if pb == 0 else
                     [kb.PSB[:, 512 * c:512 * (c + 1)] for c in range(4)])
            for k in range(ND):
                for c in range(4):
                    w = []
                    if k == 0 and c == 0:
                        w = [wl_done] + _as_waits(bank_free[pb])
                        if mq == 0 and mm == 0:
                            w += [ln["fm_ready"][ND - 1], tup_done]
                    p.op("tensor", lambda e, k=k, c=c, m=m, wb=wb, banks=banks: e.matmul(
                            banks[c],
                            UPW[wb][:, 512 * k + 128 * (m % 4):512 * k + 128 * (m % 4) + 128],
                            kb.xfm(k)[:, 512 * c:512 * (c + 1)],
                            start=(k == 0), stop=False),
                         waits=w, tag=f"f{ci}up{m}")
            for c in range(4):
                p.op("tensor", lambda e, c=c, m=m, banks=banks, LUB=LUB: e.matmul(
                        banks[c], LUB[:, 128 * m:128 * (m + 1)],
                        kb.TUP[:, 512 * c:512 * (c + 1)],
                        start=False, stop=True),
                     waits=[lub_ld.done] if (c == 0 and m == 0) else None,
                     incs=(ul := mmc.inc()) if c == 3 else None,
                     tag=f"f{ci}uplora{m}")
            gfb = m % 2
            for c in range(4):
                p.op("scalar", lambda e, c=c, m=m, gfb=gfb, banks=banks: e.activation(
                        GF[gfb][:, 512 * c:512 * (c + 1)], banks[c],
                        AF.Relu, bias=kb.BUP[:, m:m + 1], scale=1.0),
                     waits=([ul.done] + _as_waits(gf_free[gfb])) if c == 0 else None,
                     incs=(ge := evc.inc()), tag=f"f{ci}gev{m}")
            bank_free[pb] = [ge.done]
            p.op("sync", lambda e, m=m, gfb=gfb: e.dma_start(
                    out=g_nat[128 * m:128 * (m + 1), :], in_=GF[gfb]),
                 waits=[ge.done], incs=(gs := gc[gfb].inc(16)), tag=f"f{ci}gst{m}")
            gf_free[gfb] = [gs.done]

        upw_free[wb] = [_Wait(mmc, mmc.count)]
    up_all_mm = _Wait(mmc, mmc.count)
    g_done_all = [gc[0].at(), gc[1].at()]

    td_mm_done = [None, None]
    for m in range(NH):
        b = m % 2
        p.op("sync", lambda e, m=m, b=b: e.dma_start(
                out=kb.WB[b][:], in_=g_nat[128 * m:128 * (m + 1), :]),
             waits=(g_done_all + _as_waits(bank_free[1]) if m == 0 else
                    _as_waits(td_mm_done[b])),
             incs=(gl := tdc[b].inc(16)), tag=f"f{ci}tdl{m}")
        for c in range(4):
            p.op("tensor", lambda e, m=m, b=b, c=c: e.matmul(
                    kb.PSB[0:16, 512 * c:512 * (c + 1)],
                    kb.DLA[:, 16 * m:16 * (m + 1)],
                    kb.WB[b][:, 512 * c:512 * (c + 1)],
                    start=(m == 0), stop=(m == NH - 1)),
                 waits=[gl.done, up_all_mm] if c == 0 and m == 0 else
                       ([gl.done] if c == 0 else None),
                 incs=(tdm := mmc.inc()) if c == 3 else None,
                 tag=f"f{ci}tdmm")
        td_mm_done[b] = [tdm.done]
    for c in range(4):
        p.op("scalar", lambda e, c=c: e.activation(
                kb.TDN[:, 512 * c:512 * (c + 1)],
                kb.PSB[0:16, 512 * c:512 * (c + 1)], AF.Copy),
             waits=[tdm.done] if c == 0 else None,
             incs=(tdn_ev := evc.inc()), tag=f"f{ci}tdnev")
    tdn_done = tdn_ev.done

    WRESQ = kb.carve(0, 64 * 512, BF16)
    GSB = [kb.carve(65536 + 16384 * i, 8192, BF16) for i in range(2)]
    ROW1 = kb.carve(98304, 2048, BF16)[0:1, :]
    rowc = p.chan("f_row")
    p.op("sync", lambda e: e.dma_start(out=ROW1, in_=kb.scr["bdn_row_bf"]),
         waits=[kb._flags["bdn_row_bf"]] + g_done_all,
         incs=(row_ld := rowc.inc(16)), tag=f"f{ci}rowld")
    banks8 = [kb.PSS[i][:, 0:512] for i in range(4)] + \
             [kb.PSB[:, 512 * i:512 * (i + 1)] for i in range(4)]
    bank8_free = [bank_free[0]] * 4 + [[tdn_done]] * 4
    gsb_free = [None, None]
    xt_free = [None, None]
    wres_free = None
    for qtr in range(4):
        for k in range(NH):
            p.op("sync", lambda e, qtr=qtr, k=k: e.dma_start_transpose(
                    WRESQ[:, 512 * k:512 * (k + 1)],
                    dq["wdn_bf"][512 * qtr:512 * (qtr + 1),
                                 128 * k:128 * (k + 1)]),
                 waits=([up_all_mm, tdn_done] + _as_waits(wres_free) +
                        (dq["done"] if qtr == 0 and k == 0 else []))
                       if k == 0 else None,
                 incs=(wr := wrc.inc(16)), tag=f"f{ci}dw{qtr}_{k}")
        wres_ready = wr.done
        for j in range(NT):
            gb = j % 2
            p.op("gpsimd", lambda e, j=j, gb=gb: e.dma_start(
                    out=GSB[gb].rearrange("p (k t) -> p k t", t=128),
                    in_=g_nat[:, 128 * j:128 * (j + 1)].rearrange(
                        "(k p) t -> p k t", p=128)),
                 waits=(g_done_all if (qtr == 0 and j < 2) else []) +
                       _as_waits(gsb_free[gb]),
                 incs=(gl2 := gl_c[gb].inc(16)), tag=f"f{ci}gsl{qtr}_{j}")
            bk = (qtr * NT + j) % 8
            for k in range(NH):
                w = []
                if k == 0:
                    w = [gl2.done, wres_ready] + _as_waits(bank8_free[bk])
                p.op("tensor", lambda e, k=k, gb=gb, bk=bk: e.matmul(
                        banks8[bk], GSB[gb][:, 128 * k:128 * (k + 1)],
                        WRESQ[:, 512 * k:512 * (k + 1)],
                        start=(k == 0), stop=False),
                     waits=w, tag=f"f{ci}dn{qtr}_{j}")
            p.op("tensor", lambda e, j=j, qtr=qtr, bk=bk: e.matmul(
                    banks8[bk], kb.TDN[:, 128 * j:128 * (j + 1)],
                    kb.DLB[:, 512 * qtr:512 * (qtr + 1)],
                    start=False, stop=False),
                 waits=[tdn_done] if (qtr == 0 and j == 0) else None,
                 tag=f"f{ci}dnlora")
            p.op("tensor", lambda e, qtr=qtr, bk=bk, ROW1=ROW1: e.matmul(
                    banks8[bk], kb.onesr[:],
                    ROW1[:, 512 * qtr:512 * (qtr + 1)],
                    start=False, stop=True),
                 waits=[row_ld.done] if (qtr == 0 and j == 0) else None,
                 incs=(dmm := mmc.inc()), tag=f"f{ci}dnbias")
            gsb_free[gb] = [_Wait(mmc, mmc.count)]
            xb = j % 2
            p.op("sync", lambda e, j=j, qtr=qtr, xb=xb: e.dma_start(
                    out=kb.XT[xb][:, 0:512],
                    in_=x_in[128 * j:128 * (j + 1),
                             512 * qtr:512 * (qtr + 1)]),
                 waits=_as_waits(xt_free[xb]), incs=(xl := dnc[xb].inc(16)),
                 tag=f"f{ci}xl{qtr}_{j}")
            p.op("vector", lambda e, xb=xb, bk=bk: e.scalar_tensor_tensor(
                    kb.XT[xb][:, 512:1024], banks8[bk], 0.5,
                    kb.XT[xb][:, 0:512], ALU.mult, ALU.add),
                 waits=[dmm.done, xl.done], incs=(add := dnv.inc()),
                 tag=f"f{ci}add{qtr}_{j}")
            bank8_free[bk] = [add.done]
            p.op("sync", lambda e, j=j, qtr=qtr, xb=xb: e.dma_start(
                    out=x_out[128 * j:128 * (j + 1),
                              512 * qtr:512 * (qtr + 1)],
                    in_=kb.XT[xb][:, 512:1024]),
                 waits=[add.done], incs=(xo := xoc.inc(16)),
                 tag=f"f{ci}xo{qtr}_{j}")
            xt_free[xb] = [add.done]
        wres_free = [_Wait(mmc, mmc.count)]
    return dict(done=xoc.at(), xoc=xoc)


def emit_mamba(kb, x_in, x_out, start=None):
    p, nc = kb.p, kb.nc
    kb.act_set(SET_NL_EXP)

    if kb.once("mamba_setup"):
        sld = p.chan("f_sld")
        sldv = p.chan("g_dv")
        slds = p.chan("g_ev")
        wx3_bf, _ = kb.cast_weight("wx3T", "mb")
        wdt_bf, _ = kb.cast_weight("wdtT", "mb")
        w1 = w2 = kb.cast_group_done("mb")
        kb.MWX = nc.alloc_sbuf_tensor("MWX", [128, 16 * 96], BF16)
        kb.MWDT = nc.alloc_sbuf_tensor("MWDT", [64, 2048], BF16)
        kb.MBDT = nc.alloc_sbuf_tensor("MBDT", [128, 16], F32)
        kb.MA = nc.alloc_sbuf_tensor("MA", [128, 256], F32)
        kb.MD = nc.alloc_sbuf_tensor("MD", [128, 16], F32)
        kb.SEL = nc.alloc_sbuf_tensor("SEL", [16, 2048], BF16)
        p.op("sync", lambda e: e.dma_start(
                out=kb.MWX[:].rearrange("p (k m) -> p k m", m=96),
                in_=wx3_bf.rearrange("(k p) m -> p k m", p=128)),
             waits=[w1], incs=sld.inc(16))
        p.op("sync", lambda e: e.dma_start(out=kb.MWDT[:], in_=wdt_bf),
             waits=[w2], incs=sld.inc(16))
        p.op("sync", lambda e: e.dma_start(out=kb.MBDT[:], in_=kb.din["bdt_pd"]),
             incs=sld.inc(16))
        p.op("sync", lambda e: e.dma_start(
                out=kb.MA[:].rearrange("p (k n) -> p k n", n=16),
                in_=kb.din["alog"].rearrange("(k p) n -> p k n", p=128)),
             incs=(la := sld.inc(16)))
        p.op("sync", lambda e: e.dma_start(out=kb.MD[:], in_=kb.din["d_pd"]),
             incs=sld.inc(16))
        p.op("sync", lambda e: e.dma_start(out=kb.SEL[:], in_=kb.din["sel16_bf"]),
             incs=sld.inc(16))
        p.op("scalar", lambda e: e.activation(kb.MA[:], kb.MA[:], AF.Exp),
             waits=[sld.at()], incs=(ea := slds.inc()))
        p.op("vector", lambda e: e.tensor_scalar_mul(kb.MA[:], kb.MA[:], -1.0),
             waits=[ea.done], incs=(eneg := sldv.inc()))
        kb._flags["mb_small"] = [sld.at(), sldv.at()]
    small = kb._flags["mb_small"]

    xn_tm = kb.dram("mb_xn_tm", (S, D), BF16)
    y_fm = kb.dram("mb_y_fm", (D, S), BF16)
    ln = emit_ln(kb, x_in, xn_tm, "LN", start=start)

    DPB = kb.carve(65536, 2048, BF16)[0:64, :]
    BCT = kb.carve(69632, 2048, BF16)[0:16, :]
    CCT = kb.carve(73728, 2048, BF16)[0:16, :]
    DELTA = kb.carve(77824, 2048, F32)
    DU = kb.carve(86016, 2048, F32)
    AEXP = kb.carve(94208, 2048, F32)
    UBUF = kb.carve(102400, 2048, F32)
    HBUF = kb.carve(110592, 2048, BF16)
    YM = kb.carve(114688, 2048, BF16)
    YACC = kb.carve(118784, 2048, F32)
    YTB = kb.carve(126976, 2048, BF16)

    mm = p.chan("g_mm")
    ev = p.chan("g_ev")
    dv = p.chan("g_dv")
    yst = p.chan("f_g0")
    xoc = p.chan("f_xo")

    banksets = [(kb.PSS[0][0:64, :], kb.PSS[1][0:16, :], kb.PSS[2][0:16, :]),
                (kb.PSS[3][0:64, :], kb.PSB[0:16, 0:512], kb.PSB[0:16, 512:1024])]
    bs_free = [None, None]
    for c in range(4):
        bs = banksets[c % 2]
        for k in range(ND):
            w = []
            if k == 0:
                w = [ln["fm_ready"][k]] + small + _as_waits(bs_free[c % 2])
            else:
                w = [ln["fm_ready"][k]] if c == 0 else []
            for j, (lo, hi) in enumerate([(0, 64), (64, 80), (80, 96)]):
                p.op("tensor", lambda e, k=k, c=c, j=j, lo=lo, hi=hi, bs=bs: e.matmul(
                        bs[j], kb.MWX[:, 96 * k + lo:96 * k + hi],
                        kb.xfm(k)[:, 512 * c:512 * (c + 1)],
                        start=(k == 0), stop=(k == ND - 1)),
                     waits=w if j == 0 else None,
                     incs=(xmm := mm.inc()) if (k == ND - 1 and j == 2) else None,
                     tag=f"mb_xp{c}")
        for j, dst in enumerate([DPB, BCT, CCT]):
            p.op("scalar", lambda e, j=j, c=c, dst=dst, bs=bs: e.activation(
                    dst[:, 512 * c:512 * (c + 1)], bs[j], AF.Identity),
                 waits=[xmm.done] if j == 0 else None,
                 incs=(xev := ev.inc()), tag=f"mb_xpev{c}")
        bs_free[c % 2] = [xev.done]
    xp_done = xev.done

    delta_mm_free = _Wait(ev, ev.count)
    ymul_prev = None
    umul_prev = None
    scan_prev = None
    tmp0_free = None
    delta_free = None
    ytb_free = None
    kb._mb_wb_free = [None, None]
    kb._mb_xt_free = [None, None]
    for dt in range(ND):
        for c in range(4):
            p.op("tensor", lambda e, dt=dt, c=c: e.matmul(
                    kb.PSB[:, 512 * c:512 * (c + 1)],
                    kb.MWDT[:, 128 * dt:128 * (dt + 1)],
                    DPB[:, 512 * c:512 * (c + 1)],
                    start=True, stop=True),
                 waits=([xp_done, delta_mm_free] if c == 0 else None),
                 incs=(dmm := mm.inc()) if c == 3 else None, tag=f"mb_dl{dt}")
        p.op("scalar", lambda e, dt=dt: e.activation(
                kb.TMP[0][:], kb.PSB[:], AF.Identity,
                bias=kb.MBDT[:, dt:dt + 1], scale=1.0),
             waits=[dmm.done] + _as_waits(scan_prev) + _as_waits(tmp0_free),
             incs=(p0 := ev.inc()), tag=f"mb_sp0_{dt}")
        delta_mm_free = p0.done
        p.op("scalar", lambda e: e.activation(AEXP, kb.TMP[0][:], AF.Abs),
             incs=(p1 := ev.inc()))
        p.op("scalar", lambda e: e.activation(AEXP, AEXP, AF.Exp, scale=-1.0),
             incs=(p2 := ev.inc()))
        p.op("scalar", lambda e: e.activation(AEXP, AEXP, AF.Ln, bias=1.0),
             incs=(p3 := ev.inc()))
        p.op("vector", lambda e: e.tensor_scalar_max(UBUF, kb.TMP[0][:], 0.0),
             waits=[p0.done] + _as_waits(scan_prev), incs=(p4 := dv.inc()))
        tmp0_free = [p4.done]
        p.op("vector", lambda e: e.tensor_add(DELTA, UBUF, AEXP),
             waits=[p3.done, p4.done] + _as_waits(delta_free),
             incs=(p5 := dv.inc()), tag=f"mb_sp5_{dt}")
        p.op("vector", lambda e, dt=dt: e.tensor_mul(DU, DELTA, kb.xfm(dt)),
             waits=[p5.done], incs=(pdu := dv.inc()), tag=f"mb_du{dt}")
        p.op("vector", lambda e, dt=dt: e.tensor_scalar_mul(
                YACC, kb.xfm(dt), kb.MD[:, dt:dt + 1]),
             waits=[pdu.done], incs=(py0 := dv.inc()), tag=f"mb_xd{dt}")
        yadd_prev = py0
        for n in range(16):
            for c in range(4):
                p.op("tensor", lambda e, n=n, c=c: e.matmul(
                        kb.PSB[:, 512 * c:512 * (c + 1)],
                        kb.SEL[:, 128 * n:128 * (n + 1)],
                        BCT[:, 512 * c:512 * (c + 1)],
                        start=True, stop=True),
                     waits=([delta_mm_free] + _as_waits(umul_prev))
                           if c == 0 else None,
                     incs=(bmm := mm.inc()) if c == 3 else None,
                     tag=f"mb_bb{dt}_{n}")
            for c in range(4):
                p.op("tensor", lambda e, n=n, c=c: e.matmul(
                        kb.PSS[c][:], kb.SEL[:, 128 * n:128 * (n + 1)],
                        CCT[:, 512 * c:512 * (c + 1)],
                        start=True, stop=True),
                     waits=(_as_waits(ymul_prev) + ([xp_done] if dt == 0 and n == 0 else []))
                           if c == 0 else None,
                     incs=(cmm := mm.inc()) if c == 3 else None,
                     tag=f"mb_cb{dt}_{n}")
            p.op("scalar", lambda e, dt=dt, n=n: e.activation(
                    AEXP, DELTA, AF.Exp, scale=kb.MA[:, 16 * dt + n:16 * dt + n + 1]),
                 waits=[p5.done] if n == 0 else _as_waits(scan_prev),
                 incs=(pa := ev.inc()), tag=f"mb_a{dt}_{n}")
            if n == 15:
                delta_free = [pa.done]
            p.op("vector", lambda e: e.tensor_mul(UBUF, DU, kb.PSB[:]),
                 waits=[bmm.done, pdu.done] + _as_waits(scan_prev),
                 incs=(pu := dv.inc()), tag=f"mb_u{dt}_{n}")
            umul_prev = [pu.done]
            delta_mm_free = pu.done
            p.op("vector", lambda e: e.tensor_tensor_scan(
                    HBUF, AEXP, UBUF, 0.0, ALU.mult, ALU.add),
                 waits=[pa.done] + _as_waits(ymul_prev),
                 incs=(ph := dv.inc()), tag=f"mb_h{dt}_{n}")
            scan_prev = [ph.done]
            for c in range(4):
                p.op("vector", lambda e, c=c: e.tensor_mul(
                        YM[:, 512 * c:512 * (c + 1)], HBUF[:, 512 * c:512 * (c + 1)],
                        kb.PSS[c][:]),
                     waits=[ph.done, cmm.done] if c == 0 else None,
                     incs=(pm := dv.inc()), tag=f"mb_ym{dt}_{n}")
            ymul_prev = [pm.done]
            p.op("vector", lambda e: e.tensor_add(YACC, YACC, YM),
                 waits=[pm.done, yadd_prev.done], incs=(pya := dv.inc()),
                 tag=f"mb_ya{dt}_{n}")
            yadd_prev = pya
        p.op("vector", lambda e: e.tensor_copy(YTB, YACC),
             waits=[pya.done] + _as_waits(ytb_free),
             incs=(pyt := dv.inc()), tag=f"mb_yt{dt}")
        p.op("sync", lambda e, dt=dt: e.dma_start(
                out=y_fm[128 * dt:128 * (dt + 1), :], in_=YTB),
             waits=[pyt.done], incs=(ys := yst.inc(16)), tag=f"mb_yst{dt}")
        ytb_free = [ys.done]
    y_all = yst.at()

    trl = [p.chan("LN_tr"), p.chan("f_td1")]
    xlc = [p.chan("f_dn0"), p.chan("f_dn1")]
    add_prev = None
    for j in range(NT):
        wbuf = kb.WB[j % 2]
        p.op("sync", lambda e, j=j, wbuf=wbuf: e.dma_start_transpose(
                wbuf[:], y_fm[:, 128 * j:128 * (j + 1)]),
             waits=[y_all] if j < 2 else _as_waits(kb._mb_wb_free[j % 2]),
             incs=(ty := trl[j % 2].inc(16)), tag=f"mb_ytr{j}")
        p.op("sync", lambda e, j=j: e.dma_start(
                out=kb.XT[j % 2][:], in_=x_in[128 * j:128 * (j + 1), :]),
             waits=_as_waits(kb._mb_xt_free[j % 2]) if j >= 2 else [],
             incs=(xl := xlc[j % 2].inc(16)), tag=f"mb_xl{j}")
        p.op("vector", lambda e, j=j, wbuf=wbuf: e.tensor_add(
                kb.TMP[0][:], kb.XT[j % 2][:], wbuf[:]),
             waits=[ty.done, xl.done] + _as_waits(add_prev),
             incs=(pad := dv.inc()), tag=f"mb_add{j}")
        p.op("sync", lambda e, j=j: e.dma_start(
                out=x_out[128 * j:128 * (j + 1), :], in_=kb.TMP[0][:]),
             waits=[pad.done], incs=(xo := xoc.inc(16)), tag=f"mb_xo{j}")
        add_prev = [xo.done]
        kb._mb_wb_free[j % 2] = [pad.done]
        kb._mb_xt_free[j % 2] = [pad.done]
    return dict(done=xoc.at())


MLA_SCALE = float((128 + 64) ** -0.5)


def emit_mla(kb, x_in, x_out, start=None, stages=("proj", "heads", "wo"),
             nhead=16):
    p, nc = kb.p, kb.nc
    kb.act_set(SET_NL_EXP)

    if kb.once("mla_setup"):
        sld = p.chan("f_sld")
        for w in ["wqTr", "wkvaTx", "wkvbn", "wkvbvT", "woT"]:
            kb.cast_weight(w, "ml")
        kb.cast_group_done("ml")
        kb.KVW = nc.alloc_sbuf_tensor("KVW", [128, 4], F32)
        kb.PSW = nc.alloc_sbuf_tensor("PSW", [128, 128], BF16)
        mskr = kb.ensure_msk4()
        p.op("sync", lambda e: e.dma_start(out=kb.KVW[:], in_=kb.din["kvw_pd"]),
             incs=sld.inc(16))
        p.op("sync", lambda e: e.dma_start(out=kb.PSW[:], in_=kb.din["pswap_bf"]),
             incs=sld.inc(16))
        kb._flags["ml_small"] = [sld.at(), mskr]
    small = kb._flags["ml_small"]

    xn_tm = kb.dram("ml_xn_tm", (S, D), BF16)
    qn_hbm = kb.dram("ml_qn", (2048, S), BF16)
    qp_hbm = kb.dram("ml_qp", (1024, S), BF16)
    kvc_hbm = kb.dram("ml_kvc", (512, S), BF16)
    ov_hbm = kb.dram("ml_ov", (D, S), BF16)
    wq_bf = kb.scr["wqTr_bf"]
    wkva_bf = kb.scr["wkvaTx_bf"]
    wkvbn_bf = kb.scr["wkvbn_bf"]
    wkvbv_bf = kb.scr["wkvbvT_bf"]
    wo_bf = kb.scr["woT_bf"]
    cwaits = [kb._flags[k + "_bf"] for k in
              ["wqTr", "wkvaTx", "wkvbn", "wkvbvT", "woT"]]

    ln = emit_ln(kb, x_in, xn_tm, "LN", start=start)

    mm = p.chan("g_mm")
    ev = p.chan("g_ev")
    dv = p.chan("g_dv")
    wl = [p.chan("ml_wl0"), p.chan("ml_wl1")]
    rt = p.chan("f_lub")
    kvtc = p.chan("ml_kvt")
    qst = p.chan("ml_qst")
    xoc = p.chan("f_xo")

    ROPC = kb.carve(65536, 2048, BF16)
    ROPS = kb.carve(69632, 2048, BF16)
    WQB = [kb.carve(73728 + 4096 * i, 2048, BF16) for i in range(2)]
    QNB = kb.carve(81920, 2048, BF16)
    RPB = kb.carve(86016, 2048, BF16)
    RT2 = kb.carve(94208, 2048, BF16)
    QPB = kb.carve(90112, 2048, BF16)
    SQB = kb.carve(98304, 2048, BF16)
    KVL = [kb.carve(102400 + 4096 * i, 2048, BF16) for i in range(4)]
    KPE = kb.carve(118784, 2048, BF16)
    RSR = kb.carve(122880, 2048, F32)[0:1, :]
    p.op("sync", lambda e: e.dma_start(out=ROPC, in_=kb.din["ropecos_bf"]),
         incs=(rc := rt.inc(16)), tag="ml_ropec")
    p.op("sync", lambda e: e.dma_start(out=ROPS, in_=kb.din["ropesin_bf"]),
         incs=(rs := rt.inc(16)), tag="ml_ropes")
    rope_ready = [rt.at()]

    def rope_pair(src_bf, swap_psum_set, out_bf, waits_src, tagp):
        toks = []
        for c in range(4):
            p.op("tensor", lambda e, c=c: e.matmul(
                    swap_psum_set[c], kb.PSW[:],
                    src_bf[:, 512 * c:512 * (c + 1)], start=True, stop=True),
                 waits=(waits_src + small) if c == 0 else None,
                 incs=(sw := mm.inc()) if c == 3 else None, tag=tagp + "sw")
        for c in range(4):
            p.op("vector", lambda e, c=c: e.tensor_mul(
                    out_bf[:, 512 * c:512 * (c + 1)],
                    src_bf[:, 512 * c:512 * (c + 1)],
                    ROPC[:, 512 * c:512 * (c + 1)]),
                 waits=([sw.done] + rope_ready) if c == 0 else None,
                 incs=dv.inc(), tag=tagp + "c")
            p.op("vector", lambda e, c=c: e.tensor_mul(
                    RT2[:, 512 * c:512 * (c + 1)], swap_psum_set[c],
                    ROPS[:, 512 * c:512 * (c + 1)]),
                 incs=dv.inc(), tag=tagp + "s")
            p.op("vector", lambda e, c=c: e.tensor_add(
                    out_bf[:, 512 * c:512 * (c + 1)],
                    out_bf[:, 512 * c:512 * (c + 1)],
                    RT2[:, 512 * c:512 * (c + 1)]),
                 incs=(tk := dv.inc()), tag=tagp + "a")
            toks.append(tk)
        return toks[-1]

    wqb_free = [None, None]
    rope_busy = None
    qnb_free = None
    rpb_free = None
    pss_free = None
    for m in range(24):
        b = m % 2
        p.op("sync", lambda e, m=m, b=b: e.dma_start(
                out=WQB[b].rearrange("p (k m) -> p k m", m=128),
                in_=wq_bf[:, 128 * m:128 * (m + 1)].rearrange(
                    "(k p) m -> p k m", p=128)),
             waits=[cwaits[0]] + _as_waits(wqb_free[b]) if m < 2 else
                   _as_waits(wqb_free[b]),
             incs=(qw := wl[b].inc(16)), tag=f"ml_qw{m}")
        for k in range(ND):
            for c in range(4):
                w = None
                if k == 0 and c == 0:
                    w = [qw.done] + _as_waits(pss_free)
                    if m == 0:
                        w += [ln["fm_ready"][ND - 1]]
                elif c == 0 and m == 0:
                    w = [ln["fm_ready"][k]]
                p.op("tensor", lambda e, k=k, c=c, b=b: e.matmul(
                        kb.PSS[c][:], WQB[b][:, 128 * k:128 * (k + 1)],
                        kb.xfm(k)[:, 512 * c:512 * (c + 1)],
                        start=(k == 0), stop=(k == ND - 1)),
                     waits=w,
                     incs=(qm := mm.inc()) if (k == ND - 1 and c == 3) else None,
                     tag=f"ml_q{m}")
        wqb_free[b] = [qm.done]
        if m < 16:
            for c in range(4):
                p.op("scalar", lambda e, c=c: e.activation(
                        QNB[:, 512 * c:512 * (c + 1)], kb.PSS[c][:], AF.Identity),
                     waits=([qm.done] + _as_waits(qnb_free)) if c == 0 else None,
                     incs=(qe := ev.inc()), tag=f"ml_qev{m}")
            pss_free = [qe.done]
            p.op("sync", lambda e, m=m: e.dma_start(
                    out=qn_hbm[128 * m:128 * (m + 1), :], in_=QNB),
                 waits=[qe.done], incs=(qs := qst.inc(16)), tag=f"ml_qnst{m}")
            qnb_free = [qs.done]
        else:
            for c in range(4):
                p.op("scalar", lambda e, c=c: e.activation(
                        RPB[:, 512 * c:512 * (c + 1)], kb.PSS[c][:], AF.Identity),
                     waits=([qm.done] + _as_waits(rpb_free)) if c == 0 else None,
                     incs=(qe := ev.inc()), tag=f"ml_qpev{m}")
            tk = rope_pair(RPB, [kb.PSS[c][:] for c in range(4)], QPB,
                           [qe.done] + _as_waits(rope_busy), f"ml_rp{m}")
            rope_busy = [tk.done]
            pss_free = [tk.done]
            rpb_free = [tk.done]
            pr = m - 16
            p.op("sync", lambda e, pr=pr: e.dma_start(
                    out=qp_hbm[128 * pr:128 * (pr + 1), :], in_=QPB),
                 waits=[tk.done], incs=(qs := qst.inc(16)), tag=f"ml_qpst{pr}")
            rope_busy = [qs.done]
    q_all = qst.at()

    sq_mm = None
    for m in range(5):
        b = m % 2
        p.op("sync", lambda e, m=m, b=b: e.dma_start(
                out=WQB[b].rearrange("p (k m) -> p k m", m=128),
                in_=wkva_bf[:, 128 * m:128 * (m + 1)].rearrange(
                    "(k p) m -> p k m", p=128)),
             waits=[cwaits[1]] + _as_waits(wqb_free[b]),
             incs=(kw := wl[b].inc(16)), tag=f"ml_kw{m}")
        for k in range(ND):
            for c in range(4):
                p.op("tensor", lambda e, k=k, c=c, b=b: e.matmul(
                        kb.PSS[c][:], WQB[b][:, 128 * k:128 * (k + 1)],
                        kb.xfm(k)[:, 512 * c:512 * (c + 1)],
                        start=(k == 0), stop=(k == ND - 1)),
                     waits=([kw.done] + _as_waits(pss_free))
                           if (k == 0 and c == 0) else None,
                     incs=(km := mm.inc()) if (k == ND - 1 and c == 3) else None,
                     tag=f"ml_kv{m}")
        wqb_free[b] = [km.done]
        if m < 4:
            for c in range(4):
                p.op("scalar", lambda e, c=c: e.activation(
                        kb.TMP[0][:, 512 * c:512 * (c + 1)], kb.PSS[c][:],
                        AF.Identity),
                     waits=([km.done] + (_as_waits([kc.done]) if m > 0 else []))
                           if c == 0 else None,
                     incs=(ke := ev.inc()), tag=f"ml_kvev{m}")
            p.op("vector", lambda e, m=m: e.tensor_copy(KVL[m], kb.TMP[0][:]),
                 waits=[ke.done], incs=(kc := dv.inc()), tag=f"ml_kvcp{m}")
            p.op("scalar", lambda e: e.activation(SQB, kb.TMP[0][:], AF.Square),
                 waits=([ke.done] + ([sq_mm.done] if m > 0 else [])),
                 incs=(ksq := ev.inc()), tag=f"ml_sq{m}")
            pss_free = [ke.done]
            for c in range(4):
                p.op("tensor", lambda e, c=c, m=m: e.matmul(
                        kb.PSB[0:1, 512 * c:512 * (c + 1)], kb.onesc[:],
                        SQB[:, 512 * c:512 * (c + 1)],
                        start=(m == 0), stop=(m == 3)),
                     waits=[ksq.done] if c == 0 else None,
                     incs=(sq_mm := mm.inc()) if c == 3 else None,
                     tag=f"ml_sqmm{m}")
        else:
            for c in range(4):
                p.op("scalar", lambda e, c=c: e.activation(
                        RPB[:, 512 * c:512 * (c + 1)], kb.PSS[c][:], AF.Identity),
                     waits=([km.done] + _as_waits(rpb_free)) if c == 0 else None,
                     incs=(ke := ev.inc()), tag=f"ml_kpev")
            tk = rope_pair(RPB, [kb.PSS[c][:] for c in range(4)], KPE,
                           [ke.done] + _as_waits(rope_busy), "ml_kpe")
            pss_free = [tk.done]
    kpe_done = tk.done

    p.op("vector", lambda e: e.tensor_scalar(
            RSR, kb.PSB[0:1, 0:2048], 1.0 / 512, EPS_RMS, ALU.mult, ALU.add),
         waits=[sq_mm.done], incs=(r1 := dv.inc()), tag="ml_rs1")
    p.op("scalar", lambda e: e.activation(RSR, RSR, AF.Ln),
         waits=[r1.done], incs=(r2 := ev.inc()))
    p.op("scalar", lambda e: e.activation(RSR, RSR, AF.Exp, scale=-0.5),
         incs=(r3 := ev.inc()))
    p.op("vector", lambda e: e.tensor_copy(kb.ROW2[:], RSR),
         waits=[r3.done], incs=(r4 := dv.inc()), tag="ml_rs4")
    for c in range(4):
        p.op("tensor", lambda e, c=c: e.matmul(
                kb.PSS[c][:], kb.onesr[:], kb.ROW2[0:1, 512 * c:512 * (c + 1)],
                start=True, stop=True),
             waits=([r4.done] + _as_waits(pss_free)) if c == 0 else None,
             incs=(rb := mm.inc()) if c == 3 else None, tag="ml_rbc")
    for ct in range(4):
        for c in range(4):
            p.op("vector", lambda e, ct=ct, c=c: e.scalar_tensor_tensor(
                    KVL[ct][:, 512 * c:512 * (c + 1)],
                    KVL[ct][:, 512 * c:512 * (c + 1)],
                    kb.KVW[:, ct:ct + 1], kb.PSS[c][:], ALU.mult, ALU.mult),
                 waits=[rb.done, small[0]] if (ct == 0 and c == 0) else None,
                 incs=(kvn := dv.inc()), tag=f"ml_kvn{ct}")
    kvc_done = kvn.done
    pss_free = [kvn.done]
    for ct in range(4):
        p.op("sync", lambda e, ct=ct: e.dma_start(
                out=kvc_hbm[128 * ct:128 * (ct + 1), :], in_=KVL[ct]),
             waits=[kvc_done], incs=(ks2 := qst.inc(16)), tag=f"ml_kvst{ct}")
    kv_st = ks2.done
    if "heads" not in stages:
        return dict(done=_Wait(qst, qst.count))

    KVT = [kb.carve(1024 * j, 512, BF16) for j in range(16)]
    ATN = [kb.carve(16384 + 1024 * j, 512, BF16) for j in range(16)]
    PRB = [kb.carve(32768 + 4096 * i, 2048, BF16) for i in range(4)]
    QLT = [kb.carve(49152 + 4096 * i, 2048, BF16) for i in range(4)]
    QNH = [kb.carve(65536 + 4096 * i, 2048, BF16) for i in range(2)]
    QPH = [kb.carve(73728 + 4096 * i, 2048, BF16) for i in range(2)]
    OLB = kb.carve(81920, 2048, BF16)
    OVH = kb.carve(86016, 2048, BF16)
    WKB = [kb.carve(90112 + 1024 * i, 512, BF16) for i in range(2)]
    WVB = [kb.carve(92160 + 1024 * i, 512, BF16) for i in range(2)]

    hlc = {k: [p.chan(f"ml_h{k}0"), p.chan(f"ml_h{k}1")]
           for k in ("qn", "wk", "wv", "qp")}
    for j in range(16):
        p.op("sync", lambda e, j=j: e.dma_start_transpose(
                KVT[j], kvc_hbm[:, 128 * j:128 * (j + 1)]),
             waits=[kv_st, q_all] if j == 0 else None,
             incs=(kt := kvtc.inc(16)), tag=f"ml_kvt{j}")
    kvt_ready = kt.done

    hload_free = {"qn": [None, None], "qp": [None, None],
                  "wk": [None, None], "wv": [None, None]}
    kb._atn_evac_b = [None, None]
    kb._psb_free = [None] * 4
    sc = p.chan("g_dv")
    atn_evac = None
    olb_free = None
    ovh_free = None
    ctx_free = None
    prb_free = None
    psb_scores_free = None
    for h in range(nhead):
        b = h % 2
        p.op("sync", lambda e, h=h, b=b: e.dma_start(
                out=QNH[b], in_=qn_hbm[128 * h:128 * (h + 1), :]),
             waits=([q_all, kpe_done] if h < 2 else []) +
                   _as_waits(hload_free["qn"][b]),
             incs=(lqn := hlc["qn"][b].inc(16)), tag=f"ml_hqn{h}")
        p.op("sync", lambda e, h=h, b=b: e.dma_start(
                out=WKB[b], in_=wkvbn_bf[128 * h:128 * (h + 1), :]),
             waits=([cwaits[2]] if h < 2 else []) + _as_waits(hload_free["wk"][b]),
             incs=(lwk := hlc["wk"][b].inc(16)), tag=f"ml_hwk{h}")
        p.op("sync", lambda e, h=h, b=b: e.dma_start(
                out=WVB[b].rearrange("p (ct m) -> p ct m", m=128),
                in_=wkvbv_bf[:, 128 * h:128 * (h + 1)].rearrange(
                    "(ct p) m -> p ct m", p=128)),
             waits=([cwaits[3]] if h < 2 else []) + _as_waits(hload_free["wv"][b]),
             incs=(lwv := hlc["wv"][b].inc(16)), tag=f"ml_hwv{h}")
        if h % 2 == 0:
            pr = h // 2
            p.op("sync", lambda e, pr=pr: e.dma_start(
                    out=QPH[pr % 2], in_=qp_hbm[128 * pr:128 * (pr + 1), :]),
                 waits=([q_all, kpe_done] if pr < 2 else []) +
                       _as_waits(hload_free["qp"][pr % 2]),
                 incs=(lqp := hlc["qp"][pr % 2].inc(16)), tag=f"ml_hqp{pr}")
            hload_qp = lqp.done
        qph_buf = QPH[(h // 2) % 2]
        qpe_lo = 64 * (h % 2)

        qle = None
        for ct in range(4):
            for c in range(4):
                if ct == 0 and c == 0:
                    w = ([lqn.done, lwk.done] + _as_waits(ctx_free) +
                         (_as_waits(pss_free) if h == 0 else []))
                elif c == 0:
                    w = [qle.done]
                else:
                    w = None
                p.op("tensor", lambda e, ct=ct, c=c, b=b: e.matmul(
                        kb.PSS[c][:], WKB[b][:, 128 * ct:128 * (ct + 1)],
                        QNH[b][:, 512 * c:512 * (c + 1)],
                        start=True, stop=True),
                     waits=w,
                     incs=(qlm := mm.inc()) if c == 3 else None,
                     tag=f"ml_ql{h}_{ct}")
            for c in range(4):
                p.op("scalar", lambda e, ct=ct, c=c: e.activation(
                        QLT[ct][:, 512 * c:512 * (c + 1)], kb.PSS[c][:],
                        AF.Identity),
                     waits=[qlm.done] if c == 0 else None,
                     incs=(qle := ev.inc()), tag=f"ml_qle{h}_{ct}")
        qlat_done = qle.done
        if h == 0 and "ml_dbg_qlat" in kb.dbg:
            dbg_ql = kb.dram("ml_dbg_qlat", (512, S), BF16)
            for ct in range(4):
                p.op("sync", lambda e, ct=ct: e.dma_start(
                        out=dbg_ql[128 * ct:128 * (ct + 1), :], in_=QLT[ct]),
                     waits=[qlat_done], incs=qst.inc(16), tag="dbg_ql")

        for qc in range(4):
            nkc = qc + 1
            nkt = 4 * (qc + 1)
            for qil in range(4):
                qa = 4 * qc + qil
                base = qil if qc == 0 else (2 * (qil % 2) if qc == 1 else 0)
                for kc in range(nkc):
                    bk = base + kc
                    for ct in range(4):
                        w = None
                        if ct == 0 and kc == 0:
                            w = _as_waits(kb._psb_free[base])
                            if qil == 0:
                                w += [qlat_done, kvt_ready]
                            if h == 0 and qc == 0 and qil == 0:
                                w += [kpe_done]
                        p.op("tensor", lambda e, qa=qa, bk=bk, kc=kc, ct=ct:
                             e.matmul(
                                kb.PSB[:, 512 * bk:512 * (bk + 1)],
                                QLT[ct][:, 128 * qa:128 * (qa + 1)],
                                KVL[ct][:, 512 * kc:512 * (kc + 1)],
                                start=(ct == 0), stop=False),
                             waits=w, tag=f"ml_s{h}_{qa}")
                    p.op("tensor", lambda e, qa=qa, bk=bk, kc=kc, qph_buf=qph_buf,
                         qpe_lo=qpe_lo, nkc=nkc: e.matmul(
                                kb.PSB[:, 512 * bk:512 * (bk + 1)],
                                qph_buf[qpe_lo:qpe_lo + 64,
                                        128 * qa:128 * (qa + 1)],
                                KPE[qpe_lo:qpe_lo + 64,
                                    512 * kc:512 * (kc + 1)],
                                start=False, stop=(kc != nkc - 1)),
                         waits=[hload_qp] if (qc == 0 and qil == 0 and kc == 0)
                               else None,
                         tag=f"ml_sp{h}_{qa}")
                p.op("tensor", lambda e, qa=qa, qil=qil, qc=qc, base=base:
                     e.matmul(
                        kb.PSB[:, 512 * (base + qc):512 * (base + qc + 1)],
                        kb.ident[:],
                        kb.MSK4[:, 512 * qil:512 * (qil + 1)],
                        start=False, stop=True),
                     incs=(smm := mm.inc()), tag=f"ml_sm{h}_{qa}")
                nm = kb.SM[:, 160 + qil:161 + qil]
                nms = kb.SM[:, 164 + qil:165 + qil]
                sums = kb.SM[:, 168 + qil:169 + qil]
                rec = kb.SM[:, 172 + qil:173 + qil]
                p.op("vector", lambda e, nm=nm, nkc=nkc, base=base: e.reduce_max(
                        nm, kb.PSB[:, 512 * base:512 * (base + nkc)], AX.X,
                        negate=True),
                     waits=[smm.done], incs=(mx := dv.inc()),
                     tag=f"ml_mx{h}_{qa}")
                p.op("vector", lambda e, nm=nm, nms=nms: e.tensor_scalar_mul(
                        nms, nm, MLA_SCALE),
                     waits=[mx.done], incs=(mxs := dv.inc()))
                p.op("scalar", lambda e, qil=qil, nms=nms, sums=sums, nkc=nkc,
                     base=base: e.activation(
                        PRB[qil][:, 0:512 * nkc],
                        kb.PSB[:, 512 * base:512 * (base + nkc)],
                        AF.Exp, bias=nms, scale=MLA_SCALE, accum_out=sums),
                     waits=[mxs.done] + (_as_waits(prb_free) if qil == 0 else []),
                     incs=(ex := ev.inc()), tag=f"ml_ex{h}_{qa}")
                for b2 in range(base, base + nkc):
                    kb._psb_free[b2] = [ex.done]
                p.op("vector", lambda e, sums=sums, rec=rec: e.reciprocal(
                        rec, sums),
                     waits=[ex.done], incs=(rc2 := dv.inc()))
                p.op("vector", lambda e, qil=qil, rec=rec: e.tensor_scalar_mul(
                        kb.SMB[:, 128 * qil:128 * (qil + 1)], kb.ident[:], rec),
                     waits=[rc2.done], incs=(dg := dv.inc()),
                     tag=f"ml_dg{h}_{qa}")
            diag_done = dg.done
            tr_ev = None
            for kt in range(nkt):
                ps = kb.PSS[kt % 2][:]
                for qil in range(4):
                    p.op("tensor", lambda e, kt=kt, qil=qil, ps=ps: e.matmul(
                            ps[:, 128 * qil:128 * (qil + 1)],
                            PRB[qil][:, 128 * kt:128 * (kt + 1)],
                            kb.SMB[:, 128 * qil:128 * (qil + 1)],
                            start=True, stop=True),
                         waits=([diag_done] + _as_waits(kb._atn_evac_b[kt % 2])
                                + (_as_waits(pss_free) if (h == 0 and qc == 0
                                                           and kt < 2) else []))
                         if qil == 0 else None,
                         incs=(tmm := mm.inc()) if qil == 3 else None,
                         tag=f"ml_tr{h}_{kt}")
                p.op("scalar", lambda e, kt=kt, ps=ps: e.activation(
                        ATN[kt], ps, AF.Identity),
                     waits=[tmm.done], incs=(tr_ev := ev.inc()),
                     tag=f"ml_tre{h}_{kt}")
                kb._atn_evac_b[kt % 2] = [tr_ev.done]
            atn_done = tr_ev.done
            prb_free = [tmm.done]
            if h == 0 and "ml_dbg_atn" in kb.dbg:
                dbg_at = kb.dram("ml_dbg_atn", (4, 2048, 512), BF16)
                for kt in range(nkt):
                    p.op("sync", lambda e, kt=kt, qc=qc: e.dma_start(
                            out=dbg_at[qc, 128 * kt:128 * (kt + 1), :],
                            in_=ATN[kt]),
                         waits=[atn_done], incs=qst.inc(16), tag="dbg_at")
            for ct in range(4):
                for kt in range(nkt):
                    p.op("tensor", lambda e, ct=ct, kt=kt, nkt=nkt: e.matmul(
                            kb.PSS[ct][:], KVT[kt][:, 128 * ct:128 * (ct + 1)],
                            ATN[kt], start=(kt == 0), stop=(kt == nkt - 1)),
                         waits=[atn_done] + _as_waits(olb_free)
                               if (ct == 0 and kt == 0) else None,
                         incs=(cm2 := mm.inc()) if kt == nkt - 1 else None,
                         tag=f"ml_ol{h}_{ct}")
                p.op("scalar", lambda e, ct=ct: e.activation(
                        OLB[:, 512 * ct:512 * (ct + 1)], kb.PSS[ct][:],
                        AF.Identity),
                     waits=[cm2.done], incs=(ole := ev.inc()),
                     tag=f"ml_ole{h}_{ct}")
            ctx_free = [ole.done]
            for ct in range(4):
                p.op("tensor", lambda e, ct=ct, b=b: e.matmul(
                        kb.PSB[:, 0:512], WVB[b][:, 128 * ct:128 * (ct + 1)],
                        OLB[:, 512 * ct:512 * (ct + 1)],
                        start=(ct == 0), stop=(ct == 3)),
                     waits=([lwv.done, ole.done] + _as_waits(kb._psb_free[0]))
                           if ct == 0 else None,
                     incs=(ovm := mm.inc()) if ct == 3 else None,
                     tag=f"ml_ov{h}_{qc}")
            p.op("scalar", lambda e, qc=qc: e.activation(
                    OVH[:, 512 * qc:512 * (qc + 1)], kb.PSB[:, 0:512],
                    AF.Identity),
                 waits=[ovm.done] + (_as_waits(ovh_free) if qc == 0 else []),
                 incs=(ove := ev.inc()), tag=f"ml_ove{h}_{qc}")
            kb._psb_free[0] = [ove.done]
            olb_free = [ovm.done]
        p.op("sync", lambda e, h=h: e.dma_start(
                out=ov_hbm[128 * h:128 * (h + 1), :], in_=OVH),
             waits=[ove.done], incs=(ovs := qst.inc(16)), tag=f"ml_ovst{h}")
        ovh_free = [ovs.done]
        hload_free["qn"][b] = [qlm.done]
        hload_free["wk"][b] = [qlm.done]
        hload_free["wv"][b] = [ovm.done]
        if h % 2 == 1:
            hload_free["qp"][(h // 2) % 2] = [smm.done]
    ov_all = qst.at()
    if "wo" not in stages:
        return dict(done=ov_all)

    WOT = [kb.carve(4096 * k, 2048, BF16) for k in range(16)]
    OVS = [kb.carve(65536 + 4096 * i, 2048, BF16) for i in range(2)]
    wo_c = p.chan("f_wr")
    wog = [p.chan("f_gl0"), p.chan("f_gl1")]
    mxl = [p.chan("f_dn0"), p.chan("f_dn1")]
    for k in range(16):
        p.op("sync", lambda e, k=k: e.dma_start(
                out=WOT[k], in_=wo_bf[128 * k:128 * (k + 1), :]),
             waits=[ov_all, cwaits[4]] if k == 0 else None,
             incs=(wol := wo_c.inc(16)), tag=f"ml_wol{k}")
    wo_ready = wol.done
    ovs_free = [None, None]
    bank_free2 = [None] * 8
    add_prev = None
    xt_free = [None, None]
    for j in range(NT):
        ob = j % 2
        p.op("gpsimd", lambda e, j=j, ob=ob: e.dma_start(
                out=OVS[ob].rearrange("p (k t) -> p k t", t=128),
                in_=ov_hbm[:, 128 * j:128 * (j + 1)].rearrange(
                    "(k p) t -> p k t", p=128)),
             waits=([ov_all] if j < 2 else []) + _as_waits(ovs_free[ob]),
             incs=(ols := wog[ob].inc(16)), tag=f"ml_ovsl{j}")
        p.op("sync", lambda e, j=j, ob=ob: e.dma_start(
                out=kb.XT[ob][:], in_=x_in[128 * j:128 * (j + 1), :]),
             waits=_as_waits(xt_free[ob]), incs=(xl := mxl[ob].inc(16)),
             tag=f"ml_xl{j}")
        bks = [j % 2 * 4 + c for c in range(4)]
        for c in range(4):
            bkid = bks[c]
            bank = (kb.PSS[bkid][:] if bkid < 4 else
                    kb.PSB[:, 512 * (bkid - 4):512 * (bkid - 3)])
            for k in range(16):
                p.op("tensor", lambda e, k=k, c=c, ob=ob, bank=bank: e.matmul(
                        bank, OVS[ob][:, 128 * k:128 * (k + 1)],
                        WOT[k][:, 512 * c:512 * (c + 1)],
                        start=(k == 0), stop=(k == 15)),
                     waits=([ols.done, wo_ready] + _as_waits(bank_free2[bkid]))
                           if k == 0 else None,
                     incs=(wm := mm.inc()) if k == 15 else None,
                     tag=f"ml_wo{j}_{c}")
            p.op("vector", lambda e, c=c, ob=ob, bank=bank: e.tensor_add(
                    kb.TMP[0][:, 512 * c:512 * (c + 1)],
                    kb.XT[ob][:, 512 * c:512 * (c + 1)], bank),
                 waits=[wm.done, xl.done] + (_as_waits(add_prev) if c == 0 else []),
                 incs=(ad := dv.inc()), tag=f"ml_add{j}_{c}")
            bank_free2[bkid] = [ad.done]
        ovs_free[ob] = [wm.done]
        p.op("sync", lambda e, j=j: e.dma_start(
                out=x_out[128 * j:128 * (j + 1), :], in_=kb.TMP[0][:]),
             waits=[ad.done], incs=(xo := xoc.inc(16)), tag=f"ml_xo{j}")
        add_prev = [xo.done]
        xt_free[ob] = [ad.done]
    return dict(done=xoc.at())


MIX_SCALE = float(128 ** -0.5)


def emit_mix1(kb, x_in, x_out, ci, start=None,
              stages=("pre", "qkv", "heads", "wo", "ff")):
    p, nc = kb.p, kb.nc
    kb.act_set(SET_NL_EXP)

    start = _as_waits(start)
    if kb.once("mx_setup"):
        sld = p.chan("mx_sld")
        for w in ["wrT", "wdT1", "wupT1", "wqT2", "wkT2", "wvT2", "woT2",
                  "wff1T", "wff2T", "sbias_row"]:
            kb.cast_weight(w, "mx")
        kb.cast_group_done("mx")
        kb.LN1S = nc.alloc_sbuf_tensor("LN1S", [128, 16], F32)
        kb.LN1B = nc.alloc_sbuf_tensor("LN1B", [128, 16], F32)
        kb.LN2S = nc.alloc_sbuf_tensor("LN2S", [128, 16], F32)
        kb.LN2B = nc.alloc_sbuf_tensor("LN2B", [128, 16], F32)
        kb.LGT = nc.alloc_sbuf_tensor("LGT", [128, 64], F32)
        p.op("vector", lambda e: e.memset(kb.LGT[:], 0.0), tag="mx_lgt0")
        kb.BIAS_BC = nc.alloc_sbuf_tensor("BIAS_BC", [128, 2048], BF16)
        kb.WDT1 = nc.alloc_sbuf_tensor("WDT1", [128, 512], BF16)
        kb.WRT = nc.alloc_sbuf_tensor("WRT", [128, 48], BF16)
        for nm, t in [("ln1s_pd", kb.LN1S), ("ln1b_pd", kb.LN1B),
                      ("ln2s_pd", kb.LN2S), ("ln2b_pd", kb.LN2B)]:
            p.op("sync", lambda e, nm=nm, t=t: e.dma_start(
                    out=t[:], in_=kb.din[nm]), incs=sld.inc(16))
        sb_bf = kb.scr["sbias_row_bf"]; wsb = kb._flags["sbias_row_bf"]
        p.op("sync", lambda e: e.dma_start(out=kb.ROW2[:], in_=sb_bf),
             waits=[wsb], incs=sld.inc(16))
        wd1_bf = kb.scr["wdT1_bf"]; wd1w = kb._flags["wdT1_bf"]
        p.op("sync", lambda e: e.dma_start(
                out=kb.WDT1[:].rearrange("p (k r) -> p k r", r=32),
                in_=wd1_bf.rearrange("(k p) r -> p k r", p=128)),
             waits=[wd1w], incs=sld.inc(16))
        kb.cast_weight("wupT1")
        wr_bf = kb.scr["wrT_bf"]; wrw = kb._flags["wrT_bf"]
        p.op("sync", lambda e: e.dma_start(
                out=kb.WRT[:].rearrange("p (k r) -> p k r", r=3),
                in_=wr_bf.rearrange("(k p) r -> p k r", p=128)),
             waits=[wrw], incs=sld.inc(16))
        small_ready = sld.at()
        bmm = p.chan("mx_bb")
        for c in range(4):
            p.op("tensor", lambda e, c=c: e.matmul(
                    kb.PSS[c][:], kb.onesr[:],
                    kb.ROW2[0:1, 512 * c:512 * (c + 1)], start=True, stop=True),
                 waits=([small_ready] + start) if c == 0 else None,
                 incs=(bm := bmm.inc()) if c == 3 else None, tag="mx_bbc")
        bevc = p.chan("g_ev")
        for c in range(4):
            p.op("scalar", lambda e, c=c: e.activation(
                    kb.BIAS_BC[:, 512 * c:512 * (c + 1)], kb.PSS[c][:],
                    AF.Identity),
                 waits=[bm.done] if c == 0 else None,
                 incs=(bfin := bevc.inc()), tag="mx_bbe")
        kb._flags["mx_small"] = [small_ready, bfin.done, kb.ensure_msk4()]
        kb._flags["mx_bias_pss_free"] = [bfin.done]
    small = kb._flags["mx_small"]

    xn_tm = kb.dram(f"mx{ci}_xn", (S, D), BF16)
    xn1_tm = kb.dram(f"mx{ci}_xn1", (S, D), BF16)
    xn2_tm = kb.dram(f"mx{ci}_xn2", (S, D), BF16)
    xi_hbm = kb.dram(f"mx{ci}_xi", (S, D), F32)
    q_hbm = kb.dram(f"mx{ci}_q", (D, S), BF16)
    k_hbm = kb.dram(f"mx{ci}_k", (D, S), BF16)
    v_hbm = kb.dram(f"mx{ci}_v", (D, S), BF16)
    ctx_hbm = kb.dram(f"mx{ci}_ctx", (D, S), BF16)
    g2_hbm = kb.dram(f"mx{ci}_g", (HID, D), BF16)

    SM = kb.SM
    T1B = kb.carve(106496, 2048, BF16)[0:32, :]
    WUPT1 = kb.carve(110592, 2048, BF16)[0:32, :]
    wupc = p.chan("mx_wup")
    p.op("sync", lambda e: e.dma_start(out=WUPT1, in_=kb.scr["wupT1_bf"]),
         waits=[kb._flags["wupT1_bf"]] + start, incs=(wup_ld := wupc.inc(16)),
         tag=f"mx{ci}wupld")
    lnr = emit_ln(kb, x_in, xn_tm, "LN", start=start,
                  inner_stats=lambda i: (SM[:, 176 + i:177 + i],
                                         SM[:, 192 + i:193 + i]))
    mm = p.chan("g_mm")
    ev = p.chan("g_ev")
    dv = p.chan("g_dv")
    st1 = p.chan("mx_st")

    ln1_toks = []
    for i in range(NT):
        s_c = SM[:, 176 + i:177 + i]
        q_c = SM[:, 192 + i:193 + i]
        m_c = SM[:, 368 + i:369 + i]
        v_c = SM[:, 384 + i:385 + i]
        r_c = SM[:, 208 + i:209 + i]
        n_c = SM[:, 224 + i:225 + i]
        p.op("vector", lambda e, s=s_c, o=m_c: e.tensor_scalar_mul(o, s, 1.0 / D),
             waits=[lnr["inner_done"][i][0]], incs=(a1 := dv.inc()))
        p.op("scalar", lambda e, m=m_c, o=v_c: e.activation(o, m, AF.Square),
             waits=[a1.done, lnr["inner_done"][i][1]], incs=(a2 := ev.inc()))
        p.op("vector", lambda e, q=q_c, m2=v_c, o=v_c: e.scalar_tensor_tensor(
                o, q, 1.0 / D, m2, ALU.mult, ALU.subtract),
             waits=[a2.done], incs=(a3 := dv.inc()))
        p.op("vector", lambda e, o=v_c: e.tensor_scalar_add(o, o, EPS_LN),
             waits=[a3.done], incs=(a4 := dv.inc()))
        p.op("scalar", lambda e, v=v_c, o=r_c: e.activation(o, v, AF.Ln),
             waits=[a4.done], incs=(a5 := ev.inc()))
        p.op("scalar", lambda e, r=r_c: e.activation(r, r, AF.Exp, scale=-0.5),
             incs=(a6 := ev.inc()))
        p.op("vector", lambda e, m=m_c, r=r_c, o=n_c: e.scalar_tensor_tensor(
                o, m, -1.0, r, ALU.mult, ALU.mult),
             waits=[a6.done], incs=(a7 := dv.inc()))
        ln1_toks.append(a7.done)

    msk_toks = []
    lg_prev = None
    for j in range(NT):
        bk = j % 2
        for k in range(ND):
            p.op("tensor", lambda e, j=j, k=k, bk=bk: e.matmul(
                    kb.PSB[:, 512 * bk:512 * bk + 3],
                    kb.xfm(k)[:, 128 * j:128 * (j + 1)],
                    kb.WRT[:, 3 * k:3 * (k + 1)],
                    start=(k == 0), stop=(k == ND - 1)),
                 waits=(lnr["fm_ready"][k] + [] if False else
                        ([lnr["fm_ready"][k]] + small +
                         _as_waits(kb._flags.get("mx_bias_pss_free")) +
                         _as_waits(lg_prev)))
                       if k == 0 else None,
                 incs=(lgm := mm.inc()) if k == ND - 1 else None,
                 tag=f"mx_rt{j}")
        p.op("scalar", lambda e, j=j, bk=bk: e.activation(
                kb.LGT[:, 4 * j:4 * j + 3], kb.PSB[:, 512 * bk:512 * bk + 3],
                AF.Identity),
             waits=[lgm.done], incs=(lge := ev.inc()), tag=f"mx_rte{j}")
        lg_prev = [lge.done] if j % 2 == 1 else lg_prev
        l0 = kb.LGT[:, 4 * j:4 * j + 1]
        l1 = kb.LGT[:, 4 * j + 1:4 * j + 2]
        l2 = kb.LGT[:, 4 * j + 2:4 * j + 3]
        g01 = SM[:, 400 + j:401 + j]
        g02 = SM[:, 416 + j:417 + j]
        g12 = SM[:, 432 + j:433 + j]
        is0 = SM[:, 304 + j:305 + j]
        is1 = SM[:, 320 + j:321 + j]
        is2 = SM[:, 336 + j:337 + j]
        is01 = SM[:, 352 + j:353 + j]
        p.op("vector", lambda e, a=l0, b2=l1, o=g01: e.tensor_tensor(
                o, a, b2, ALU.is_ge), waits=[lge.done], incs=(w1 := dv.inc()))
        p.op("vector", lambda e, a=l0, b2=l2, o=g02: e.tensor_tensor(
                o, a, b2, ALU.is_ge), incs=(w2 := dv.inc()))
        p.op("vector", lambda e, a=l1, b2=l2, o=g12: e.tensor_tensor(
                o, a, b2, ALU.is_ge), incs=(w3 := dv.inc()))
        p.op("vector", lambda e, a=g01, b2=g02, o=is0: e.tensor_mul(o, a, b2),
             waits=[w3.done], incs=(w4 := dv.inc()))
        p.op("vector", lambda e, a=is0, o=is1: e.tensor_scalar(
                o, a, -1.0, 1.0, ALU.mult, ALU.add),
             waits=[w4.done], incs=(w5 := dv.inc()))
        p.op("vector", lambda e, a=is1, b2=g12, o=is1: e.tensor_mul(o, a, b2),
             waits=[w5.done], incs=(w6 := dv.inc()))
        p.op("vector", lambda e, a=is0, b2=is1, o=is01: e.tensor_add(o, a, b2),
             waits=[w6.done], incs=(w7 := dv.inc()))
        p.op("vector", lambda e, a=is01, o=is2: e.tensor_scalar(
                o, a, -1.0, 1.0, ALU.mult, ALU.add),
             waits=[w7.done], incs=(mk := dv.inc()),
             tag=f"mx_msk{j}")
        msk_toks.append(mk.done)
    masks_done = mk.done
    if f"mx{ci}_msk2" in kb.dbg:
        dbgM2 = kb.dram(f"mx{ci}_msk2", (128, 576), F32)
        p.op("sync", lambda e: e.dma_start(out=dbgM2[:, 0:512], in_=SM[:]),
             waits=[masks_done], incs=st1.inc(16), tag="dbgMsk2")
        p.op("sync", lambda e: e.dma_start(out=dbgM2[:, 512:576], in_=kb.LGT[:]),
             incs=st1.inc(16), tag="dbgLgt2")

    for c in range(4):
        for k in range(ND):
            p.op("tensor", lambda e, c=c, k=k: e.matmul(
                    kb.PSB[0:32, 1024 + 512 * (c % 2):1024 + 512 * (c % 2) + 512]
                    if False else kb.PSS[c][0:32, :],
                    kb.WDT1[:, 32 * k:32 * (k + 1)],
                    kb.xfm(k)[:, 512 * c:512 * (c + 1)],
                    start=(k == 0), stop=(k == ND - 1)),
                 waits=([lnr["fm_ready"][k]] + small +
                        _as_waits(kb._flags.get("mx_bias_pss_free")))
                       if k == 0 else None,
                 incs=(t1m := mm.inc()) if k == ND - 1 else None,
                 tag=f"mx_t1{c}")
    for c in range(4):
        p.op("scalar", lambda e, c=c: e.activation(
                T1B[:, 512 * c:512 * (c + 1)], kb.PSS[c][0:32, :],
                AF.Identity),
             waits=[t1m.done] if c == 0 else None, incs=(t1e := ev.inc()),
             tag=f"mx_t1e{c}")
    t1_done = t1e.done

    x1st = [p.chan("mx_x1a"), p.chan("mx_x1b")]
    xld = [p.chan("mx_xla"), p.chan("mx_xlb")]
    wb_free = [None, None]
    for i in range(NT):
        b = i % 2
        wbuf = kb.WB[b]
        p.op("sync", lambda e, i=i, b=b: e.dma_start(
                out=kb.XT[b][:, 0:1024].bitcast(BF16),
                in_=xn_tm[128 * i:128 * (i + 1), :]),
             waits=([lnr["tm_done"][0], lnr["tm_done"][1]] if i < 2 else []) +
                   _as_waits(wb_free[b]),
             incs=(xr := xld[b].inc(16)), tag=f"mx_x1l{i}")
        r_c = SM[:, 208 + i:209 + i]
        n_c = SM[:, 224 + i:225 + i]
        p.op("scalar", lambda e, b=b, r=r_c, n=n_c, wbuf=wbuf: e.activation(
                wbuf[:], kb.XT[b][:, 0:1024].bitcast(BF16), AF.Identity,
                bias=n, scale=r),
             waits=[xr.done, ln1_toks[i]], incs=(xno := ev.inc()),
             tag=f"mx_x1n{i}")
        p.op("sync", lambda e, i=i, wbuf=wbuf: e.dma_start(
                out=xn1_tm[128 * i:128 * (i + 1), :], in_=wbuf[:]),
             waits=[xno.done], incs=(x1s := x1st[b].inc(16)),
             tag=f"mx_x1s{i}")
        wb_free[b] = [x1s.done]
    xn1_all = [x1st[0].at(), x1st[1].at()]

    trc = p.chan("mx_tr")
    s1ap = p.chan("mx_s1")
    for dt in range(ND):
        p.op("sync", lambda e, dt=dt: e.dma_start_transpose(
                kb.xfm(dt), xn1_tm[:, 128 * dt:128 * (dt + 1)]),
             waits=(xn1_all + [t1m.done, lgm.done]) if dt == 0 else None,
             incs=(tq := trc.inc(16)), tag=f"mx_x1t{dt}")
    tr_all = tq.done
    for dt in range(ND):
        p.op("vector", lambda e, dt=dt: e.tensor_scalar(
                kb.xfm(dt), kb.xfm(dt), kb.LN1S[:, dt:dt + 1],
                kb.LN1B[:, dt:dt + 1], ALU.mult, ALU.add),
             waits=[tr_all, masks_done] if dt == 0 else None,
             incs=(sb1 := dv.inc()), tag=f"mx_s1b{dt}")
    xn1_ready = sb1.done

    if "qkv" not in stages:
        return dict(done=_Wait(dv, dv.count))
    WQB = [kb.carve(65536 + 4096 * i, 2048, BF16) for i in range(2)]
    QNB = [kb.carve(73728 + 4096 * i, 2048, BF16) for i in range(2)]
    wlc = [p.chan("mx_wl0"), p.chan("mx_wl1")]
    pst = [p.chan("mx_ps0"), p.chan("mx_ps1")]
    wqb_free = [None, None]
    qnb_free = [None, None]
    pss_free = kb._flags.get("mx_bias_pss_free")
    mi = 0
    for wname, out_h in [("wqT2", q_hbm), ("wkT2", k_hbm), ("wvT2", v_hbm)]:
        w_bf = kb.scr[wname + "_bf"]
        cw = kb._flags[wname + "_bf"]
        for m2 in range(16):
            b = mi % 2
            p.op("sync", lambda e, m2=m2, b=b, w_bf=w_bf: e.dma_start(
                    out=WQB[b].rearrange("p (k m) -> p k m", m=128),
                    in_=w_bf[:, 128 * m2:128 * (m2 + 1)].rearrange(
                        "(k p) m -> p k m", p=128)),
                 waits=([cw] if mi < 2 else []) + _as_waits(wqb_free[b]),
                 incs=(qw := wlc[b].inc(16)), tag=f"mx_w{wname}{m2}")
            for k in range(ND):
                for c in range(4):
                    if k == 0 and c == 0:
                        w = [qw.done, xn1_ready] + _as_waits(pss_free)
                    else:
                        w = None
                    p.op("tensor", lambda e, k=k, c=c, b=b: e.matmul(
                            kb.PSS[c][:], WQB[b][:, 128 * k:128 * (k + 1)],
                            kb.xfm(k)[:, 512 * c:512 * (c + 1)],
                            start=(k == 0), stop=(k == ND - 1)),
                         waits=w,
                         incs=(qm := mm.inc()) if (k == ND - 1 and c == 3) else None,
                         tag=f"mx_p{wname}{m2}")
            wqb_free[b] = [qm.done]
            for c in range(4):
                p.op("scalar", lambda e, c=c, b=b: e.activation(
                        QNB[b][:, 512 * c:512 * (c + 1)], kb.PSS[c][:],
                        AF.Identity),
                     waits=([qm.done] + _as_waits(qnb_free[b])) if c == 0 else None,
                     incs=(qe := ev.inc()), tag=f"mx_e{wname}{m2}")
            pss_free = [qe.done]
            p.op("sync", lambda e, m2=m2, b=b, out_h=out_h: e.dma_start(
                    out=out_h[128 * m2:128 * (m2 + 1), :], in_=QNB[b]),
                 waits=[qe.done], incs=(qs := pst[b].inc(16)),
                 tag=f"mx_s{wname}{m2}")
            qnb_free[b] = [qs.done]
            mi += 1
    qkv_all = [pst[0].at(), pst[1].at()]

    if "heads" not in stages:
        return dict(done=qkv_all[0])
    KH = [kb.carve(4096 * i, 2048, BF16) for i in range(2)]
    QH = [kb.carve(8192 + 4096 * i, 2048, BF16) for i in range(2)]
    VTM = [kb.carve(16384 + 4096 * i, 2048, BF16) for i in range(2)]
    PRB = [kb.carve(24576 + 4096 * i, 2048, BF16) for i in range(4)]
    ATN = [kb.carve(40960 + 1024 * j, 512, BF16) for j in range(16)]
    CXH = kb.carve(57344, 2048, BF16)
    hlc = {"k": [p.chan("ml_hwk0"), p.chan("ml_hwk1")],
           "q": [p.chan("ml_hqn0"), p.chan("ml_hqn1")],
           "v": [p.chan("ml_hwv0"), p.chan("ml_hwv1")]}
    cst = p.chan("mx_cst")
    hfree = {k2: [None, None] for k2 in ("k", "q", "v")}
    kb._atn_evac_b = [None, None]
    kb._psb_free = [None] * 4
    atn_evac = None
    ctx_free = None
    cxh_free = None
    prb_free = None
    psb_scores_free = None
    for h in range(16):
        b = h % 2
        p.op("sync", lambda e, h=h, b=b: e.dma_start(
                out=KH[b], in_=k_hbm[128 * h:128 * (h + 1), :]),
             waits=(qkv_all if h < 2 else []) + _as_waits(hfree["k"][b]),
             incs=(lk := hlc["k"][b].inc(16)), tag=f"mx_hk{h}")
        p.op("sync", lambda e, h=h, b=b: e.dma_start(
                out=QH[b], in_=q_hbm[128 * h:128 * (h + 1), :]),
             waits=(qkv_all if h < 2 else []) + _as_waits(hfree["q"][b]),
             incs=(lq := hlc["q"][b].inc(16)), tag=f"mx_hq{h}")
        for kt in range(16):
            p.op("sync", lambda e, h=h, b=b, kt=kt: e.dma_start_transpose(
                    VTM[b][:, 128 * kt:128 * (kt + 1)],
                    v_hbm[128 * h:128 * (h + 1), 128 * kt:128 * (kt + 1)]),
                 waits=((qkv_all if h < 2 else []) + _as_waits(hfree["v"][b]))
                       if kt == 0 else None,
                 incs=(lv := hlc["v"][b].inc(16)), tag=f"mx_hv{h}")
        for qc in range(4):
            nkc = qc + 1
            nkt = 4 * (qc + 1)
            for qil in range(4):
                qa = 4 * qc + qil
                base = qil if qc == 0 else (2 * (qil % 2) if qc == 1 else 0)
                for kc in range(nkc):
                    bk = base + kc
                    w = None
                    if kc == 0:
                        w = _as_waits(kb._psb_free[base])
                        if qil == 0:
                            w = w + [lq.done, lk.done]
                    p.op("tensor", lambda e, qa=qa, bk=bk, kc=kc, b=b, qc=qc:
                         e.matmul(
                            kb.PSB[:, 512 * bk:512 * (bk + 1)],
                            QH[b][:, 128 * qa:128 * (qa + 1)],
                            KH[b][:, 512 * kc:512 * (kc + 1)],
                            start=True, stop=(kc != qc)),
                         waits=w, tag=f"mx_s{h}_{qa}")
                p.op("tensor", lambda e, qa=qa, qil=qil, qc=qc, base=base:
                     e.matmul(
                        kb.PSB[:, 512 * (base + qc):512 * (base + qc + 1)],
                        kb.ident[:],
                        kb.MSK4[:, 512 * qil:512 * (qil + 1)],
                        start=False, stop=True),
                     incs=(smm := mm.inc()), tag=f"mx_m{h}_{qa}")
                nm = SM[:, 448 + qil:449 + qil]
                nms = SM[:, 452 + qil:453 + qil]
                sums = SM[:, 456 + qil:457 + qil]
                rec = SM[:, 460 + qil:461 + qil]
                p.op("vector", lambda e, nm=nm, nkc=nkc, base=base: e.reduce_max(
                        nm, kb.PSB[:, 512 * base:512 * (base + nkc)], AX.X,
                        negate=True),
                     waits=[smm.done], incs=(mx := dv.inc()),
                     tag=f"mx_mx{h}_{qa}")
                p.op("vector", lambda e, nm=nm, nms=nms: e.tensor_scalar_mul(
                        nms, nm, MIX_SCALE), waits=[mx.done],
                     incs=(mxs := dv.inc()))
                p.op("scalar", lambda e, qil=qil, nms=nms, sums=sums, nkc=nkc,
                     base=base: e.activation(
                        PRB[qil][:, 0:512 * nkc],
                        kb.PSB[:, 512 * base:512 * (base + nkc)],
                        AF.Exp, bias=nms, scale=MIX_SCALE, accum_out=sums),
                     waits=[mxs.done] + (_as_waits(prb_free) if qil == 0 else []),
                     incs=(ex := ev.inc()), tag=f"mx_ex{h}_{qa}")
                for b2 in range(base, base + nkc):
                    kb._psb_free[b2] = [ex.done]
                p.op("vector", lambda e, sums=sums, rec=rec: e.reciprocal(
                        rec, sums), waits=[ex.done], incs=(rc2 := dv.inc()))
                p.op("vector", lambda e, qil=qil, rec=rec: e.tensor_scalar_mul(
                        kb.SMB[:, 128 * qil:128 * (qil + 1)], kb.ident[:], rec),
                     waits=[rc2.done], incs=(dg := dv.inc()),
                     tag=f"mx_dg{h}_{qa}")
            diag_done = dg.done
            tr_ev = None
            for kt in range(nkt):
                ps = kb.PSS[kt % 2][:]
                for qil in range(4):
                    p.op("tensor", lambda e, kt=kt, qil=qil, ps=ps: e.matmul(
                            ps[:, 128 * qil:128 * (qil + 1)],
                            PRB[qil][:, 128 * kt:128 * (kt + 1)],
                            kb.SMB[:, 128 * qil:128 * (qil + 1)],
                            start=True, stop=True),
                         waits=([diag_done] + _as_waits(kb._atn_evac_b[kt % 2]) +
                                (_as_waits(ctx_free) if kt < 2 else []))
                         if qil == 0 else None,
                         incs=(tmm := mm.inc()) if qil == 3 else None,
                         tag=f"mx_tr{h}_{kt}")
                p.op("scalar", lambda e, kt=kt, ps=ps: e.activation(
                        ATN[kt], ps, AF.Identity),
                     waits=[tmm.done], incs=(tr_ev := ev.inc()),
                     tag=f"mx_tre{h}_{kt}")
                kb._atn_evac_b[kt % 2] = [tr_ev.done]
            atn_done = tr_ev.done
            prb_free = [tmm.done]
            for kt in range(nkt):
                p.op("tensor", lambda e, kt=kt, b=b, nkt=nkt: e.matmul(
                        kb.PSS[2][:], VTM[b][:, 128 * kt:128 * (kt + 1)],
                        ATN[kt], start=(kt == 0), stop=(kt == nkt - 1)),
                     waits=([atn_done, lv.done] + _as_waits(ctx_free))
                           if kt == 0 else None,
                     incs=(cm2 := mm.inc()) if kt == nkt - 1 else None,
                     tag=f"mx_cx{h}_{qc}")
            p.op("scalar", lambda e, qc=qc: e.activation(
                    CXH[:, 512 * qc:512 * (qc + 1)], kb.PSS[2][:], AF.Identity),
                 waits=[cm2.done] + (_as_waits(cxh_free) if qc == 0 else []),
                 incs=(cxe := ev.inc()), tag=f"mx_cxe{h}_{qc}")
            ctx_free = [cxe.done]
        p.op("sync", lambda e, h=h: e.dma_start(
                out=ctx_hbm[128 * h:128 * (h + 1), :], in_=CXH),
             waits=[cxe.done], incs=(cxs := cst.inc(16)), tag=f"mx_cxs{h}")
        cxh_free = [cxs.done]
        hfree["k"][b] = [_Wait(mm, mm.count)]
        hfree["q"][b] = [_Wait(mm, mm.count)]
        hfree["v"][b] = [_Wait(mm, mm.count)]
    ctx_all = cst.at()

    WOT2 = [kb.carve(4096 * k, 2048, BF16) for k in range(16)]
    CVS = [kb.carve(65536 + 4096 * i, 2048, BF16) for i in range(2)]
    woc = p.chan("mx_wo")
    cvl = [p.chan("mx_cv0"), p.chan("mx_cv1")]
    xnl = [p.chan("mx_xn0"), p.chan("mx_xn1")]
    xis = [p.chan("mx_xi0"), p.chan("mx_xi1")]
    x2s = [p.chan("mx_x20"), p.chan("mx_x21")]
    wo_bf = kb.scr["woT2_bf"]
    for k in range(16):
        p.op("sync", lambda e, k=k: e.dma_start(
                out=WOT2[k], in_=wo_bf[128 * k:128 * (k + 1), :]),
             waits=[ctx_all, kb._flags["woT2_bf"]] if k == 0 else None,
             incs=(wol := woc.inc(16)), tag=f"mx_wol{k}")
    wo_ready = wol.done
    XN2B = [kb.carve(73728 + 4096 * i, 2048, BF16) for i in range(2)]
    cvs_free = [None, None]
    wbx_free = [None, None]
    xn2b_free = [None, None]
    tmp0_free = None
    bank_free = [None] * 8
    banksA = [kb.PSS[i][:] for i in range(4)] + \
             [kb.PSB[:, 512 * i:512 * (i + 1)] for i in range(4)]
    for j in range(NT):
        ob = j % 2
        p.op("gpsimd", lambda e, j=j, ob=ob: e.dma_start(
                out=CVS[ob].rearrange("p (k t) -> p k t", t=128),
                in_=ctx_hbm[:, 128 * j:128 * (j + 1)].rearrange(
                    "(k p) t -> p k t", p=128)),
             waits=([ctx_all] if j < 2 else []) + _as_waits(cvs_free[ob]),
             incs=(cvld := cvl[ob].inc(16)), tag=f"mx_cvl{j}")
        wbuf = kb.WB[ob]
        p.op("sync", lambda e, j=j, wbuf=wbuf: e.dma_start(
                out=wbuf[:], in_=xn_tm[128 * j:128 * (j + 1), :]),
             waits=_as_waits(wbx_free[ob]) + (lnr["tm_done"] if j < 2 else []),
             incs=(xnr := xnl[ob].inc(16)),
             tag=f"mx_xnr{j}")
        for c in range(4):
            bk = (j % 2) * 4 + c
            for k in range(16):
                p.op("tensor", lambda e, k=k, c=c, ob=ob, bk=bk: e.matmul(
                        banksA[bk], CVS[ob][:, 128 * k:128 * (k + 1)],
                        WOT2[k][:, 512 * c:512 * (c + 1)],
                        start=(k == 0), stop=(k == 15)),
                     waits=([cvld.done, wo_ready] + _as_waits(bank_free[bk]) +
                            (_as_waits(kb._psb_free[bk - 4]) if (j < 2 and bk >= 4)
                             else []) +
                            (_as_waits(ctx_free) if j < 2 else []))
                           if k == 0 else None,
                     incs=(wm := mm.inc()) if k == 15 else None,
                     tag=f"mx_wo{j}_{c}")
            p.op("vector", lambda e, c=c, ob=ob, bk=bk, wbuf=wbuf: e.tensor_add(
                    kb.TMP[0][:, 512 * c:512 * (c + 1)],
                    wbuf[:, 512 * c:512 * (c + 1)], banksA[bk]),
                 waits=([wm.done, xnr.done] + _as_waits(tmp0_free))
                       if c == 0 else [wm.done],
                 incs=(xia := dv.inc()), tag=f"mx_xia{j}_{c}")
            bank_free[bk] = [xia.done]
        p.op("sync", lambda e, j=j: e.dma_start(
                out=xi_hbm[128 * j:128 * (j + 1), :], in_=kb.TMP[0][:]),
             waits=[xia.done], incs=(xist := xis[ob].inc(16)),
             tag=f"mx_xis{j}")
        wbx_free[ob] = [xia.done]
        cvs_free[ob] = [wm.done]
        s_c = SM[:, 240 + j:241 + j]
        q_c = SM[:, 256 + j:257 + j]
        m_c = SM[:, 464 + j:465 + j]
        v_c = SM[:, 468 + j:469 + j]
        r_c = SM[:, 272 + j:273 + j]
        n_c = SM[:, 288 + j:289 + j]
        p.op("vector", lambda e, o=s_c: e.reduce_sum(o, kb.TMP[0][:], AX.X),
             waits=[xia.done], incs=(b1 := dv.inc()))
        p.op("scalar", lambda e, o=q_c, ob=ob: e.activation(
                kb.XT[ob][:], kb.TMP[0][:], AF.Square, accum_out=o),
             waits=[xia.done], incs=(b2 := ev.inc()))
        p.op("vector", lambda e, s=s_c, o=m_c: e.tensor_scalar_mul(o, s, 1.0 / D),
             waits=[b1.done], incs=(b3 := dv.inc()))
        p.op("scalar", lambda e, m=m_c, o=v_c: e.activation(o, m, AF.Square),
             waits=[b3.done], incs=(b4 := ev.inc()))
        p.op("vector", lambda e, q=q_c, m2=v_c, o=v_c: e.scalar_tensor_tensor(
                o, q, 1.0 / D, m2, ALU.mult, ALU.subtract),
             waits=[b2.done, b4.done], incs=(b5 := dv.inc()))
        p.op("vector", lambda e, o=v_c: e.tensor_scalar_add(o, o, EPS_LN),
             waits=[b5.done], incs=(b6 := dv.inc()))
        p.op("scalar", lambda e, v=v_c, o=r_c: e.activation(o, v, AF.Ln),
             waits=[b6.done], incs=(b7 := ev.inc()))
        p.op("scalar", lambda e, r=r_c: e.activation(r, r, AF.Exp, scale=-0.5),
             incs=(b8 := ev.inc()))
        p.op("vector", lambda e, m=m_c, r=r_c, o=n_c: e.scalar_tensor_tensor(
                o, m, -1.0, r, ALU.mult, ALU.mult),
             waits=[b8.done], incs=(b9 := dv.inc()))
        p.op("scalar", lambda e, ob=ob, r=r_c, n=n_c: e.activation(
                XN2B[ob], kb.TMP[0][:], AF.Identity, bias=n, scale=r),
             waits=[b9.done] + _as_waits(xn2b_free[ob]),
             incs=(x2n := ev.inc()), tag=f"mx_x2n{j}")
        tmp0_free = [x2n.done, xist.done]
        p.op("sync", lambda e, j=j, ob=ob: e.dma_start(
                out=xn2_tm[128 * j:128 * (j + 1), :], in_=XN2B[ob]),
             waits=[x2n.done], incs=(x2st := x2s[ob].inc(16)),
             tag=f"mx_x2s{j}")
        xn2b_free[ob] = [x2st.done]
    xn2_all = [x2s[0].at(), x2s[1].at()]
    wo_mm_all = _Wait(mm, mm.count)

    if "ff" not in stages:
        return dict(done=xn2_all[0])
    tr2 = p.chan("mx_tr2")
    for dt in range(ND):
        p.op("sync", lambda e, dt=dt: e.dma_start_transpose(
                kb.xfm(dt), xn2_tm[:, 128 * dt:128 * (dt + 1)]),
             waits=(xn2_all + [wo_mm_all]) if dt == 0 else None,
             incs=(t2 := tr2.inc(16)), tag=f"mx_x2t{dt}")
    tr2_all = t2.done
    for dt in range(ND):
        p.op("vector", lambda e, dt=dt: e.tensor_scalar(
                kb.xfm(dt), kb.xfm(dt), kb.LN2S[:, dt:dt + 1],
                kb.LN2B[:, dt:dt + 1], ALU.mult, ALU.add),
             waits=[tr2_all] if dt == 0 else None,
             incs=(sb2 := dv.inc()), tag=f"mx_s2b{dt}")
    xn2_ready = sb2.done

    kb.act_set(SET_GELU_T)
    UPW = [kb.carve(65536 + 16384 * i, 8192, BF16) for i in range(2)]
    GF = [kb.carve(98304 + 4096 * i, 2048, BF16) for i in range(2)]
    f1c = [p.chan("mx_f10"), p.chan("mx_f11")]
    gsc = [p.chan("mx_gs0"), p.chan("mx_gs1")]
    ff1_bf = kb.scr["wff1T_bf"]
    ff2_bf = kb.scr["wff2T_bf"]
    upw_free = [None, None]
    gf_free = [None, None]
    pbank_free = {0: _as_waits(bank_free[0]) + _as_waits(bank_free[1]) +
                     _as_waits(bank_free[2]) + _as_waits(bank_free[3]),
                  1: _as_waits(bank_free[4]) + _as_waits(bank_free[5]) +
                     _as_waits(bank_free[6]) + _as_waits(bank_free[7])}
    for mq in range(NH // 4):
        wb2 = mq % 2
        for k in range(ND):
            p.op("sync", lambda e, mq=mq, k=k, wb2=wb2: e.dma_start(
                    out=UPW[wb2][:, 512 * k:512 * (k + 1)],
                    in_=ff1_bf[128 * k:128 * (k + 1),
                               512 * mq:512 * (mq + 1)]),
                 waits=(([kb._flags["wff1T_bf"], wo_mm_all] + xn2_all)
                        if mq < 2 and k == 0
                        else []) + (_as_waits(upw_free[wb2]) if k == 0 else []),
                 incs=(f1l := f1c[wb2].inc(16)), tag=f"mx_f1w{mq}_{k}")
        f1_ready = f1l.done
        for mm2 in range(4):
            m = 4 * mq + mm2
            pb = m % 2
            banks = ([kb.PSS[c][:] for c in range(4)] if pb == 0 else
                     [kb.PSB[:, 512 * c:512 * (c + 1)] for c in range(4)])
            for k in range(ND):
                for c in range(4):
                    w = None
                    if k == 0 and c == 0:
                        w = [f1_ready, xn2_ready] + _as_waits(pbank_free[pb])
                    p.op("tensor", lambda e, k=k, c=c, m=m, wb2=wb2, banks=banks:
                         e.matmul(
                            banks[c],
                            UPW[wb2][:, 512 * k + 128 * (m % 4):
                                     512 * k + 128 * (m % 4) + 128],
                            kb.xfm(k)[:, 512 * c:512 * (c + 1)],
                            start=(k == 0), stop=(k == ND - 1)),
                         waits=w,
                         incs=(f1m := mm.inc()) if (k == ND - 1 and c == 3)
                              else None,
                         tag=f"mx_f1{m}")
            gfb = m % 2
            for c in range(4):
                p.op("scalar", lambda e, c=c, gfb=gfb, banks=banks: e.activation(
                        GF[gfb][:, 512 * c:512 * (c + 1)], banks[c],
                        AF.Gelu_apprx_tanh),
                     waits=([f1m.done] + _as_waits(gf_free[gfb])) if c == 0
                           else None,
                     incs=(ge := ev.inc()), tag=f"mx_gev{m}")
            pbank_free[pb] = [ge.done]
            p.op("sync", lambda e, m=m, gfb=gfb: e.dma_start(
                    out=g2_hbm[128 * m:128 * (m + 1), :], in_=GF[gfb]),
                 waits=[ge.done], incs=(gst := gsc[gfb].inc(16)),
                 tag=f"mx_gst{m}")
            gf_free[gfb] = [gst.done]
        upw_free[wb2] = [_Wait(mm, mm.count)]
    up2_mm_all = _Wait(mm, mm.count)
    g2_all = [gsc[0].at(), gsc[1].at()]
    kb.act_set(SET_NL_EXP)

    WRESQ = kb.carve(0, 64 * 512, BF16)
    GSB = [kb.carve(65536 + 16384 * i, 8192, BF16) for i in range(2)]
    wrc2 = p.chan("mx_wr2")
    glc2 = [p.chan("mx_gl0"), p.chan("mx_gl1")]
    xoc = p.chan("mx_xoc")
    xac = [p.chan("mx_xa0"), p.chan("mx_xa1")]
    xbc = [p.chan("mx_xb0"), p.chan("mx_xb1")]
    xcc = [p.chan("mx_xc0"), p.chan("mx_xc1")]
    gsb_free = [None, None]
    wres_free = None
    zb_free = [pbank_free[0], pbank_free[0]]
    fb_free = [pbank_free[0], pbank_free[0]]
    xt_free = [None, None]
    smb_free = None
    blend_prev = None
    for qtr in range(4):
        for k in range(NH):
            p.op("sync", lambda e, qtr=qtr, k=k: e.dma_start(
                    out=WRESQ[:, 512 * k:512 * (k + 1)],
                    in_=ff2_bf[128 * k:128 * (k + 1),
                               512 * qtr:512 * (qtr + 1)]),
                 waits=([up2_mm_all, kb._flags["wff2T_bf"]] +
                        _as_waits(wres_free)) if k == 0 else None,
                 incs=(wr2 := wrc2.inc(16)), tag=f"mx_f2w{qtr}_{k}")
        wres_ready = wr2.done
        for j in range(NT):
            gb = j % 2
            p.op("gpsimd", lambda e, j=j, gb=gb: e.dma_start(
                    out=GSB[gb].rearrange("p (k t) -> p k t", t=128),
                    in_=g2_hbm[:, 128 * j:128 * (j + 1)].rearrange(
                        "(k p) t -> p k t", p=128)),
                 waits=(g2_all if (qtr == 0 and j < 2) else []) +
                       _as_waits(gsb_free[gb]),
                 incs=(gl2 := glc2[gb].inc(16)), tag=f"mx_gsl{qtr}_{j}")
            fbk = kb.PSS[j % 2][:]
            zbk = kb.PSS[2 + j % 2][:]
            for k in range(NH):
                w = None
                if k == 0:
                    w = [gl2.done, wres_ready] + _as_waits(fb_free[j % 2]) + \
                        ([up2_mm_all] if (qtr == 0 and j < 2) else [])
                p.op("tensor", lambda e, k=k, gb=gb, fbk=fbk: e.matmul(
                        fbk, GSB[gb][:, 128 * k:128 * (k + 1)],
                        WRESQ[:, 512 * k:512 * (k + 1)],
                        start=(k == 0), stop=(k == NH - 1)),
                     waits=w,
                     incs=(fmm := mm.inc()) if k == NH - 1 else None,
                     tag=f"mx_f2{qtr}_{j}")
            p.op("tensor", lambda e, j=j, qtr=qtr, zbk=zbk: e.matmul(
                    zbk, T1B[:, 128 * j:128 * (j + 1)],
                    WUPT1[:, 512 * qtr:512 * (qtr + 1)],
                    start=True, stop=True),
                 waits=([t1_done, wup_ld.done] if (qtr == 0 and j < 2) else []) +
                       _as_waits(zb_free[j % 2]),
                 incs=(zmm := mm.inc()), tag=f"mx_z{qtr}_{j}")
            gsb_free[gb] = [_Wait(mm, mm.count)]
            xb2 = j % 2
            p.op("sync", lambda e, j=j, qtr=qtr, xb2=xb2: e.dma_start(
                    out=kb.XT[xb2][:, 0:512],
                    in_=x_in[128 * j:128 * (j + 1), 512 * qtr:512 * (qtr + 1)]),
                 waits=_as_waits(xt_free[xb2]), incs=(lxo := xac[xb2].inc(16)),
                 tag=f"mx_bxl{qtr}_{j}")
            p.op("sync", lambda e, j=j, qtr=qtr, xb2=xb2: e.dma_start(
                    out=kb.XT[xb2][:, 512:1024],
                    in_=xi_hbm[128 * j:128 * (j + 1),
                               512 * qtr:512 * (qtr + 1)]),
                 waits=[xis[xb2].at()] if qtr == 0 and j < 2 else None,
                 incs=(lxi := xbc[xb2].inc(16)), tag=f"mx_bxi{qtr}_{j}")
            p.op("sync", lambda e, j=j, qtr=qtr, xb2=xb2: e.dma_start(
                    out=kb.XT[xb2][:, 1024:1280].bitcast(BF16),
                    in_=xn_tm[128 * j:128 * (j + 1),
                              512 * qtr:512 * (qtr + 1)]),
                 incs=(lxn := xcc[xb2].inc(16)), tag=f"mx_bxn{qtr}_{j}")
            is0 = SM[:, 304 + j:305 + j]
            is1 = SM[:, 320 + j:321 + j]
            is2 = SM[:, 336 + j:337 + j]
            is01 = SM[:, 352 + j:353 + j]
            acc = kb.TMP[0][:, 0:512]
            fin = kb.TMP[0][:, 512:1024]
            xnbf = kb.XT[xb2][:, 1024:1280].bitcast(BF16)
            p.op("vector", lambda e, xb2=xb2, is2=is2, acc=acc: e.tensor_scalar(
                    acc, kb.XT[xb2][:, 512:1024], is2, None, ALU.mult),
                 waits=[lxi.done, masks_done] + _as_waits(blend_prev),
                 incs=(c1 := dv.inc()), tag=f"mx_b1{qtr}_{j}")
            if f"mx{ci}_F" in kb.dbg:
                dbgF = kb.dram(f"mx{ci}_F", (S, D), F32)
                FD = kb.carve(98304, 512, F32)
                p.op("scalar", lambda e, fbk=fbk, FD=FD: e.activation(
                        FD, fbk, AF.Identity),
                     waits=[fmm.done], incs=(fde := ev.inc()), tag="dbgFe")
                p.op("sync", lambda e, j=j, qtr=qtr, FD=FD: e.dma_start(
                        out=dbgF[128 * j:128 * (j + 1),
                                 512 * qtr:512 * (qtr + 1)], in_=FD),
                     waits=[fde.done], incs=(fds := xoc.inc(16)), tag="dbgFs")
                fwait = [fmm.done, fds.done]
            else:
                fwait = [fmm.done]
            p.op("vector", lambda e, fbk=fbk, is2=is2, acc=acc:
                 e.scalar_tensor_tensor(acc, fbk, is2, acc, ALU.mult, ALU.add),
                 waits=fwait + [c1.done], incs=(c2 := dv.inc()), tag=f"mx_b2{qtr}_{j}")
            fb_free[j % 2] = [c2.done]
            p.op("vector", lambda e, zbk=zbk, is1=is1, acc=acc:
                 e.scalar_tensor_tensor(acc, zbk, is1, acc, ALU.mult, ALU.add),
                 waits=[zmm.done, c2.done], incs=(c3 := dv.inc()), tag=f"mx_b3{qtr}_{j}")
            zb_free[j % 2] = [c3.done]
            p.op("vector", lambda e, xnbf=xnbf, is01=is01, acc=acc:
                 e.scalar_tensor_tensor(acc, xnbf, is01, acc, ALU.mult, ALU.add),
                 waits=[lxn.done, c3.done], incs=(c4 := dv.inc()), tag=f"mx_b4{qtr}_{j}")
            p.op("vector", lambda e, qtr=qtr, is0=is0, acc=acc:
                 e.scalar_tensor_tensor(
                    acc, kb.BIAS_BC[:, 512 * qtr:512 * (qtr + 1)], is0, acc,
                    ALU.mult, ALU.add),
                 waits=[c4.done], incs=(c5 := dv.inc()), tag=f"mx_b5{qtr}_{j}")
            p.op("vector", lambda e, xb2=xb2, acc=acc, fin=fin:
                 e.scalar_tensor_tensor(fin, acc, 0.5, kb.XT[xb2][:, 0:512],
                                        ALU.mult, ALU.add),
                 waits=[lxo.done, c5.done], incs=(c6 := dv.inc()), tag=f"mx_b6{qtr}_{j}")
            p.op("sync", lambda e, j=j, qtr=qtr, fin=fin: e.dma_start(
                    out=x_out[128 * j:128 * (j + 1),
                              512 * qtr:512 * (qtr + 1)], in_=fin),
                 waits=[c6.done], incs=(xo := xoc.inc(16)),
                 tag=f"mx_xo{qtr}_{j}")
            xt_free[xb2] = [c6.done]
            blend_prev = [xo.done]
        wres_free = [_Wait(mm, mm.count)]
    if f"mx{ci}_msk" in kb.dbg:
        dbgM = kb.dram(f"mx{ci}_msk", (128, 128), F32)
        p.op("sync", lambda e: e.dma_start(out=dbgM[:, 0:64],
                                           in_=SM[:, 304:368]),
             waits=[xoc.at()], incs=xoc.inc(16), tag="dbgMsk")
        p.op("sync", lambda e: e.dma_start(out=dbgM[:, 64:128],
                                           in_=kb.LGT[:]),
             incs=xoc.inc(16), tag="dbgLgt")
    return dict(done=xoc.at())
from concourse.bass_utils import run_bass_kernel_spmd

FUSED = True


def _build_fused(shapes):
    nc = bass.Bass("TRN2", target_bir_lowering=False,
                   detect_race_conditions=False)
    kbb = KB(nc, shapes)
    x_in = kbb.din["x_in"]
    xa = kbb.dram("x_a", (S, D), F32)
    xb = kbb.dram("x_b", (S, D), F32)
    y = kbb.dram_out("y", (S, D), F32)
    r1 = emit_ffn0(kbb, x_in, xa, 0, start=[kbb.const_ready])
    r2 = emit_mamba(kbb, xa, xb, start=[r1["done"]])
    r3 = emit_ffn0(kbb, xb, xa, 1, start=[r2["done"]])
    r4 = emit_mix1(kbb, xa, xb, 0, start=[r3["done"]])
    r5 = emit_mla(kbb, xb, xa, start=[r4["done"]])
    r6 = emit_mix1(kbb, xa, y, 1, start=[r5["done"]])
    kbb.p.op("sync", lambda e: None, waits=[r6["done"]], tag="final")
    kbb.p.build()
    return nc


def _build_one(which, ci, shapes):
    nc = bass.Bass("TRN2", target_bir_lowering=False,
                   detect_race_conditions=False)
    kbb = KB(nc, shapes)
    x_in = kbb.din["x_in"]
    x_out = kbb.dram_out("x_out", (S, D), F32)
    if which == "ffn0":
        r = emit_ffn0(kbb, x_in, x_out, ci, start=[kbb.const_ready])
    elif which == "mamba":
        r = emit_mamba(kbb, x_in, x_out, start=[kbb.const_ready])
    elif which == "mla":
        r = emit_mla(kbb, x_in, x_out, start=[kbb.const_ready])
    else:
        r = emit_mix1(kbb, x_in, x_out, ci, start=[kbb.const_ready])
    kbb.p.op("sync", lambda e: None, waits=[r["done"]], tag="final")
    kbb.p.build()
    return nc


def kernel(**inputs):
    g, xs = prepare_inputs(inputs)
    shapes = shapes_of(g)
    shapes["x_in"] = ((S, D), F32)
    ncores = len(xs)
    if FUSED:
        nc = _build_fused(shapes)
        in_maps = [dict(g, x_in=np.ascontiguousarray(xs[c]))
                   for c in range(ncores)]
        res = run_bass_kernel_spmd(nc, in_maps, core_ids=list(range(ncores)))
        return np.stack([np.asarray(res.results[c]["y"])
                         for c in range(ncores)], 0)
    cur = xs
    ncs = {}
    for which, ci in [("ffn0", 0), ("mamba", None), ("ffn0", 0), ("mix1", 0),
                      ("mla", None), ("mix1", 0)]:
        if which not in ncs:
            ncs[which] = _build_one(which, ci, shapes)
        nc = ncs[which]
        in_maps = [dict(g, x_in=np.ascontiguousarray(cur[c]))
                   for c in range(ncores)]
        res = run_bass_kernel_spmd(nc, in_maps, core_ids=list(range(ncores)))
        cur = [np.asarray(res.results[c]["x_out"]) for c in range(ncores)]
    return np.stack(cur, 0)
